# version 03
# baseline (speedup 1.0000x reference)
import contextlib
import numpy as np
import concourse.bass as bass
import concourse.mybir as mybir
from concourse.bass_core_rust import IndirectOffsetOnAxis
from concourse.bass_utils import run_bass_kernel_spmd

F32 = mybir.dt.float32
BF16 = mybir.dt.bfloat16
I32 = mybir.dt.int32
AF = mybir.ActivationFunctionType
OP = mybir.AluOpType
AXX = mybir.AxisListType.X

ENGINES = ("tensor", "vector", "scalar", "gpsimd", "sync")
ENG_SEM = {"tensor": "sT", "vector": "sV", "scalar": "sA", "gpsimd": "sG", "sync": "sQ"}
Q_SEM = {"gpsimd": "qG", "sync": "qS", "scalar": "qA", "vector": "qV"}

E = 2048
KCH = 16
TOK = 1024
TT = 512
NTT = TOK // TT
SEQ = 2048
NVH = 16
NKH = 8
C = 64
NCH = SEQ // C
DK = 128
DV = 128
NEXP = 8
HEXP = 1024
HSH = 2048
EPS_LN = 1e-5
EPS_RMS = 1e-6
EPS_L2 = 1e-6


class PB:

    def __init__(self, nc, es):
        self.nc = nc
        self.es = es
        self.stream = {e: [] for e in ENGINES}
        names = sorted(set(ENG_SEM.values()) | set(Q_SEM.values()))
        self.sem = {n: es.enter_context(nc.semaphore(n)) for n in names}
        self.cnt = {n: 0 for n in names}
        self.floor = {e: {n: 0 for n in names} for e in ENGINES}
        self.last_w = {}
        self.last_r = {}
        self.pending = {e: [] for e in ENGINES}
        self.n_ops = 0

    def _wait(self, eng, ev):
        if ev is None:
            return
        if isinstance(ev, tuple) and ev[0] == "PEND":
            if ev[1] == eng:
                return
            raise RuntimeError(f"cross-engine dependency on un-inc'd op from {ev[1]} -> {eng}")
        s, v = ev
        if self.floor[eng][s] >= v:
            return
        self.floor[eng][s] = v
        sem = self.sem[s]
        self.stream[eng].append(lambda h, sem=sem, v=v: h.wait_ge(sem, v))

    def _resolve_pending(self, eng, ev):
        for rec in self.pending[eng]:
            kind, key = rec
            if kind == "w":
                if isinstance(self.last_w.get(key), tuple) and self.last_w[key][0] == "PEND":
                    self.last_w[key] = ev
            else:
                lst = self.last_r.get(key, [])
                self.last_r[key] = [ev if (isinstance(x, tuple) and x[0] == "PEND") else x for x in lst]
        self.pending[eng] = []

    def op(self, eng, fname, *args, rd=(), wr=(), inc=True, sem=None, **kw):
        import traceback as _tb
        _site = "".join(_tb.format_stack(limit=4)[:-1])
        for k in rd:
            self._wait(eng, self.last_w.get(k))
        for k in wr:
            self._wait(eng, self.last_w.get(k))
            for ev in self.last_r.get(k, ()):
                self._wait(eng, ev)
        sname = sem or ENG_SEM[eng]
        if inc:
            self.cnt[sname] += 16 if sname.startswith("q") else 1
            ev = (sname, self.cnt[sname])
            semh = self.sem[sname]
            amt = 16 if sname.startswith("q") else 1
            def fn(h, fname=fname, args=args, kw=kw, semh=semh, amt=amt, _site=_site):
                try:
                    getattr(h, fname)(*args, **kw).then_inc(semh, amt)
                except Exception as e:
                    raise RuntimeError(f"emit fail {fname}: {e}\nEMIT SITE:\n{_site}")
        else:
            ev = ("PEND", eng)
            def fn(h, fname=fname, args=args, kw=kw, _site=_site):
                try:
                    getattr(h, fname)(*args, **kw)
                except Exception as e:
                    raise RuntimeError(f"emit fail {fname}: {e}\nEMIT SITE:\n{_site}")
        self.stream[eng].append(fn)
        self.n_ops += 1
        for k in wr:
            self.last_w[k] = ev
            self.last_r[k] = []
        for k in rd:
            self.last_r.setdefault(k, []).append(ev)
        if not inc:
            for k in wr:
                self.pending[eng].append(("w", k))
            for k in rd:
                self.pending[eng].append(("r", k))
        else:
            self._resolve_pending(eng, ev)
        return ev

    def barrier(self):
        snap = dict(self.cnt)
        for e in ENGINES:
            for s, v in snap.items():
                if v > self.floor[e][s]:
                    self.floor[e][s] = v
                    sem = self.sem[s]
                    self.stream[e].append(lambda h, sem=sem, v=v: h.wait_ge(sem, v))

    def transfer_key(self, old, new):
        if old in self.last_w:
            prev = self.last_w.get(new)
            self.last_w[new] = self.last_w[old]
        self.last_r.setdefault(new, []).extend(self.last_r.get(old, ()))

    def dma_sem(self, slot):
        name = f"q_{slot}"
        if name not in self.sem:
            self.sem[name] = self.es.enter_context(self.nc.semaphore(name))
            self.cnt[name] = 0
            for e in ENGINES:
                self.floor[e][name] = 0
        return name

    def dma(self, eng, out, in_, rd=(), wr=(), slot=None, **kw):
        slot = slot or (wr[0] if wr else "misc")
        return self.op(eng, "dma_start", rd=rd, wr=wr, sem=self.dma_sem(slot),
                       out=out, in_=in_, **kw)

    def finalize(self, block):
        for eng in ENGINES:
            dec = getattr(block, eng)
            stream = self.stream[eng]
            def make(stream):
                def body(h):
                    for fn in stream:
                        fn(h)
                return body
            dec(make(stream))


def sbuf(nc, es, name, shape, dtype):
    return es.enter_context(nc.sbuf_tensor(name, shape, dtype))


def psum(nc, es, name, shape, dtype=F32):
    return es.enter_context(nc.psum_tensor(name, shape, dtype))


def emit_ffn(p, nc, es, g, tag):
    x_res = g["x_res"]
    with contextlib.ExitStack() as ffn_es:
        S = lambda name, shape, dt: sbuf(nc, ffn_es, f"{tag}_{name}", shape, dt)
        P = lambda name, shape, dt=F32: psum(nc, ffn_es, f"{tag}_{name}", shape, dt)
        psA = P("psA", [128, 512]); psB = P("psB", [128, 512])
        psC = P("psC", [128, 512]); psD = P("psD", [128, 512])
        psE = P("psE", [128, 512]); psF = P("psF", [128, 512])
        psG_ = P("psG", [128, 512]); psT8 = P("psT8", [8, 512], BF16)
        xln = S("xln", [128, KCH * TT], BF16)
        xbf = S("xbf", [128, 2 * TT], BF16)
        xsq = S("xsq", [128, 2 * TT], BF16)
        strow = S("strow", [1, 2 * TT], F32)
        a_sb = S("a_sb", [128, TT], F32)
        b_sb = S("b_sb", [128, TT], F32)
        sc_sb = S("sc_sb", [128, 4 * 8], F32)
        r1 = S("r1", [128, 4 * 8], F32)
        r2 = S("r2", [128, 4 * 8], F32)
        rsc = S("rsc", [128, 4 * 6], F32)
        combb = S("combb", [128, 4 * 8], BF16)
        compR = S("compR", [8, TT], BF16)
        sgrow = S("sgrow", [1, TT], F32)
        cbc = S("cbc", [128, TT], BF16)
        sgbc = S("sgbc", [128, TT], BF16)
        hp = S("hp", [128, 64 * TT], BF16)
        hsh = S("hsh", [128, 16 * TT], BF16)
        s1g = S("s1g", [128, 2 * TT], BF16)
        htmp = S("htmp", [128, 2 * TT], F32)
        wring = S("wring", [128, 10240], BF16)
        wgu = wring
        ident_tr = g["ident_bf"]

        wg_d, wu_d, wd_d, shg_d, shu_d, shd_d, gw_d, seg_d = (
            g["wg"], g["wu"], g["wd"], g["shg"], g["shu"], g["shd"], g["gw"], g["seg"])

        for tt in range(NTT):
            t0 = tt * TT
            xk = lambda k: x_res[:, k * TOK + t0: k * TOK + t0 + TT]
            kx = f"{tag}xres"

            for k in range(KCH):
                r = k % 2
                xb_ap = xbf[:, r * TT:(r + 1) * TT]
                xs_ap = xsq[:, r * TT:(r + 1) * TT]
                p.op("vector", "tensor_copy", xb_ap, xk(k), rd=[kx], wr=[f"xbf{r}"])
                p.op("scalar", "activation", xs_ap, xk(k), AF.Square, rd=[kx], wr=[f"xsq{r}"])
                p.op("tensor", "matmul", psA[0:1, :], g["ones_bf"][0:128, 0:1], xb_ap,
                     start=(k == 0), stop=(k == KCH - 1),
                     rd=[f"xbf{r}", "ones_bf"], wr=["psA"])
                p.op("tensor", "matmul", psB[0:1, :], g["ones_bf"][0:128, 0:1], xs_ap,
                     start=(k == 0), stop=(k == KCH - 1),
                     rd=[f"xsq{r}", "ones_bf"], wr=["psB"])
            mean = strow[0:1, 0:TT]; e2 = strow[0:1, TT:2*TT]; var = e2
            rstd = e2; brow = mean
            p.op("vector", "tensor_scalar", mean, psA[0:1, :], 1.0 / E, None, OP.mult,
                 rd=["psA"], wr=["st_m"])
            p.op("vector", "tensor_scalar", e2, psB[0:1, :], 1.0 / E, None, OP.mult,
                 rd=["psB"], wr=["st_e2"])
            tmpv = sgrow[0:1, 0:TT]
            p.op("vector", "tensor_mul", tmpv, mean, mean, rd=["st_m"], wr=["sgA"])
            p.op("vector", "tensor_sub", var, e2, tmpv, rd=["st_e2", "sgA"], wr=["st_e2"])
            p.op("vector", "tensor_scalar", var, var, EPS_LN, None, OP.add, rd=["st_e2"], wr=["st_e2"])
            p.op("vector", "reciprocal", var, var, rd=["st_e2"], wr=["st_e2"])
            p.op("scalar", "activation", rstd, var, AF.Sqrt, rd=["st_e2"], wr=["st_e2"])
            p.op("vector", "scalar_tensor_tensor", brow, mean, -1.0, rstd, OP.mult, OP.mult,
                 rd=["st_m", "st_e2"], wr=["st_m"])
            p.op("tensor", "matmul", psC[:, :], g["ones_f1"][0:1, 0:128], rstd,
                 start=True, stop=True, rd=["st_e2", "ones_f1"], wr=["psC"])
            p.op("tensor", "matmul", psD[:, :], g["ones_f1"][0:1, 0:128], brow,
                 start=True, stop=True, rd=["st_m", "ones_f1"], wr=["psD"])
            p.op("vector", "tensor_copy", a_sb[:, :], psC[:, :], rd=["psC"], wr=["a_sb"])
            p.op("vector", "tensor_copy", b_sb[:, :], psD[:, :], rd=["psD"], wr=["b_sb"])
            for k in range(KCH):
                dst = xln[:, k * TT:(k + 1) * TT]
                r = k % 2
                tmp = htmp[:, r * TT:(r + 1) * TT]
                p.op("vector", "tensor_mul", tmp, xk(k), a_sb[:, :], rd=[kx, "a_sb"], wr=[f"htmp{r}"])
                p.op("vector", "tensor_add", dst, tmp, b_sb[:, :], rd=[f"htmp{r}", "b_sb"], wr=["xln"])

            for st in range(4):
                for k in range(KCH):
                    p.op("tensor", "matmul", psE[0:128, st * 8:(st + 1) * 8],
                         xln[:, k * TT + st * 128: k * TT + st * 128 + 128],
                         g["gw_sb"][:, k * 8:(k + 1) * 8],
                         start=(k == 0), stop=(k == KCH - 1),
                         rd=["xln", "gw_sb"], wr=["psE"], inc=(k == KCH - 1))
            p.op("vector", "tensor_copy", sc_sb[:, :], psE[:, 0:32], rd=["psE"], wr=["sc"])
            for st in range(4):
                sc = sc_sb[:, st * 8:(st + 1) * 8]
                eq1 = r1[:, st * 8:(st + 1) * 8]
                s2 = r2[:, st * 8:(st + 1) * 8]
                m1 = rsc[:, st * 6 + 0: st * 6 + 1]; m2 = rsc[:, st * 6 + 1: st * 6 + 2]
                ex = rsc[:, st * 6 + 2: st * 6 + 3]; w1 = rsc[:, st * 6 + 3: st * 6 + 4]
                w2 = rsc[:, st * 6 + 4: st * 6 + 5]; dd = rsc[:, st * 6 + 5: st * 6 + 6]
                p.op("vector", "reduce_max", m1, sc, AXX, rd=["sc"], wr=["m1"])
                p.op("vector", "tensor_scalar", eq1, sc, m1, None, OP.is_equal, rd=["sc", "m1"], wr=["eq1"])
                p.op("vector", "scalar_tensor_tensor", s2, eq1, -1e9, sc, OP.mult, OP.add,
                     rd=["eq1", "sc"], wr=["s2"])
                p.op("vector", "reduce_max", m2, s2, AXX, rd=["s2"], wr=["m2"])
                p.op("vector", "tensor_sub", dd, m2, m1, rd=["m2", "m1"], wr=["dd"])
                p.op("scalar", "activation", ex, dd, AF.Exp, rd=["dd"], wr=["ex"])
                p.op("vector", "tensor_scalar", w1, ex, 1.0, None, OP.add, rd=["ex"], wr=["w1"])
                p.op("vector", "reciprocal", w1, w1, rd=["w1"], wr=["w1"])
                p.op("vector", "tensor_mul", w2, ex, w1, rd=["ex", "w1"], wr=["w2"])
                p.op("vector", "tensor_scalar", s2, s2, m2, None, OP.is_equal, rd=["s2", "m2"], wr=["s2"])
                p.op("vector", "tensor_scalar", eq1, eq1, w1, None, OP.mult, rd=["eq1", "w1"], wr=["eq1"])
                p.op("vector", "scalar_tensor_tensor", eq1, s2, w2, eq1, OP.mult, OP.add,
                     rd=["s2", "w2", "eq1"], wr=["eq1"])
                p.op("vector", "tensor_copy", combb[:, st * 8:(st + 1) * 8], eq1, rd=["eq1"], wr=["combb"])
                p.op("tensor", "transpose", psT8[0:8, st * 128:(st + 1) * 128],
                     combb[:, st * 8:(st + 1) * 8], ident_tr[:, :],
                     rd=["combb", "ident_bf"], wr=["psT8"])
            p.op("vector", "tensor_copy", compR[:, :], psT8[0:8, 0:TT], rd=["psT8"], wr=["compR"])

            for k in range(KCH):
                p.op("tensor", "matmul", psF[0:1, 0:TT], g["seg_sb"][:, k:k+1],
                     xln[:, k * TT:(k + 1) * TT],
                     start=(k == 0), stop=(k == KCH - 1),
                     rd=["xln", "seg_sb"], wr=["psF"], inc=(k == KCH - 1))
            sgA = sgrow[0:1, 0:TT]
            p.op("scalar", "activation", sgA, psF[0:1, 0:TT], AF.Exp, scale=g["cneg1"][0:1, 0:1],
                 rd=["psF"], wr=["sgA"])
            p.op("vector", "tensor_scalar", sgA, sgA, 1.0, None, OP.add, rd=["sgA"], wr=["sgA"])
            p.op("vector", "reciprocal", sgA, sgA, rd=["sgA"], wr=["sgA"])
            p.op("tensor", "matmul", psF[:, :], g["ones_f1"][0:1, 0:128], sgA,
                 start=True, stop=True, rd=["sgA", "ones_f1"], wr=["psF"])
            p.op("vector", "tensor_copy", sgbc[:, :], psF[:, :], rd=["psF"], wr=["sgbc"])

            def wgu_load(e, h, r):
                base = r * 2 * KCH * 128
                dst_g = wgu[:, base: base + KCH * 128]
                dst_u = wgu[:, base + KCH * 128: base + 2 * KCH * 128]
                in_g = wg_d[e].rearrange("(k p) h -> p k h", p=128)[:, :, h * 128:(h + 1) * 128]
                in_u = wu_d[e].rearrange("(k p) h -> p k h", p=128)[:, :, h * 128:(h + 1) * 128]
                p.dma("gpsimd", dst_g.rearrange("p (k h) -> p k h", k=KCH), in_g, wr=[f"wgu{r}"])
                p.dma("gpsimd", dst_u.rearrange("p (k h) -> p k h", k=KCH), in_u, wr=[f"wgu{r}"])

            pairs = [(e, h) for e in range(NEXP) for h in range(8)]
            wgu_load(*pairs[0], 0)
            wgu_load(*pairs[1], 1)
            for i, (e, h) in enumerate(pairs):
                r = i % 2
                base = r * 2 * KCH * 128
                if h == 0:
                    p.op("tensor", "matmul", psG_[:, :], g["oh8_sb"][0:8, e*128:(e+1)*128],
                         compR[0:8, :], start=True, stop=True, rd=["compR", "oh8_sb"], wr=["psG"])
                    p.op("vector", "tensor_copy", cbc[:, :], psG_[:, :], rd=["psG"], wr=["cbc"])
                for k in range(KCH):
                    p.op("tensor", "matmul", psA[:, :], wgu[:, base + k * 128: base + (k+1) * 128],
                         xln[:, k * TT:(k + 1) * TT], start=(k == 0), stop=(k == KCH - 1),
                         rd=[f"wgu{r}", "xln"], wr=["psA"], inc=(k == KCH - 1))
                for k in range(KCH):
                    p.op("tensor", "matmul", psB[:, :], wgu[:, base + KCH*128 + k * 128: base + KCH*128 + (k+1) * 128],
                         xln[:, k * TT:(k + 1) * TT], start=(k == 0), stop=(k == KCH - 1),
                         rd=[f"wgu{r}", "xln"], wr=["psB"], inc=(k == KCH - 1))
                s1 = s1g[:, r * TT:(r + 1) * TT]
                tmp = htmp[:, r * TT:(r + 1) * TT]
                p.op("scalar", "activation", s1, psA[:, :], AF.Silu, rd=["psA"], wr=[f"s1{r}"])
                p.op("vector", "tensor_mul", tmp, psB[:, :], s1, rd=["psB", f"s1{r}"], wr=[f"htmp{r}"])
                hslot = hp[:, (e * 8 + h) * TT: (e * 8 + h + 1) * TT]
                p.op("vector", "tensor_mul", hslot, tmp, cbc[:, :], rd=[f"htmp{r}", "cbc"], wr=["hp"])
                if i + 2 < len(pairs):
                    wgu_load(*pairs[i + 2], r)

            def wsh_load(hs, r):
                base = r * 2 * KCH * 128
                dst_g = wgu[:, base: base + KCH * 128]
                dst_u = wgu[:, base + KCH * 128: base + 2 * KCH * 128]
                in_g = shg_d.rearrange("(k p) h -> p k h", p=128)[:, :, hs * 128:(hs + 1) * 128]
                in_u = shu_d.rearrange("(k p) h -> p k h", p=128)[:, :, hs * 128:(hs + 1) * 128]
                p.dma("gpsimd", dst_g.rearrange("p (k h) -> p k h", k=KCH), in_g, wr=[f"wgu{r}"])
                p.dma("gpsimd", dst_u.rearrange("p (k h) -> p k h", k=KCH), in_u, wr=[f"wgu{r}"])
            wsh_load(0, 0); wsh_load(1, 1)
            for hs in range(16):
                r = hs % 2
                base = r * 2 * KCH * 128
                for k in range(KCH):
                    p.op("tensor", "matmul", psA[:, :], wgu[:, base + k * 128: base + (k+1) * 128],
                         xln[:, k * TT:(k + 1) * TT], start=(k == 0), stop=(k == KCH - 1),
                         rd=[f"wgu{r}", "xln"], wr=["psA"], inc=(k == KCH - 1))
                for k in range(KCH):
                    p.op("tensor", "matmul", psB[:, :], wgu[:, base + KCH*128 + k * 128: base + KCH*128 + (k+1) * 128],
                         xln[:, k * TT:(k + 1) * TT], start=(k == 0), stop=(k == KCH - 1),
                         rd=[f"wgu{r}", "xln"], wr=["psB"], inc=(k == KCH - 1))
                s1 = s1g[:, r * TT:(r + 1) * TT]
                tmp = htmp[:, r * TT:(r + 1) * TT]
                p.op("scalar", "activation", s1, psA[:, :], AF.Silu, rd=["psA"], wr=[f"s1{r}"])
                p.op("vector", "tensor_mul", tmp, psB[:, :], s1, rd=["psB", f"s1{r}"], wr=[f"htmp{r}"])
                p.op("vector", "tensor_mul", hsh[:, hs * TT:(hs + 1) * TT], tmp, sgbc[:, :],
                     rd=[f"htmp{r}", "sgbc"], wr=["hsh"])
                if hs + 2 < 16:
                    wsh_load(hs + 2, r)

            for j in range(6):
                p.transfer_key("wgu0", f"wde{j}")
                p.transfer_key("wgu1", f"wde{j}")
            for r in range(2):
                p.transfer_key("wgu0", f"wshd{r}")
                p.transfer_key("wgu1", f"wshd{r}")

            def wde_load(m, e, j):
                dst = wring[:, j * 1024:(j + 1) * 1024]
                in_t = wd_d[e].rearrange("(hk p) m -> p hk m", p=128)[:, :, m * 128:(m + 1) * 128]
                p.dma("gpsimd", dst.rearrange("p (hk m) -> p hk m", hk=8), in_t, wr=[f"wde{j}"])

            def wshd_load(m, r):
                dst = wring[:, 6144 + r * 2048: 6144 + (r + 1) * 2048]
                inS = shd_d.rearrange("(hs p) m -> p hs m", p=128)[:, :, m * 128:(m + 1) * 128]
                p.dma("gpsimd", dst.rearrange("p (hs m) -> p hs m", hs=16), inS, wr=[f"wshd{r}"])

            me_list = [(m, e) for m in range(KCH) for e in range(NEXP)]
            for jj in range(6):
                wde_load(*me_list[jj], jj)
            wshd_load(0, 0); wshd_load(1, 1)
            for m in range(KCH):
                r = m % 2
                pd = psC if (m % 2 == 0) else psD
                pk = "psC" if (m % 2 == 0) else "psD"
                n_mm = NEXP * 8 + 16
                i_mm = 0
                for e in range(NEXP):
                    j = (m * NEXP + e) % 6
                    for hk in range(8):
                        p.op("tensor", "matmul", pd[:, :],
                             wring[:, j * 1024 + hk * 128: j * 1024 + (hk + 1) * 128],
                             hp[:, (e * 8 + hk) * TT: (e * 8 + hk + 1) * TT],
                             start=(i_mm == 0), stop=(i_mm == n_mm - 1),
                             rd=[f"wde{j}", "hp"], wr=[pk], inc=(hk == 7))
                        i_mm += 1
                    nxt = m * NEXP + e + 6
                    if nxt < len(me_list):
                        wde_load(*me_list[nxt], nxt % 6)
                for hs in range(16):
                    p.op("tensor", "matmul", pd[:, :],
                         wring[:, 6144 + r * 2048 + hs * 128: 6144 + r * 2048 + (hs + 1) * 128],
                         hsh[:, hs * TT: (hs + 1) * TT],
                         start=(i_mm == 0), stop=(i_mm == n_mm - 1),
                         rd=[f"wshd{r}", "hsh"], wr=[pk], inc=(i_mm == n_mm - 1))
                    i_mm += 1
                if m + 2 < KCH:
                    wshd_load(m + 2, r)
                dst = x_res[:, m * TOK + t0: m * TOK + t0 + TT]
                p.op("vector", "scalar_tensor_tensor", dst, pd[:, :], 0.5, dst, OP.mult, OP.add,
                     rd=[pk, kx], wr=[kx])
            for j in range(6):
                p.transfer_key(f"wde{j}", "wgu0"); p.transfer_key(f"wde{j}", "wgu1")
            for r in range(2):
                p.transfer_key(f"wshd{r}", "wgu0"); p.transfer_key(f"wshd{r}", "wgu1")



def emit_ln_to(p, nc, es, g, x_res, out_cb, tag):
    with contextlib.ExitStack() as les:
        S = lambda name, shape, dt: sbuf(nc, les, f"{tag}_{name}", shape, dt)
        P = lambda name, shape, dt=F32: psum(nc, les, f"{tag}_{name}", shape, dt)
        psA = P("psA", [128, 512]); psB = P("psB", [128, 512])
        psC = P("psC", [128, 512]); psD = P("psD", [128, 512])
        xbf = S("xbf", [128, 2 * TT], BF16)
        xsq = S("xsq", [128, 2 * TT], BF16)
        ftmp = S("ftmp", [128, 2 * TT], F32)
        strow = S("strow", [1, 2 * TT], F32)
        srow2 = S("srow2", [1, TT], F32)
        a_sb = S("a_sb", [128, TT], F32)
        b_sb = S("b_sb", [128, TT], F32)
        xln = S("xln", [128, KCH * TT], BF16)
        kx = f"{tag}xres"
        for tt in range(NTT):
            t0 = tt * TT
            xk = lambda k: x_res[:, k * TOK + t0: k * TOK + t0 + TT]
            for k in range(KCH):
                r = k % 2
                xb_ap = xbf[:, r * TT:(r + 1) * TT]
                xs_ap = xsq[:, r * TT:(r + 1) * TT]
                p.op("vector", "tensor_copy", xb_ap, xk(k), rd=[kx], wr=[f"{tag}xbf{r}"])
                p.op("scalar", "activation", xs_ap, xk(k), AF.Square, rd=[kx], wr=[f"{tag}xsq{r}"])
                p.op("tensor", "matmul", psA[0:1, :], g["ones_bf"][0:128, 0:1], xb_ap,
                     start=(k == 0), stop=(k == KCH - 1), rd=[f"{tag}xbf{r}", "ones_bf"], wr=[f"{tag}psA"])
                p.op("tensor", "matmul", psB[0:1, :], g["ones_bf"][0:128, 0:1], xs_ap,
                     start=(k == 0), stop=(k == KCH - 1), rd=[f"{tag}xsq{r}", "ones_bf"], wr=[f"{tag}psB"])
            mean = strow[0:1, 0:TT]; e2 = strow[0:1, TT:2*TT]
            p.op("vector", "tensor_scalar", mean, psA[0:1, :], 1.0 / E, None, OP.mult,
                 rd=[f"{tag}psA"], wr=[f"{tag}m"])
            p.op("vector", "tensor_scalar", e2, psB[0:1, :], 1.0 / E, None, OP.mult,
                 rd=[f"{tag}psB"], wr=[f"{tag}e"])
            p.op("vector", "tensor_mul", srow2[0:1, :], mean, mean, rd=[f"{tag}m"], wr=[f"{tag}s2"])
            p.op("vector", "tensor_sub", e2, e2, srow2[0:1, :], rd=[f"{tag}e", f"{tag}s2"], wr=[f"{tag}e"])
            p.op("vector", "tensor_scalar", e2, e2, EPS_LN, None, OP.add, rd=[f"{tag}e"], wr=[f"{tag}e"])
            p.op("vector", "reciprocal", e2, e2, rd=[f"{tag}e"], wr=[f"{tag}e"])
            p.op("scalar", "activation", e2, e2, AF.Sqrt, rd=[f"{tag}e"], wr=[f"{tag}e"])
            p.op("vector", "scalar_tensor_tensor", mean, mean, -1.0, e2, OP.mult, OP.mult,
                 rd=[f"{tag}m", f"{tag}e"], wr=[f"{tag}m"])
            p.op("tensor", "matmul", psC[:, :], g["ones_f1"][0:1, 0:128], e2,
                 start=True, stop=True, rd=[f"{tag}e", "ones_f1"], wr=[f"{tag}psC"])
            p.op("tensor", "matmul", psD[:, :], g["ones_f1"][0:1, 0:128], mean,
                 start=True, stop=True, rd=[f"{tag}m", "ones_f1"], wr=[f"{tag}psD"])
            p.op("vector", "tensor_copy", a_sb[:, :], psC[:, :], rd=[f"{tag}psC"], wr=[f"{tag}a"])
            p.op("vector", "tensor_copy", b_sb[:, :], psD[:, :], rd=[f"{tag}psD"], wr=[f"{tag}b"])
            for k in range(KCH):
                dst = xln[:, k * TT:(k + 1) * TT]
                r = k % 2
                tmp = ftmp[:, r * TT:(r + 1) * TT]
                p.op("vector", "tensor_mul", tmp, xk(k), a_sb[:, :], rd=[kx, f"{tag}a"], wr=[f"{tag}ft{r}"])
                p.op("vector", "tensor_add", dst, tmp, b_sb[:, :], rd=[f"{tag}ft{r}", f"{tag}b"], wr=[f"{tag}xln"])
            out_cb(tt, xln, f"{tag}xln")


def emit_attention(p, nc, es, g, d):
    SQD = 1.0 / (DK ** 0.5)
    GRP = 8
    NGRP = NCH // GRP
    ag1_in, ag1_out = d["ag1_in"], d["ag1_out"]
    ag2_in, ag2_out = d["ag2_in"], d["ag2_out"]
    gated_dram = d["gated_dram"]

    p.op("gpsimd", "collective_compute", "AllGather", OP.bypass,
         rd=["ag1_in"], wr=["ag1_out"],
         replica_groups=[list(range(8))], ins=[ag1_in.ap()], outs=[ag1_out.ap()])

    S = lambda name, shape, dt: sbuf(nc, es, f"at_{name}", shape, dt)
    P = lambda name, shape, dt=F32: psum(nc, es, f"at_{name}", shape, dt)

    xln_seq = S("xlnseq", [128, KCH * SEQ], BF16)
    offs = g["offs_sb"]
    for half in range(2):
        for k in range(KCH):
            p.op("gpsimd", "indirect_dma_start",
                 rd=["ag1_out", "offs_sb"], wr=["xlnseq"], sem=p.dma_sem("xlnseq"),
                 out=xln_seq[:, k * SEQ + half * TOK: k * SEQ + (half + 1) * TOK],
                 out_offset=None,
                 in_=ag1_out.ap().rearrange("(p a) f -> p (a f)", p=128),
                 in_offset=IndirectOffsetOnAxis(offs[:, half * KCH + k: half * KCH + k + 1], 1))

    psU = P("psU", [128, 512]); psW1 = P("psW1", [128, 512])
    psQ = P("psQ", [128, 512]); psO = [P("psO0", [128, 512]), P("psO1", [128, 512])]
    psX = P("psX", [128, 512])
    psS = psX[:, 0:128]
    psT32v = psX[:, 256:320]
    psVt = P("psVt", [64, 128])
    psV = psVt[:, :]
    psTb = P("psTb", [128, 128], BF16)
    psR = psU[0:64, :]

    if ACUT <= 1:
        return None
    gam = S("gam", [64, SEQ], F32)

    egr = S("egr", [64, SEQ], BF16)
    kwr = S("kwr", [64, SEQ], BF16)
    gam_tm = S("gamtm", [128, 16 * KCH], F32)
    kw_tm_f = S("kwtmf", [128, 16 * KCH], F32)
    eglb = S("eglb", [128, NVH * NCH], F32)
    glrow = S("glrow", [64, NCH], F32)
    p.op("vector", "memset", gam[:, :], 0.0, wr=["gam"])
    p.op("vector", "memset", egr[:, :], 0.0, wr=["egr"])
    p.op("vector", "memset", kwr[:, :], 0.0, wr=["kwr"])
    p.op("vector", "memset", glrow[:, :], 0.0, wr=["glrow"])
    row_es = contextlib.ExitStack()
    bav = sbuf(nc, row_es, "at_bav", [64, SEQ], F32)
    wba_sb = sbuf(nc, row_es, "at_wba", [128, KCH * 64], BF16)
    p.dma("gpsimd", wba_sb.ap().rearrange("p (k c) -> p k c", k=KCH),
          d["wba"].rearrange("(k p) c -> p k c", p=128), wr=["wba_sb"], slot="wba")
    for tt in range(4):
        for k in range(KCH):
            p.op("tensor", "matmul", psR[:, :], wba_sb[:, k * 64:(k + 1) * 64],
                 xln_seq[:, k * SEQ + tt * 512: k * SEQ + (tt + 1) * 512],
                 start=(k == 0), stop=(k == KCH - 1),
                 rd=["wba_sb", "xlnseq"], wr=["at_psU"])
        p.op("vector", "tensor_copy", bav[:, tt * 512:(tt + 1) * 512], psR[:, :],
             rd=["at_psU"], wr=["bav"])
    brows = bav[0:16, :]; arows = bav[32:48, :]
    grows = bav[32:48, :]
    p.op("scalar", "activation", brows, brows, AF.Exp, scale=g["cneg1"][0:16, 0:1],
         rd=["bav"], wr=["bav"])
    p.op("vector", "tensor_scalar", brows, brows, 1.0, None, OP.add, rd=["bav"], wr=["bav"])
    p.op("vector", "reciprocal", brows, brows, rd=["bav"], wr=["bav"])
    spa = gam[32:48, :]
    p.op("vector", "tensor_scalar", arows, arows, g["dt_col"][32:48, 0:1], None, OP.add,
         rd=["bav", "dt_col"], wr=["bav"])
    p.op("scalar", "activation", spa, arows, AF.Abs, rd=["bav"], wr=["gam"])
    p.op("scalar", "activation", spa, spa, AF.Exp, scale=g["cneg1"][32:48, 0:1],
         rd=["gam"], wr=["gam"])
    p.op("scalar", "activation", spa, spa, AF.Ln, bias=g["cone"][32:48, 0:1],
         rd=["gam"], wr=["gam"])
    p.op("vector", "tensor_scalar_max", arows, arows, 0.0, rd=["bav"], wr=["bav"])
    p.op("vector", "tensor_add", arows, arows, spa, rd=["bav", "gam"], wr=["bav"])
    nea = sbuf(nc, row_es, "at_nea", [64, 1], F32)
    p.op("scalar", "activation", nea[32:48, :], g["alog_col"][32:48, 0:1], AF.Exp,
         rd=["alog_col"], wr=["nea"])
    p.op("vector", "tensor_scalar", nea[32:48, :], nea[32:48, :], -1.0, None, OP.mult,
         rd=["nea"], wr=["nea"])
    p.op("vector", "tensor_scalar", arows, arows, nea[32:48, 0:1], None, OP.mult,
         rd=["bav", "nea"], wr=["bav"])
    p.op("scalar", "activation", egr[32:48, :], grows, AF.Exp, rd=["bav"], wr=["egr"])
    p.op("vector", "tensor_copy", egr[0:16, :], brows, rd=["bav"], wr=["egr"])
    p.op("vector", "tensor_copy", glrow[32:48, :],
         bav[32:48, :].rearrange("p (c t) -> p c t", t=C)[:, :, C-1:C].rearrange("p c t -> p (c t)"),
         rd=["bav"], wr=["glrow"])
    p.op("vector", "tensor_copy", gam[32:48, :].rearrange("p (c t) -> p c t", t=C),
         bav[32:48, :].rearrange("p (c t) -> p c t", t=C)[:, :, C-1:C].broadcast_to((16, NCH, C)),
         rd=["bav"], wr=["gam"])
    p.op("vector", "tensor_sub", gam[32:48, :], gam[32:48, :], grows,
         rd=["gam", "bav"], wr=["gam"])
    p.op("scalar", "activation", kwr[32:48, :], gam[32:48, :], AF.Exp, rd=["gam"], wr=["kwr"])
    for c in range(NCH):
        p.op("vector", "tensor_tensor_scan", gam[32:48, c * C:(c + 1) * C],
             grows[:, c * C:(c + 1) * C], grows[:, c * C:(c + 1) * C], 0.0, OP.add, OP.bypass,
             rd=["bav"], wr=["gam"])
    psT32 = psT32v
    for i in range(KCH):
        p.op("tensor", "transpose", psT32[:, 0:64], gam[0:64, i * 128:(i + 1) * 128],
             g["ident_f1"][0:64, 0:64], rd=["gam", "ident_f1"], wr=["psT32"])
        p.op("vector", "tensor_copy", gam_tm[:, i * 16:(i + 1) * 16], psT32[:, 32:48],
             rd=["psT32"], wr=["gamtm"])
        p.op("tensor", "transpose", psTb[:, 0:64], kwr[0:64, i * 128:(i + 1) * 128],
             g["ident_bf"][0:64, 0:64], rd=["kwr", "ident_bf"], wr=["at_psTb"])
        p.op("vector", "tensor_copy", kw_tm_f[:, i * 16:(i + 1) * 16], psTb[:, 32:48],
             rd=["at_psTb"], wr=["kwtmf"])
    for vh in range(NVH):
        p.op("tensor", "matmul", psU[:, 0:NCH], g["ohg_f32"][32:64, vh * 128:(vh + 1) * 128],
             glrow[32:64, :], start=True, stop=True, rd=["glrow", "ohg_f32"], wr=["at_psU"])
        p.op("scalar", "activation", eglb[:, vh * NCH:(vh + 1) * NCH], psU[:, 0:NCH], AF.Exp,
             rd=["at_psU"], wr=["eglb"])
    row_es.close()
    p.barrier()
    return dict(xln_seq=xln_seq, bav=None, gam=gam, egr=egr, kwr=kwr,
                gam_tm=gam_tm, kw_tm_f=kw_tm_f, eglb=eglb,
                psU=psU, psW1=psW1, psQ=psQ, psO=psO, psS=psS, psV=psV, psTb=psTb)


def emit_subgroup(p, nc, es, g, d, ctx, sg):
    SQD = 1.0 / (DK ** 0.5)
    GRP = 8
    NGRP = NCH // GRP
    xln_seq = ctx["xln_seq"]; gam = ctx["gam"]; egr = ctx["egr"]
    gam_tm = ctx["gam_tm"]; kw_tm_f = ctx["kw_tm_f"]; eglb = ctx["eglb"]
    psU = ctx["psU"]; psW1 = ctx["psW1"]; psQ = ctx["psQ"]; psO = ctx["psO"]
    psS = ctx["psS"]; psV = ctx["psV"]; psTb = ctx["psTb"]
    vh0, vh1 = 2 * sg, 2 * sg + 1

    with contextlib.ExitStack() as ses:
        S = lambda name, shape, dt: sbuf(nc, ses, f"sg{sg}_{name}", shape, dt)
        raw = S("raw", [128, (3 + SEQ)], BF16)
        qc = S("qc", [128, SEQ], BF16)
        kc = S("kc", [128, SEQ], BF16)
        vc = S("vc", [128, 2 * SEQ], BF16)
        zbnc = S("zbnc", [128, 512], BF16)
        facc = S("facc", [128, 512], F32)
        wring = S("wring", [128, 2 * KCH * 128], BF16)
        rsr = S("rsr", [64, SEQ], F32)
        rsrb = S("rsrb", [64, SEQ], BF16)
        sqt = S("sqt", [128, 512], BF16)
        ktl = S("ktl", [128, 512], BF16)
        qtl = S("qtl", [128, 512], BF16)
        kbA = S("kbA", [128, 2 * 512], BF16)
        kbg = S("kbg", [128, 2 * 512], BF16)
        vbf = S("vbf", [128, 2 * 512], BF16)
        qtg = S("qtg", [128, 2 * 512], BF16)
        scb = S("scb", [128, 6 * 512], BF16)
        args = S("args", [64, 2 * 512], F32)
        dinc = S("dinc", [64, 2 * 512], BF16)
        dincT = S("dincT", [64, 2 * 512], BF16)
        wA = S("wA", [64, 2 * 512], BF16)
        wR = S("wR", [64, 2 * 512], BF16)
        powA = S("powA", [128, 5 * GRP * 64], BF16)
        powR = S("powR", [128, 6 * GRP * 64], BF16)
        aTt = S("aTt", [128, GRP * 64], BF16)
        vbtm = S("vbtm", [128, GRP * 128], BF16)
        kwtm = S("kwtm", [128, GRP * 128], BF16)
        Sst = S("Sst", [128, 2 * 128], F32)
        Sbf = S("Sbf", [128, 2 * 128], BF16)
        Zb = S("Zb", [128, 2 * 128], BF16)
        Yb = S("Yb", [128, 2 * 128], BF16)
        vnw = S("vnw", [128, 2 * 128], BF16)
        gbc = S("gbc", [128, 512], BF16)
        grs = facc[0:1, 0:512]
        gout = zbnc
        gf = sqt
        siz = S("siz", [128, 512], BF16)

        KEY = lambda n: f"sg{sg}_{n}"

        cols = [("q", sg, qc, 0), ("k", 8 + sg, kc, 1),
                ("v0", 16 + 2 * sg, vc[:, 0:SEQ], 2), ("v1", 17 + 2 * sg, vc[:, SEQ:2*SEQ], 3),
                ("z0", 32 + 2 * sg, d["z_dram"][:, 0:SEQ], None), ("z1", 33 + 2 * sg, d["z_dram"][:, SEQ:2*SEQ], None)]

        def wq_load(cc, r):
            dst = wring[:, r * KCH * 128:(r + 1) * KCH * 128]
            in_t = d["wqkvz"].rearrange("(k p) c -> p k c", p=128)[:, :, cc * 128:(cc + 1) * 128]
            p.dma("gpsimd", dst.rearrange("p (k c) -> p k c", k=KCH), in_t, wr=[KEY(f"wq{r}")], slot=f"wq{r}")

        p.op("vector", "memset", raw[:, 0:3], 0.0, wr=[KEY("raw0"), KEY("raw1")])
        wq_load(cols[0][1], 0)
        wq_load(cols[1][1], 1)
        for ci, (nm, cc, dstbuf, convrow) in enumerate(cols):
            r = ci % 2
            rawslot = raw[:, 0:(3 + SEQ)]
            for tt in range(4):
                for k in range(KCH):
                    p.op("tensor", "matmul", psQ[:, :],
                         wring[:, r * KCH * 128 + k * 128: r * KCH * 128 + (k + 1) * 128],
                         xln_seq[:, k * SEQ + tt * 512: k * SEQ + (tt + 1) * 512],
                         start=(k == 0), stop=(k == KCH - 1),
                         rd=[KEY(f"wq{r}"), "xlnseq"], wr=["at_psQ"], inc=(k == KCH - 1))
                if convrow is None:
                    p.op("vector", "tensor_copy", zbnc[:, :], psQ[:, :],
                         rd=["at_psQ"], wr=[KEY("zbnc")])
                    p.dma("sync", dstbuf[:, tt * 512:(tt + 1) * 512], zbnc[:, :],
                          rd=[KEY("zbnc")], wr=[KEY("zdram")], slot="zdram")
                else:
                    p.op("vector", "tensor_copy", rawslot[:, 3 + tt * 512: 3 + (tt + 1) * 512], psQ[:, :],
                         rd=["at_psQ"], wr=[KEY(f"raw{r}")])
            if ci + 2 < len(cols):
                wq_load(cols[ci + 2][1], r)
            if convrow is not None:
                wcol = lambda j: g["convw_sb"][:, cc * 4 + j: cc * 4 + j + 1]
                bcol = g["convb_sb"][:, cc: cc + 1]
                for tt in range(4):
                    a0 = facc[:, 0:512]
                    base = tt * 512
                    p.op("vector", "tensor_scalar", a0, rawslot[:, base: base + 512],
                         wcol(0), None, OP.mult, rd=[KEY(f"raw{r}"), "convw_sb"], wr=[KEY("fa0")])
                    for j in range(1, 4):
                        p.op("vector", "scalar_tensor_tensor", a0, rawslot[:, base + j: base + j + 512],
                             wcol(j), a0, OP.mult, OP.add,
                             rd=[KEY(f"raw{r}"), "convw_sb", KEY("fa0")], wr=[KEY("fa0")])
                    p.op("scalar", "activation", dstbuf[:, base: base + 512], a0, AF.Silu,
                         bias=bcol, rd=[KEY("fa0"), "convb_sb"], wr=[KEY(nm)])

        if SCUT <= 1:
            return
        p.op("vector", "memset", rsr[:, :], 1.0, wr=[KEY("rsr")])
        p.op("vector", "memset", rsrb[:, :], 0.0, wr=[KEY("rsrb")])
        for qi, (buf, key, row) in enumerate([(qc, KEY("q"), 0), (kc, KEY("k"), 32)]):
            for tt in range(4):
                p.op("scalar", "activation", sqt[:, :], buf[:, tt * 512:(tt + 1) * 512], AF.Square,
                     rd=[key], wr=[KEY("sqt")])
                p.op("tensor", "matmul", psU[0:1, 0:512], g["ones_bf"][0:128, 0:1], sqt[:, :],
                     start=True, stop=True, rd=[KEY("sqt"), "ones_bf"], wr=["at_psU"])
                p.op("vector", "tensor_copy", rsr[row:row+1, tt * 512:(tt + 1) * 512], psU[0:1, 0:512],
                     rd=["at_psU"], wr=[KEY("rsr")])
        p.op("vector", "tensor_scalar", rsr[0:1, :], rsr[0:1, :], float(DK), float(DK) * EPS_L2,
             OP.mult, OP.add, rd=[KEY("rsr")], wr=[KEY("rsr")])
        p.op("vector", "tensor_scalar", rsr[32:33, :], rsr[32:33, :], 1.0, EPS_L2,
             OP.mult, OP.add, rd=[KEY("rsr")], wr=[KEY("rsr")])
        p.op("vector", "reciprocal", rsr[0:1, :], rsr[0:1, :], rd=[KEY("rsr")], wr=[KEY("rsr")])
        p.op("vector", "reciprocal", rsr[32:33, :], rsr[32:33, :], rd=[KEY("rsr")], wr=[KEY("rsr")])
        p.op("scalar", "activation", rsr[0:1, :], rsr[0:1, :], AF.Sqrt, rd=[KEY("rsr")], wr=[KEY("rsr")])
        p.op("scalar", "activation", rsr[32:33, :], rsr[32:33, :], AF.Sqrt, rd=[KEY("rsr")], wr=[KEY("rsr")])
        p.op("vector", "tensor_copy", rsrb[0:1, :], rsr[0:1, :], rd=[KEY("rsr")], wr=[KEY("rsrb")])
        p.op("vector", "tensor_copy", rsrb[32:33, :], rsr[32:33, :], rd=[KEY("rsr")], wr=[KEY("rsrb")])

        if SCUT <= 2:
            return
        p.op("vector", "memset", Sst[:, :], 0.0, wr=[KEY("Sst")])
        p.op("vector", "memset", Sbf[:, :], 0.0, wr=[KEY("Sbf")])

        for grp in range(NGRP):
            g0 = grp * 512
            bc_specs = [(rsrb[0:32, g0:g0+512], 0, 0, "rsr2"),
                        (rsrb[32:64, g0:g0+512], 0, 1, "rsr2"),
                        (egr[0:32, g0:g0+512], vh0, 2, "egr32"),
                        (egr[0:32, g0:g0+512], vh1, 3, "egr32"),
                        (egr[32:64, g0:g0+512], vh0, 4, "egr32"),
                        (egr[32:64, g0:g0+512], vh1, 5, "egr32")]
            for srcap, row, slot, kind in bc_specs:
                if kind == "rsr2":
                    lhs = g["oh32_sb"][(0 if slot == 0 else 32):(32 if slot == 0 else 64), 0:128]
                    rdk = [KEY("rsrb"), "oh32_sb"]
                else:
                    base = 0 if slot in (2, 3) else 32
                    lhs = g["oh32_sb"][base:base + 32, row * 128:(row + 1) * 128]
                    rdk = ["egr", "oh32_sb"]
                p.op("tensor", "matmul", psU[:, :], lhs, srcap, start=True, stop=True,
                     rd=rdk, wr=["at_psU"])
                p.op("vector", "tensor_copy", scb[:, slot * 512:(slot + 1) * 512], psU[:, :],
                     rd=["at_psU"], wr=[KEY(f"scb{slot}")])
            p.op("vector", "tensor_mul", ktl[:, :], kc[:, g0:g0+512], scb[:, 512:1024],
                 rd=[KEY("k"), KEY("scb1")], wr=[KEY("ktl")])
            p.op("vector", "tensor_mul", qtl[:, :], qc[:, g0:g0+512], scb[:, 0:512],
                 rd=[KEY("q"), KEY("scb0")], wr=[KEY("qtl")])
            for vi, vh in enumerate((vh0, vh1)):
                dst = kbA[:, vi * 512:(vi + 1) * 512]
                p.op("vector", "tensor_mul", dst, ktl[:, :], scb[:, (2 + vi) * 512:(3 + vi) * 512],
                     rd=[KEY("ktl"), KEY(f"scb{2+vi}")], wr=[KEY("kbA")])
                p.op("vector", "tensor_mul", kbg[:, vi * 512:(vi + 1) * 512], dst,
                     scb[:, (4 + vi) * 512:(5 + vi) * 512],
                     rd=[KEY("kbA"), KEY(f"scb{4+vi}")], wr=[KEY("kbg")])
                p.op("vector", "tensor_mul", vbf[:, vi * 512:(vi + 1) * 512],
                     vc[:, vi * SEQ + g0: vi * SEQ + g0 + 512],
                     scb[:, (2 + vi) * 512:(3 + vi) * 512],
                     rd=[KEY(f"v{vi}"), KEY(f"scb{2+vi}")], wr=[KEY("vbf")])
                p.op("vector", "tensor_mul", qtg[:, vi * 512:(vi + 1) * 512], qtl[:, :],
                     scb[:, (4 + vi) * 512:(5 + vi) * 512],
                     rd=[KEY("qtl"), KEY(f"scb{4+vi}")], wr=[KEY("qtg")])
            if SCUT == 31:
                continue
            for vi, vh in enumerate((vh0, vh1)):
                p.op("tensor", "matmul", psU[0:64, :], g["ohg_f32"][32:64, vh * 128: vh * 128 + 64],
                     gam[32:64, g0:g0+512], start=True, stop=True,
                     rd=["gam", "ohg_f32"], wr=["at_psU"])
                for j in range(GRP):
                    c_abs = grp * GRP + j
                    tile_i = c_abs // 2
                    pbase = (c_abs % 2) * 64
                    gcol = gam_tm[pbase:pbase + 64, tile_i * 16 + vh: tile_i * 16 + vh + 1]
                    p.op("vector", "tensor_scalar", args[:, vi * 512 + j * 64: vi * 512 + (j + 1) * 64],
                         psU[0:64, j * 64:(j + 1) * 64], gcol, None, OP.subtract,
                         rd=["at_psU", "gamtm"], wr=[KEY("args")])
                av = args[:, vi * 512:(vi + 1) * 512]
                fsc = facc[0:64, 0:512]
                p.op("vector", "tensor_add", fsc, av, g["msk_su"][:, :],
                     rd=[KEY("args"), "msk_su"], wr=[KEY("fa0")])
                p.op("scalar", "activation", dinc[:, vi * 512:(vi + 1) * 512],
                     fsc, AF.Exp, scale=g["cneg1"][0:64, 0:1],
                     rd=[KEY("fa0")], wr=[KEY("dinc")])
                p.op("vector", "tensor_sub", fsc, av, g["msk_sl"][:, :],
                     rd=[KEY("args"), "msk_sl"], wr=[KEY("fa0")])
                p.op("scalar", "activation", dincT[:, vi * 512:(vi + 1) * 512],
                     fsc, AF.Exp,
                     rd=[KEY("fa0")], wr=[KEY("dincT")])
                p.op("vector", "tensor_mul", wA[:, vi * 512:(vi + 1) * 512],
                     dinc[:, vi * 512:(vi + 1) * 512], g["msk_nsl"][:, :],
                     rd=[KEY("dinc"), "msk_nsl"], wr=[KEY("wA")])
                p.op("vector", "tensor_mul", wR[:, vi * 512:(vi + 1) * 512],
                     dincT[:, vi * 512:(vi + 1) * 512], g["msk_nsu"][:, :],
                     rd=[KEY("dincT"), "msk_nsu"], wr=[KEY("wR")])
            if SCUT == 32:
                continue
            for vi in range(2):
                for j in range(GRP):
                    c_abs = grp * GRP + j
                    ksl = ktl[:, j * 64:(j + 1) * 64]
                    kbsl = kbA[:, vi * 512 + j * 64: vi * 512 + (j + 1) * 64]
                    qsl = qtl[:, j * 64:(j + 1) * 64]
                    p.op("tensor", "matmul", psW1[0:64, 0:64], kbsl, ksl, start=True, stop=True,
                         rd=[KEY("kbA"), KEY("ktl")], wr=["at_psW1"])
                    p.op("tensor", "matmul", psW1[0:64, 64:128], ksl, kbsl, start=True, stop=True,
                         rd=[KEY("kbA"), KEY("ktl")], wr=["at_psW1"])
                    p.op("tensor", "matmul", psW1[0:64, 128:192], ksl, qsl, start=True, stop=True,
                         rd=[KEY("qtl"), KEY("ktl")], wr=["at_psW1"])
                    arow = vi * 64
                    p.op("vector", "tensor_mul", powA[arow:arow + 64, 0 * GRP * 64 + j * 64: 0 * GRP * 64 + (j + 1) * 64],
                         psW1[0:64, 0:64], wA[:, vi * 512 + j * 64: vi * 512 + (j + 1) * 64],
                         rd=["at_psW1", KEY("wA")], wr=[KEY("powA")])
                    p.op("vector", "tensor_mul", powR[arow:arow + 64, 0 * GRP * 64 + j * 64: 0 * GRP * 64 + (j + 1) * 64],
                         psW1[0:64, 64:128], wR[:, vi * 512 + j * 64: vi * 512 + (j + 1) * 64],
                         rd=["at_psW1", KEY("wR")], wr=[KEY("powR")])
                    p.op("vector", "tensor_mul", aTt[arow:arow + 64, j * 64:(j + 1) * 64],
                         psW1[0:64, 128:192], dincT[:, vi * 512 + j * 64: vi * 512 + (j + 1) * 64],
                         rd=["at_psW1", KEY("dincT")], wr=[KEY("aTt")])
            if SCUT == 33:
                continue
            for lvl in range(1, 6):
                for vi in range(2):
                    arow = vi * 64
                    for j in range(GRP):
                        Aj = powA[arow:arow + 64, (lvl - 1) * GRP * 64 + j * 64: (lvl - 1) * GRP * 64 + (j + 1) * 64]
                        Rj = powR[arow:arow + 64, (lvl - 1) * GRP * 64 + j * 64: (lvl - 1) * GRP * 64 + (j + 1) * 64]
                        tp = None if vi == 0 else (64, 64)
                        if lvl < 5:
                            p.op("tensor", "matmul", psQ[0:64, j * 64:(j + 1) * 64] if vi == 0 else psQ[64:128, j * 64:(j + 1) * 64],
                                 Rj, Aj, start=True, stop=True, tile_position=tp,
                                 rd=[KEY("powA"), KEY("powR")], wr=["at_psQ"])
                        p.op("tensor", "matmul", psU[0:64, j * 64:(j + 1) * 64] if vi == 0 else psU[64:128, j * 64:(j + 1) * 64],
                             Aj, Rj, start=True, stop=True, tile_position=tp,
                             rd=[KEY("powA"), KEY("powR")], wr=["at_psU"])
                for vi in range(2):
                    pr = slice(vi * 64, vi * 64 + 64)
                    if lvl < 5:
                        p.op("vector", "tensor_copy", powA[pr, lvl * GRP * 64:(lvl + 1) * GRP * 64],
                             (psQ[0:64, 0:512] if vi == 0 else psQ[64:128, 0:512]),
                             rd=["at_psQ"], wr=[KEY("powA")])
                    p.op("vector", "tensor_copy", powR[pr, lvl * GRP * 64:(lvl + 1) * GRP * 64],
                         (psU[0:64, 0:512] if vi == 0 else psU[64:128, 0:512]),
                         rd=["at_psU"], wr=[KEY("powR")])
            if SCUT == 34:
                continue
            for vi in range(2):
                for j in range(GRP):
                    c_abs = grp * GRP + j
                    tile_i = c_abs // 2
                    pbase = (c_abs % 2) * 64
                    p.op("tensor", "transpose", psTb[0:64, :],
                         vbf[:, vi * 512 + j * 64: vi * 512 + (j + 1) * 64], g["ident_bf"][:, :],
                         rd=[KEY("vbf"), "ident_bf"], wr=["at_psTb"])
                    p.op("vector", "tensor_copy", vbtm[vi * 64:(vi + 1) * 64, j * 128:(j + 1) * 128],
                         psTb[0:64, :], rd=["at_psTb"], wr=[KEY("vbtm")])
                    p.op("tensor", "transpose", psTb[0:64, :],
                         ktl[:, j * 64:(j + 1) * 64], g["ident_bf"][:, :],
                         rd=[KEY("ktl"), "ident_bf"], wr=["at_psTb"])
                    kwcol = kw_tm_f[pbase:pbase + 64, tile_i * 16 + vh0 + vi: tile_i * 16 + vh0 + vi + 1]
                    p.op("vector", "tensor_scalar", kwtm[vi * 64:(vi + 1) * 64, j * 128:(j + 1) * 128],
                         psTb[0:64, :], kwcol, None, OP.mult,
                         rd=["at_psTb", "kwtmf"], wr=[KEY("kwtm")])
            if SCUT <= 3:
                continue
            for j in range(GRP):
                c_abs = grp * GRP + j
                for vi, vh in enumerate((vh0, vh1)):
                    arow = vi * 64
                    Ssl = Sst[:, vi * 128:(vi + 1) * 128]
                    Sbsl = Sbf[:, vi * 128:(vi + 1) * 128]
                    p.op("tensor", "matmul", psV,
                         kbg[:, vi * 512 + j * 64: vi * 512 + (j + 1) * 64], Sbsl,
                         start=True, stop=True, rd=[KEY("kbg"), KEY("Sbf")], wr=["at_psV"])
                    p.op("vector", "tensor_sub", Yb[arow:arow + 64, 0:128],
                         vbtm[arow:arow + 64, j * 128:(j + 1) * 128], psV,
                         rd=[KEY("vbtm"), "at_psV"], wr=[KEY("Yb")])
                    srcb, dstb = Yb, Zb
                    for lvl in range(6):
                        Rl = powR[arow:arow + 64, lvl * GRP * 64 + j * 64: lvl * GRP * 64 + (j + 1) * 64]
                        p.op("tensor", "matmul", psV, g["ident_bf"][arow:arow + 64, arow:arow + 64],
                             srcb[arow:arow + 64, 0:128], start=True, stop=False,
                             rd=[KEY("Yb" if srcb is Yb else "Zb"), "ident_bf"], wr=["at_psV"], inc=False)
                        p.op("tensor", "matmul", psV, Rl,
                             srcb[arow:arow + 64, 0:128], start=False, stop=True,
                             rd=[KEY("Yb" if srcb is Yb else "Zb"), KEY("powR")], wr=["at_psV"])
                        dk = KEY("Zb" if dstb is Zb else "Yb")
                        if lvl == 5:
                            p.op("vector", "tensor_copy", vnw[arow:arow + 64, 0:128], psV,
                                 rd=["at_psV"], wr=[KEY("vnw")])
                        else:
                            p.op("vector", "tensor_copy", dstb[arow:arow + 64, 0:128], psV,
                                 rd=["at_psV"], wr=[dk])
                        srcb, dstb = dstb, srcb
                    p.op("tensor", "matmul", psO[vi][:, j * 64:(j + 1) * 64], Sbsl,
                         qtg[:, vi * 512 + j * 64: vi * 512 + (j + 1) * 64],
                         start=True, stop=False, rd=[KEY("Sbf"), KEY("qtg")], wr=[f"at_psO{vi}"], inc=False)
                    p.op("tensor", "matmul", psO[vi][:, j * 64:(j + 1) * 64],
                         vnw[arow:arow + 64, 0:128], aTt[arow:arow + 64, j * 64:(j + 1) * 64],
                         start=False, stop=True, rd=[KEY("vnw"), KEY("aTt")], wr=[f"at_psO{vi}"])
                    p.op("tensor", "matmul", psS, kwtm[arow:arow + 64, j * 128:(j + 1) * 128],
                         vnw[arow:arow + 64, 0:128], start=True, stop=True,
                         rd=[KEY("kwtm"), KEY("vnw")], wr=["at_psS"])
                    eglcol = eglb[:, (vh0 + vi) * NCH + c_abs: (vh0 + vi) * NCH + c_abs + 1]
                    p.op("vector", "scalar_tensor_tensor", Ssl, Ssl, eglcol, psS,
                         OP.mult, OP.add, rd=[KEY("Sst"), "eglb", "at_psS"], wr=[KEY("Sst")])
                    p.op("vector", "tensor_copy", Sbsl, Ssl, rd=[KEY("Sst")], wr=[KEY("Sbf")])
            if SCUT <= 4:
                continue
            for vi, vh in enumerate((vh0, vh1)):
                p.dma("sync", zbnc[:, :], d["z_dram"][:, vi * SEQ + g0: vi * SEQ + g0 + 512],
                      rd=[KEY("zdram")], wr=[KEY("zbnc")], slot="zbnc_l")
                p.op("scalar", "activation", siz[:, :], zbnc[:, :],
                     AF.Silu, rd=[KEY("zbnc")], wr=[KEY("siz")])
                p.op("scalar", "activation", sqt[:, :], psO[vi][:, :], AF.Square,
                     rd=[f"at_psO{vi}"], wr=[KEY("sqt")])
                p.op("tensor", "matmul", psU[0:1, 0:512], g["ones_bf"][0:128, 0:1], sqt[:, :],
                     start=True, stop=True, rd=[KEY("sqt"), "ones_bf"], wr=["at_psU"])
                p.op("vector", "tensor_scalar", grs, psU[0:1, 0:512], 1.0 / DV, EPS_RMS,
                     OP.mult, OP.add, rd=["at_psU"], wr=[KEY("fa0")])
                p.op("vector", "reciprocal", grs, grs, rd=[KEY("fa0")], wr=[KEY("fa0")])
                p.op("scalar", "activation", grs, grs, AF.Sqrt,
                     rd=[KEY("fa0")], wr=[KEY("fa0")])
                p.op("tensor", "matmul", psU[:, :], g["ones_f1"][0:1, 0:128], grs,
                     start=True, stop=True, rd=[KEY("fa0"), "ones_f1"], wr=["at_psU"])
                p.op("vector", "tensor_copy", gbc[:, :], psU[:, :], rd=["at_psU"], wr=[KEY("gbc")])
                p.op("vector", "scalar_tensor_tensor", gf[:, :], psO[vi][:, :], g["nw_col"][:, 0:1],
                     gbc[:, :], OP.mult, OP.mult,
                     rd=[f"at_psO{vi}", "nw_col", KEY("gbc")], wr=[KEY("sqt")])
                p.op("vector", "tensor_mul", gout[:, :], gf[:, :], siz[:, :],
                     rd=[KEY("sqt"), KEY("siz")], wr=[KEY("zbnc")])
                p.dma("sync", d["gated_dram"].rearrange("(v h t) c -> v h t c", h=2, t=TOK)
                      if False else d["gated_dram"][sg * 256 + vi * 128: sg * 256 + (vi + 1) * 128,
                                                    g0:g0 + 512],
                      gout[:, :], rd=[KEY("zbnc")], wr=["gated_dram"], slot="gdram")
        if d.get("dbg") and sg == 0:
            dbg = d["dbg"]
            p.dma("sync", dbg["dbg_qc"], qc[:, :], rd=[KEY("q")], slot="dbg")
            p.dma("sync", dbg["dbg_kc"], kc[:, :], rd=[KEY("k")], slot="dbg")
            p.dma("sync", dbg["dbg_vc"], vc[:, :], rd=[KEY("v0"), KEY("v1")], slot="dbg")
            p.dma("sync", dbg["dbg_rsr"], rsr[:, :], rd=[KEY("rsr")], slot="dbg")
            p.dma("sync", dbg["dbg_Sst"], Sst[:, :], rd=[KEY("Sst")], slot="dbg")
            p.dma("sync", dbg["dbg_vbtm"], vbtm[:, :], rd=[KEY("vbtm")], slot="dbg")
            p.dma("sync", dbg["dbg_kwtm"], kwtm[:, :], rd=[KEY("kwtm")], slot="dbg")
            p.dma("sync", dbg["dbg_aTt"], aTt[:, :], rd=[KEY("aTt")], slot="dbg")
            p.dma("sync", dbg["dbg_powR"], powR[:, :], rd=[KEY("powR")], slot="dbg")
            p.dma("sync", dbg["dbg_qtg"], qtg[:, :], rd=[KEY("qtg")], slot="dbg")
            p.dma("sync", dbg["dbg_kbg"], kbg[:, :], rd=[KEY("kbg")], slot="dbg")
            p.dma("sync", dbg["dbg_vbf"], vbf[:, :], rd=[KEY("vbf")], slot="dbg")
            p.dma("sync", dbg["dbg_ktl"], ktl[:, :], rd=[KEY("ktl")], slot="dbg")
            p.dma("sync", dbg["dbg_dincT"], dincT[:, :], rd=[KEY("dincT")], slot="dbg")
            p.dma("sync", dbg["dbg_wA"], wA[:, :], rd=[KEY("wA")], slot="dbg")


def emit_ag2(p, nc, es, g, d):
    gated_rows = d["gated_dram"].rearrange("v (h t) -> (v h) t", h=2)
    with contextlib.ExitStack() as aes:
        bnc = sbuf(nc, aes, "ag2bnc", [128, TOK], BF16)
        for vch in range(16):
            p.op("gpsimd", "indirect_dma_start",
                 rd=["gated_dram", "offs_sb"], wr=["ag2bnc"], sem=p.dma_sem("ag2bnc"),
                 out=bnc[:, :], out_offset=None,
                 in_=d["gated_dram"].rearrange("(p a) f -> p (a f)", p=128),
                 in_offset=IndirectOffsetOnAxis(g["offs_sb"][:, 32 + vch: 33 + vch], 1))
            p.dma("gpsimd", d["ag2_in"][vch * 128:(vch + 1) * 128, :], bnc[:, :],
                  rd=["ag2bnc"], wr=["ag2_in"], slot="ag2_in")
    p.op("gpsimd", "collective_compute", "AllGather", OP.bypass,
         rd=["ag2_in"], wr=["ag2_out"],
         replica_groups=[list(range(8))], ins=[d["ag2_in"]], outs=[d["ag2_out"]])


def emit_wout(p, nc, es, g, d, x_res2):
    with contextlib.ExitStack() as wes:
        S = lambda name, shape, dt: sbuf(nc, wes, f"wo_{name}", shape, dt)
        P_ = lambda name, shape, dt=F32: psum(nc, wes, f"wo_{name}", shape, dt)
        psA = P_("psA", [128, 512]); psB = P_("psB", [128, 512])
        gt = S("gt", [128, 32 * TOK], BF16)
        wrg = S("wrg", [128, 2 * 32 * 128], BF16)
        gated_rows = d["gated_dram"].rearrange("v (h t) -> (v h) t", h=2)
        for vch in range(16):
            p.op("gpsimd", "indirect_dma_start",
                 rd=["gated_dram", "offs_sb"], wr=[f"gtg{vch // 8}"], sem=p.dma_sem(f"gtg{vch // 8}"),
                 out=gt[:, vch * TOK:(vch + 1) * TOK], out_offset=None,
                 in_=d["gated_dram"].rearrange("(p a) f -> p (a f)", p=128),
                 in_offset=IndirectOffsetOnAxis(g["offs_sb"][:, 48 + vch: 49 + vch], 1))
        for vch in range(16):
            p.op("gpsimd", "indirect_dma_start",
                 rd=["ag2_out", "offs_sb"], wr=[f"gtg{2 + vch // 8}"], sem=p.dma_sem(f"gtg{2 + vch // 8}"),
                 out=gt[:, (16 + vch) * TOK:(17 + vch) * TOK], out_offset=None,
                 in_=d["ag2_out"].rearrange("(p a) f -> p (a f)", p=128),
                 in_offset=IndirectOffsetOnAxis(g["offs_sb"][:, 64 + vch: 65 + vch], 1))

        def w_load(m, r):
            dst = wrg[:, r * 32 * 128:(r + 1) * 32 * 128]
            p.dma("gpsimd", dst[:, 0:16 * 128].rearrange("p (v m) -> p v m", v=16),
                  d["w_out_mine"].rearrange("(v p) m -> p v m", p=128)[:, :, m * 128:(m + 1) * 128],
                  wr=[f"wo_w{r}"])
            p.dma("gpsimd", dst[:, 16 * 128:32 * 128].rearrange("p (v m) -> p v m", v=16),
                  d["w_out_partner"].rearrange("(v p) m -> p v m", p=128)[:, :, m * 128:(m + 1) * 128],
                  wr=[f"wo_w{r}"])
        w_load(0, 0); w_load(1, 1)
        for m in range(KCH):
            r = m % 2
            pd = psA if r == 0 else psB
            pk = "wo_psA" if r == 0 else "wo_psB"
            for tt in range(NTT):
                for vch in range(32):
                    p.op("tensor", "matmul", pd[:, :],
                         wrg[:, r * 32 * 128 + vch * 128: r * 32 * 128 + (vch + 1) * 128],
                         gt[:, vch * TOK + tt * TT: vch * TOK + (tt + 1) * TT],
                         start=(vch == 0), stop=(vch == 31),
                         rd=[f"wo_w{r}", f"gtg{vch // 8}"], wr=[pk], inc=(vch == 31))
                dst = x_res2[:, m * TOK + tt * TT: m * TOK + (tt + 1) * TT]
                p.op("vector", "tensor_add", dst, pd[:, :], dst, rd=[pk, "ffn2xres"], wr=["ffn2xres"])
            if m + 2 < KCH:
                w_load(m + 2, r)



def build_program(stages=("ffn1", "attn", "ffn2")):
    nc = bass.Bass("TRN2", target_bir_lowering=False, debug=False)
    d = {}
    di = lambda name, shape, dt=F32: nc.dram_tensor(name, shape, dt, kind="ExternalInput").ap()
    d["x_in"] = di("x_in", [KCH, 128, TOK])
    d["wg"] = [di(f"wg{e}", [E, HEXP]) for e in range(NEXP)]
    d["wu"] = [di(f"wu{e}", [E, HEXP]) for e in range(NEXP)]
    d["wd"] = [di(f"wd{e}", [HEXP, E]) for e in range(NEXP)]
    d["shg"] = di("shg", [E, HSH]); d["shu"] = di("shu", [E, HSH])
    d["shd"] = di("shd", [HSH, E])
    d["gw"] = di("gw", [E, NEXP]); d["seg"] = di("seg", [E, 1])
    d["ident_in"] = di("ident_in", [128, 128], F32)
    d["oh8_in"] = di("oh8_in", [8, 8 * 128], F32)
    d["oh32_in"] = di("oh32_in", [64, 16 * 128], F32)
    d["oh2_in"] = di("oh2_in", [2, 2 * 128], F32)
    d["wqkvz"] = di("wqkvz", [E, 6144])
    d["wba"] = di("wba", [E, 64])
    d["convw_in"] = di("convw_in", [128, 128], F32)
    d["convb_in"] = di("convb_in", [128, 32], F32)
    d["alog_in"] = di("alog_in", [64, 1], F32)
    d["dt_in"] = di("dt_in", [64, 1], F32)
    d["nw_in"] = di("nw_in", [128, 1], F32)
    d["w_out_mine"] = di("w_out_mine", [2048, 2048])
    d["w_out_partner"] = di("w_out_partner", [2048, 2048])
    d["msk_in"] = di("msk_in", [64, 4 * 512], F32)
    d["offs_in"] = di("offs_in", [128, 80], I32)
    y = nc.dram_tensor("y", [KCH, 128, TOK], F32, kind="ExternalOutput").ap()
    dbg = {}
    if DEBUG:
        for nm, shape, dt in [("dbg_xlnseq", [128, KCH * SEQ], BF16),
                              ("dbg_gam", [64, SEQ], F32), ("dbg_egr", [64, SEQ], BF16),
                              ("dbg_kwr", [64, SEQ], BF16), ("dbg_eglb", [128, NVH * NCH], F32),
                              ("dbg_gamtm", [128, 16 * KCH], F32), ("dbg_kwtmf", [128, 16 * KCH], F32),
                              ("dbg_qc", [128, SEQ], BF16), ("dbg_kc", [128, SEQ], BF16),
                              ("dbg_vc", [128, 2 * SEQ], BF16), ("dbg_rsr", [64, SEQ], F32),
                              ("dbg_Sst", [128, 2 * 128], F32),
                              ("dbg_vbtm", [128, 8 * 128], BF16), ("dbg_kwtm", [128, 8 * 128], BF16),
                              ("dbg_aTt", [128, 8 * 64], BF16), ("dbg_powR", [128, 6 * 8 * 64], BF16),
                              ("dbg_qtg", [128, 2 * 512], BF16), ("dbg_kbg", [128, 2 * 512], BF16),
                              ("dbg_vbf", [128, 2 * 512], BF16), ("dbg_ktl", [128, 512], BF16),
                              ("dbg_gated", [2048, SEQ], BF16), ("dbg_zd", [128, 2 * SEQ], BF16),
                              ("dbg_dincT", [64, 2 * 512], BF16), ("dbg_wA", [64, 2 * 512], BF16)]:
            dbg[nm] = nc.dram_tensor(nm, shape, dt, kind="ExternalOutput").ap()
    d["dbg"] = dbg
    d["ag1_in"] = nc.dram_tensor("ag1_in", [128, KCH * TOK], BF16)
    d["ag1_out"] = nc.dram_tensor("ag1_out", [8 * 128, KCH * TOK], BF16)
    d["xres_dram"] = nc.dram_tensor("xres_dram", [128, KCH * TOK], F32).ap()
    d["gated_dram"] = nc.dram_tensor("gated_dram", [2048, SEQ], BF16).ap()
    d["ag2_in"] = nc.dram_tensor("ag2_in", [2048, TOK], BF16).ap()
    d["z_dram"] = nc.dram_tensor("z_dram", [128, 2 * SEQ], BF16).ap()
    d["ag2_out"] = nc.dram_tensor("ag2_out", [8 * 2048, TOK], BF16).ap()

    with contextlib.ExitStack() as es:
        p = PB(nc, es)
        g = dict(d)
        for nm, shape, dt in [("ones_bf", [128, 128], BF16), ("ones_f1", [1, 128], F32),
                              ("cneg1", [128, 1], F32), ("cone", [128, 1], F32),
                              ("ident_bf", [128, 128], BF16),
                              ("oh8_sb", [8, 8 * 128], BF16),
                              ("gw_sb", [128, KCH * NEXP], BF16), ("seg_sb", [128, KCH], BF16),
                              ("nw_col", [128, 1], F32),
                              ("offs_sb", [128, 80], I32)]:
            g[nm] = sbuf(nc, es, nm, shape, dt)
        g["ident64"] = g["ident_bf"]

        p.op("vector", "memset", g["ones_bf"][:, :], 1.0, wr=["ones_bf"])
        p.op("vector", "memset", g["ones_f1"][:, :], 1.0, wr=["ones_f1"])
        p.op("vector", "memset", g["cneg1"][:, :], -1.0, wr=["cneg1"])
        p.op("vector", "memset", g["cone"][:, :], 1.0, wr=["cone"])
        p.dma("gpsimd", g["ident_bf"][:, :], d["ident_in"][:, :], wr=["ident_bf"], slot="consts")
        p.dma("gpsimd", g["oh8_sb"][:, :], d["oh8_in"][:, :], wr=["oh8_sb"], slot="consts")
        p.dma("gpsimd", g["gw_sb"].ap().rearrange("p (k e) -> p k e", k=KCH),
              d["gw"].rearrange("(k p) e -> p k e", p=128), wr=["gw_sb"], slot="consts")
        p.dma("gpsimd", g["seg_sb"].ap().rearrange("p (k e) -> p k e", k=KCH),
              d["seg"].rearrange("(k p) e -> p k e", p=128), wr=["seg_sb"], slot="consts",
              allow_slow_non_contiguous=True)
        p.dma("gpsimd", g["nw_col"][:, :], d["nw_in"][:, :], wr=["nw_col"], slot="consts")
        ev = p.dma("gpsimd", g["offs_sb"][:, :], d["offs_in"][:, :], wr=["offs_sb"], slot="consts")
        for kk in ["ident_bf", "oh8_sb", "gw_sb", "seg_sb", "nw_col", "offs_sb"]:
            p.last_w[kk] = ev

        do_attn = "attn" in stages
        with contextlib.ExitStack() as s1:
            x_res = sbuf(nc, s1, "x_res", [128, KCH * TOK], F32)
            g["x_res"] = x_res
            p.dma("sync", x_res.ap().rearrange("p (k t) -> p k t", k=KCH),
                  d["x_in"].rearrange("k p t -> p k t"), wr=["ffn1xres"])
            if "ffn1" in stages:
                emit_ffn(p, nc, s1, g, "ffn1")
            if do_attn:
                p.barrier()
                def ln2_cb(tt, xln, key):
                    p.dma("sync", d["ag1_in"].ap().rearrange("p (k t) -> p k t", k=KCH)[:, :, tt * TT:(tt + 1) * TT],
                          xln.ap().rearrange("p (k t) -> p k t", k=KCH), rd=[key], wr=["ag1_in"],
                          slot="ag1_in")
                emit_ln_to(p, nc, s1, g, x_res, ln2_cb, "ln2")
                p.dma("sync", d["xres_dram"].rearrange("p (k t) -> p k t", k=KCH),
                      x_res.ap().rearrange("p (k t) -> p k t", k=KCH), rd=["ffn1xres"], wr=["xres_spill"])
            else:
                p.dma("sync", y.rearrange("k p t -> p k t"),
                      x_res.ap().rearrange("p (k t) -> p k t", k=KCH), rd=["ffn1xres"], wr=["yout"])
        if do_attn:
            p.barrier()
            with contextlib.ExitStack() as s2:
                for nm, shape, dt in [("ident_f1", [128, 128], F32),
                                      ("oh32_sb", [64, 16 * 128], BF16),
                                      ("oh2_sb", [2, 2 * 128], BF16),
                                      ("convw_sb", [128, 128], F32), ("convb_sb", [128, 32], F32),
                                      ("alog_col", [64, 1], F32), ("dt_col", [64, 1], F32),
                                      ("msk_su", [64, 512], F32), ("msk_sl", [64, 512], F32),
                                      ("msk_nsl", [64, 512], F32), ("msk_nsu", [64, 512], F32)]:
                    g[nm] = sbuf(nc, s2, nm, shape, dt)
                _ac = ["ident_f1", "oh32_sb", "oh2_sb", "convw_sb", "convb_sb",
                       "alog_col", "dt_col", "msk_su", "msk_sl", "msk_nsl", "msk_nsu"]
                p.dma("gpsimd", g["ident_f1"][:, :], d["ident_in"][:, :], wr=["ident_f1"], slot="aconsts")
                p.dma("gpsimd", g["oh32_sb"][:, :], d["oh32_in"][:, :], wr=["oh32_sb"], slot="aconsts")
                p.dma("gpsimd", g["oh2_sb"][:, :], d["oh2_in"][:, :], wr=["oh2_sb"], slot="aconsts")
                p.dma("gpsimd", g["convw_sb"][:, :], d["convw_in"][:, :], wr=["convw_sb"], slot="aconsts")
                p.dma("gpsimd", g["convb_sb"][:, :], d["convb_in"][:, :], wr=["convb_sb"], slot="aconsts")
                p.dma("gpsimd", g["alog_col"][:, :], d["alog_in"][:, :], wr=["alog_col"], slot="aconsts")
                p.dma("gpsimd", g["dt_col"][:, :], d["dt_in"][:, :], wr=["dt_col"], slot="aconsts")
                p.dma("gpsimd", g["msk_su"][:, :], d["msk_in"][:, 0:512], wr=["msk_su"], slot="aconsts")
                p.dma("gpsimd", g["msk_sl"][:, :], d["msk_in"][:, 512:1024], wr=["msk_sl"], slot="aconsts")
                p.dma("gpsimd", g["msk_nsl"][:, :], d["msk_in"][:, 1024:1536], wr=["msk_nsl"], slot="aconsts")
                p.dma("gpsimd", g["msk_nsu"][:, :], d["msk_in"][:, 1536:2048], wr=["msk_nsu"], slot="aconsts")
                g["ohg_f32"] = sbuf(nc, s2, "ohg_f32", [64, 2048], F32)
                ev = p.dma("gpsimd", g["ohg_f32"][:, :], d["oh32_in"][:, 0:2048],
                           wr=["ohg_f32"], slot="aconsts")
                for kk in _ac + ["ohg_f32"]:
                    p.last_w[kk] = ev
                ctx = emit_attention(p, nc, s2, g, d)
                if ctx is None:
                    pass
                if DEBUG and ctx is not None:
                    dbg = d["dbg"]
                    p.dma("sync", dbg["dbg_xlnseq"], ctx["xln_seq"][:, :], rd=["xlnseq"], slot="dbg")
                    p.dma("sync", dbg["dbg_gam"], ctx["gam"][:, :], rd=["gam"], slot="dbg")
                    p.dma("sync", dbg["dbg_egr"], ctx["egr"][:, :], rd=["egr"], slot="dbg")
                    p.dma("sync", dbg["dbg_kwr"], ctx["kwr"][:, :], rd=["kwr"], slot="dbg")
                    p.dma("sync", dbg["dbg_eglb"], ctx["eglb"][:, :], rd=["eglb"], slot="dbg")
                    p.dma("sync", dbg["dbg_gamtm"], ctx["gam_tm"][:, :], rd=["gamtm"], slot="dbg")
                    p.dma("sync", dbg["dbg_kwtmf"], ctx["kw_tm_f"][:, :], rd=["kwtmf"], slot="dbg")
                if ctx is not None and ACUT >= 3:
                    for sg in range(NSG):
                        emit_subgroup(p, nc, s2, g, d, ctx, sg)
                        p.barrier()
                if DEBUG:
                    p.dma("sync", d["dbg"]["dbg_gated"], d["gated_dram"][:, :], rd=["gated_dram"], slot="dbg")
                    p.dma("sync", d["dbg"]["dbg_zd"], d["z_dram"][:, :], rd=["sg7_zdram"], slot="dbg")
                if ACUT >= 4:
                    emit_ag2(p, nc, s2, g, d)
            p.barrier()
            with contextlib.ExitStack() as s3:
                x_res2 = sbuf(nc, s3, "x_res2", [128, KCH * TOK], F32)
                g["x_res"] = x_res2
                p.dma("sync", x_res2.ap().rearrange("p (k t) -> p k t", k=KCH),
                      d["xres_dram"].rearrange("p (k t) -> p k t", k=KCH),
                      rd=["xres_spill"], wr=["ffn2xres"])
                if ACUT >= 5:
                    emit_wout(p, nc, s3, g, d, x_res2)
                p.barrier()
                if "ffn2" in stages:
                    emit_ffn(p, nc, s3, g, "ffn2")
                p.dma("sync", y.rearrange("k p t -> p k t"),
                      x_res2.ap().rearrange("p (k t) -> p k t", k=KCH), rd=["ffn2xres"], wr=["yout"])
        yev = p.last_w["yout"]
        p.stream["sync"].append(lambda h, s=p.sem[yev[0]], v=yev[1]: h.wait_ge(s, v))
        block = es.enter_context(nc.Block())
        p.finalize(block)
    return nc


def host_prep(inputs):
    x = np.ascontiguousarray(np.asarray(inputs["L0_attn_hidden_states"]))
    wq = np.asarray(inputs["L0_attn_w_qkvz"]); wba = np.asarray(inputs["L0_attn_w_ba"])
    convw = np.asarray(inputs["L0_attn_conv_w"]); convb = np.asarray(inputs["L0_attn_conv_b"])
    alog = np.asarray(inputs["L0_attn_A_log"]); dtb = np.asarray(inputs["L0_attn_dt_bias"])
    nw = np.asarray(inputs["L0_attn_norm_weight"]); wo = np.asarray(inputs["L0_attn_w_out"])
    gw = np.asarray(inputs["L0_ffn_gate_w"])
    wg = np.asarray(inputs["L0_ffn_w_gate"]); wu = np.asarray(inputs["L0_ffn_w_up"])
    wd = np.asarray(inputs["L0_ffn_w_down"])
    shg = np.asarray(inputs["L0_ffn_sh_gate"]); shu = np.asarray(inputs["L0_ffn_sh_up"])
    shd = np.asarray(inputs["L0_ffn_sh_down"]); seg = np.asarray(inputs["L0_ffn_se_gate_w"])

    xt = np.ascontiguousarray(x.transpose(0, 2, 1))
    wg_t = np.ascontiguousarray(wg.transpose(0, 2, 1))
    wu_t = np.ascontiguousarray(wu.transpose(0, 2, 1))
    wd_t = np.ascontiguousarray(wd.transpose(0, 2, 1))
    shg_t = np.ascontiguousarray(shg.T); shu_t = np.ascontiguousarray(shu.T)
    shd_t = np.ascontiguousarray(shd.T)
    gw_t = np.ascontiguousarray(gw.T); seg_t = np.ascontiguousarray(seg.T)
    ident = np.eye(128, dtype=np.float32)
    oh8 = np.zeros((8, 8 * 128), np.float32)
    for e in range(8):
        oh8[e, e * 128:(e + 1) * 128] = 1.0
    oh16 = np.zeros((32, 16 * 128), np.float32)
    for e in range(16):
        oh16[e, e * 128:(e + 1) * 128] = 1.0
    oh32 = np.concatenate([oh16, oh16], axis=0)
    oh2 = np.zeros((2, 2 * 128), np.float32)
    oh2[0, 0:128] = 1.0; oh2[1, 128:256] = 1.0
    su = np.triu(np.ones((64, 64), np.float32), 1)
    sl = np.tril(np.ones((64, 64), np.float32), -1)
    msk = np.concatenate([np.tile(1000.0 * su, (1, 8)), np.tile(1000.0 * sl, (1, 8)),
                          np.tile(-sl, (1, 8)), np.tile(-su, (1, 8))], axis=1)

    in_maps = []
    for c in range(8):
        b, h = c // 2, c % 2
        gg = h
        xs = np.ascontiguousarray(xt[b, :, h * TOK:(h + 1) * TOK]).reshape(KCH, 128, TOK)
        qb = wq[:, gg * 1024:(gg + 1) * 1024]
        kb = wq[:, 2048 + gg * 1024: 2048 + (gg + 1) * 1024]
        vb = wq[:, 4096 + gg * 2048: 4096 + (gg + 1) * 2048]
        zb = wq[:, 8192 + gg * 2048: 8192 + (gg + 1) * 2048]
        wqkvz = np.ascontiguousarray(np.concatenate([qb, kb, vb, zb], axis=1))
        cw_cols = np.concatenate([convw[:, gg * 1024:(gg + 1) * 1024],
                                  convw[:, 2048 + gg * 1024: 2048 + (gg + 1) * 1024],
                                  convw[:, 4096 + gg * 2048: 4096 + (gg + 1) * 2048]], axis=1)
        cb_cols = np.concatenate([convb[gg * 1024:(gg + 1) * 1024],
                                  convb[2048 + gg * 1024: 2048 + (gg + 1) * 1024],
                                  convb[4096 + gg * 2048: 4096 + (gg + 1) * 2048]])
        convw_pt = np.ascontiguousarray(cw_cols.T.reshape(32, 128, 4).transpose(1, 0, 2).reshape(128, 128))
        convb_pt = np.ascontiguousarray(cb_cols.reshape(32, 128).T)
        wba_c = np.zeros((2048, 64), np.float32)
        wba_c[:, 0:16] = wba[:, gg * 16:(gg + 1) * 16]
        wba_c[:, 32:48] = wba[:, 32 + gg * 16: 32 + (gg + 1) * 16]
        wba_c = np.ascontiguousarray(wba_c)
        av = np.zeros((64, 1), np.float32); av[32:48, 0] = alog[gg * 16:(gg + 1) * 16]
        dv = np.zeros((64, 1), np.float32); dv[32:48, 0] = dtb[gg * 16:(gg + 1) * 16]
        nwc = np.ascontiguousarray(nw.reshape(128, 1))
        wom = np.ascontiguousarray(wo[gg * 2048:(gg + 1) * 2048])
        wop = np.ascontiguousarray(wo[(1 - gg) * 2048:(2 - gg) * 2048])
        offs = np.zeros((128, 80), np.int32)
        par = np.arange(128, dtype=np.int32)
        for half in range(2):
            for k in range(16):
                offs[:, half * 16 + k] = ((2 * b + half) * 128 + par) * (KCH * TOK) + k * TOK
        for vch in range(16):
            offs[:, 32 + vch] = ((vch * 128 + par) * 2 + (1 - h)) * TOK
            offs[:, 48 + vch] = ((vch * 128 + par) * 2 + h) * TOK
            offs[:, 64 + vch] = ((c ^ 1) * 2048 + vch * 128 + par) * TOK
        m = {"x_in": xs, "shg": shg_t, "shu": shu_t, "shd": shd_t,
             "gw": gw_t, "seg": seg_t, "ident_in": ident, "oh8_in": oh8,
             "oh32_in": oh32, "oh2_in": oh2, "wqkvz": wqkvz, "wba": wba_c,
             "convw_in": convw_pt, "convb_in": convb_pt, "alog_in": av, "dt_in": dv,
             "nw_in": nwc, "w_out_mine": wom, "w_out_partner": wop,
             "msk_in": msk, "offs_in": offs}
        for e in range(NEXP):
            m[f"wg{e}"] = wg_t[e]; m[f"wu{e}"] = wu_t[e]; m[f"wd{e}"] = wd_t[e]
        in_maps.append(m)
    return in_maps


STAGES = ("ffn1", "attn", "ffn2")
DEBUG = False
NSG = 8
ACUT = 5
SCUT = 5

def kernel(**inputs):
    in_maps = host_prep(inputs)
    nc = build_program(stages=STAGES)
    res = run_bass_kernel_spmd(nc, in_maps, core_ids=list(range(8)))
    out = np.zeros((4, 2048, 2048), np.float32)
    for c in range(8):
        b, h = c // 2, c % 2
        yk = np.asarray(res.results[c]["y"])
        out[b, h * TOK:(h + 1) * TOK, :] = yk.reshape(E, TOK).T
    return out


# version 05
# speedup vs baseline: 1.0743x; 1.0743x over previous
import contextlib
import numpy as np
import concourse.bass as bass
import concourse.mybir as mybir
from concourse.bass_core_rust import IndirectOffsetOnAxis
from concourse.bass_utils import run_bass_kernel_spmd

F32 = mybir.dt.float32
BF16 = mybir.dt.bfloat16
I32 = mybir.dt.int32
AF = mybir.ActivationFunctionType
OP = mybir.AluOpType
AXX = mybir.AxisListType.X

ENGINES = ("tensor", "vector", "scalar", "gpsimd", "sync")
ENG_SEM = {"tensor": "sT", "vector": "sV", "scalar": "sA", "gpsimd": "sG", "sync": "sQ"}
Q_SEM = {"gpsimd": "qG", "sync": "qS", "scalar": "qA", "vector": "qV"}

E = 2048
KCH = 16
TOK = 1024
TT = 512
NTT = TOK // TT
SEQ = 2048
NVH = 16
NKH = 8
C = 64
NCH = SEQ // C
DK = 128
DV = 128
NEXP = 8
HEXP = 1024
HSH = 2048
EPS_LN = 1e-5
EPS_RMS = 1e-6
EPS_L2 = 1e-6


class PB:

    def __init__(self, nc, es):
        self.nc = nc
        self.es = es
        self.stream = {e: [] for e in ENGINES}
        names = sorted(set(ENG_SEM.values()) | set(Q_SEM.values()))
        self.sem = {n: es.enter_context(nc.semaphore(n)) for n in names}
        self.cnt = {n: 0 for n in names}
        self.floor = {e: {n: 0 for n in names} for e in ENGINES}
        self.last_w = {}
        self.last_r = {}
        self.pending = {e: [] for e in ENGINES}
        self.n_ops = 0

    def _wait(self, eng, ev):
        if ev is None:
            return
        if isinstance(ev, tuple) and ev[0] == "PEND":
            if ev[1] == eng:
                return
            raise RuntimeError(f"cross-engine dependency on un-inc'd op from {ev[1]} -> {eng}")
        s, v = ev
        if self.floor[eng][s] >= v:
            return
        self.floor[eng][s] = v
        sem = self.sem[s]
        self.stream[eng].append(lambda h, sem=sem, v=v: h.wait_ge(sem, v))

    def _resolve_pending(self, eng, ev):
        for rec in self.pending[eng]:
            kind, key = rec
            if kind == "w":
                if isinstance(self.last_w.get(key), tuple) and self.last_w[key][0] == "PEND":
                    self.last_w[key] = ev
            else:
                lst = self.last_r.get(key, [])
                self.last_r[key] = [ev if (isinstance(x, tuple) and x[0] == "PEND") else x for x in lst]
        self.pending[eng] = []

    def op(self, eng, fname, *args, rd=(), wr=(), inc=True, sem=None, **kw):
        import traceback as _tb
        _site = "".join(_tb.format_stack(limit=4)[:-1])
        for k in rd:
            self._wait(eng, self.last_w.get(k))
        for k in wr:
            self._wait(eng, self.last_w.get(k))
            for ev in self.last_r.get(k, ()):
                self._wait(eng, ev)
        sname = sem or ENG_SEM[eng]
        if inc:
            self.cnt[sname] += 16 if sname.startswith("q") else 1
            ev = (sname, self.cnt[sname])
            semh = self.sem[sname]
            amt = 16 if sname.startswith("q") else 1
            def fn(h, fname=fname, args=args, kw=kw, semh=semh, amt=amt, _site=_site):
                try:
                    getattr(h, fname)(*args, **kw).then_inc(semh, amt)
                except Exception as e:
                    raise RuntimeError(f"emit fail {fname}: {e}\nEMIT SITE:\n{_site}")
        else:
            ev = ("PEND", eng)
            def fn(h, fname=fname, args=args, kw=kw, _site=_site):
                try:
                    getattr(h, fname)(*args, **kw)
                except Exception as e:
                    raise RuntimeError(f"emit fail {fname}: {e}\nEMIT SITE:\n{_site}")
        self.stream[eng].append(fn)
        self.n_ops += 1
        for k in wr:
            self.last_w[k] = ev
            self.last_r[k] = []
        for k in rd:
            self.last_r.setdefault(k, []).append(ev)
        if not inc:
            for k in wr:
                self.pending[eng].append(("w", k))
            for k in rd:
                self.pending[eng].append(("r", k))
        else:
            self._resolve_pending(eng, ev)
        return ev

    def barrier(self):
        snap = dict(self.cnt)
        for e in ENGINES:
            for s, v in snap.items():
                if v > self.floor[e][s]:
                    self.floor[e][s] = v
                    sem = self.sem[s]
                    self.stream[e].append(lambda h, sem=sem, v=v: h.wait_ge(sem, v))

    def transfer_key(self, old, new):
        if old in self.last_w:
            prev = self.last_w.get(new)
            self.last_w[new] = self.last_w[old]
        self.last_r.setdefault(new, []).extend(self.last_r.get(old, ()))

    def dma_sem(self, slot):
        name = f"q_{slot}"
        if name not in self.sem:
            self.sem[name] = self.es.enter_context(self.nc.semaphore(name))
            self.cnt[name] = 0
            for e in ENGINES:
                self.floor[e][name] = 0
        return name

    def dma(self, eng, out, in_, rd=(), wr=(), slot=None, **kw):
        slot = slot or (wr[0] if wr else "misc")
        return self.op(eng, "dma_start", rd=rd, wr=wr, sem=self.dma_sem(slot),
                       out=out, in_=in_, **kw)

    def finalize(self, block):
        for eng in ENGINES:
            dec = getattr(block, eng)
            stream = self.stream[eng]
            def make(stream):
                def body(h):
                    for fn in stream:
                        fn(h)
                return body
            dec(make(stream))


def sbuf(nc, es, name, shape, dtype):
    return es.enter_context(nc.sbuf_tensor(name, shape, dtype))


def psum(nc, es, name, shape, dtype=F32):
    return es.enter_context(nc.psum_tensor(name, shape, dtype))


def emit_ffn(p, nc, es, g, tag):
    x_res = g["x_res"]
    with contextlib.ExitStack() as ffn_es:
        S = lambda name, shape, dt: sbuf(nc, ffn_es, f"{tag}_{name}", shape, dt)
        P = lambda name, shape, dt=F32: psum(nc, ffn_es, f"{tag}_{name}", shape, dt)
        psA = P("psA", [128, 512]); psB = P("psB", [128, 512])
        psC = P("psC", [128, 512]); psD = P("psD", [128, 512])
        psE = P("psE", [128, 512]); psF = P("psF", [128, 512])
        psG_ = P("psG", [128, 512]); psT8 = P("psT8", [8, 512], BF16)
        xln = S("xln", [128, KCH * TT], BF16)
        xbf = S("xbf", [128, 2 * TT], BF16)
        xsq = S("xsq", [128, 2 * TT], BF16)
        strow = S("strow", [1, 2 * TT], F32)
        a_sb = S("a_sb", [128, TT], F32)
        b_sb = S("b_sb", [128, TT], F32)
        sc_sb = S("sc_sb", [128, 4 * 8], F32)
        r1 = S("r1", [128, 4 * 8], F32)
        r2 = S("r2", [128, 4 * 8], F32)
        rsc = S("rsc", [128, 4 * 6], F32)
        combb = S("combb", [128, 4 * 8], BF16)
        compR = S("compR", [8, TT], BF16)
        sgrow = S("sgrow", [1, TT], F32)
        cbc = S("cbc", [128, TT], BF16)
        sgbc = S("sgbc", [128, TT], BF16)
        hp = S("hp", [128, 64 * TT], BF16)
        hsh = S("hsh", [128, 16 * TT], BF16)
        s1g = S("s1g", [128, 2 * TT], BF16)
        htmp = S("htmp", [128, 2 * TT], F32)
        wring = S("wring", [128, 10240], BF16)
        wgu = wring
        ident_tr = g["ident_bf"]

        wg_d, wu_d, wd_d, shg_d, shu_d, shd_d, gw_d, seg_d = (
            g["wg"], g["wu"], g["wd"], g["shg"], g["shu"], g["shd"], g["gw"], g["seg"])

        for tt in range(NTT):
            t0 = tt * TT
            xk = lambda k: x_res[:, k * TOK + t0: k * TOK + t0 + TT]
            kx = f"{tag}xres"

            for k in range(KCH):
                r = k % 2
                xb_ap = xbf[:, r * TT:(r + 1) * TT]
                xs_ap = xsq[:, r * TT:(r + 1) * TT]
                p.op("vector", "tensor_copy", xb_ap, xk(k), rd=[kx], wr=[f"xbf{r}"])
                p.op("scalar", "activation", xs_ap, xk(k), AF.Square, rd=[kx], wr=[f"xsq{r}"])
                p.op("tensor", "matmul", psA[0:1, :], g["ones_bf"][0:128, 0:1], xb_ap,
                     start=(k == 0), stop=(k == KCH - 1),
                     rd=[f"xbf{r}", "ones_bf"], wr=["psA"])
                p.op("tensor", "matmul", psB[0:1, :], g["ones_bf"][0:128, 0:1], xs_ap,
                     start=(k == 0), stop=(k == KCH - 1),
                     rd=[f"xsq{r}", "ones_bf"], wr=["psB"])
            mean = strow[0:1, 0:TT]; e2 = strow[0:1, TT:2*TT]; var = e2
            rstd = e2; brow = mean
            p.op("vector", "tensor_scalar", mean, psA[0:1, :], 1.0 / E, None, OP.mult,
                 rd=["psA"], wr=["st_m"])
            p.op("vector", "tensor_scalar", e2, psB[0:1, :], 1.0 / E, None, OP.mult,
                 rd=["psB"], wr=["st_e2"])
            tmpv = sgrow[0:1, 0:TT]
            p.op("vector", "tensor_mul", tmpv, mean, mean, rd=["st_m"], wr=["sgA"])
            p.op("vector", "tensor_sub", var, e2, tmpv, rd=["st_e2", "sgA"], wr=["st_e2"])
            p.op("vector", "tensor_scalar", var, var, EPS_LN, None, OP.add, rd=["st_e2"], wr=["st_e2"])
            p.op("vector", "reciprocal", var, var, rd=["st_e2"], wr=["st_e2"])
            p.op("scalar", "activation", rstd, var, AF.Sqrt, rd=["st_e2"], wr=["st_e2"])
            p.op("vector", "scalar_tensor_tensor", brow, mean, -1.0, rstd, OP.mult, OP.mult,
                 rd=["st_m", "st_e2"], wr=["st_m"])
            p.op("tensor", "matmul", psC[:, :], g["ones_f1"][0:1, 0:128], rstd,
                 start=True, stop=True, rd=["st_e2", "ones_f1"], wr=["psC"])
            p.op("tensor", "matmul", psD[:, :], g["ones_f1"][0:1, 0:128], brow,
                 start=True, stop=True, rd=["st_m", "ones_f1"], wr=["psD"])
            p.op("vector", "tensor_copy", a_sb[:, :], psC[:, :], rd=["psC"], wr=["a_sb"])
            p.op("vector", "tensor_copy", b_sb[:, :], psD[:, :], rd=["psD"], wr=["b_sb"])
            for k in range(KCH):
                dst = xln[:, k * TT:(k + 1) * TT]
                r = k % 2
                tmp = htmp[:, r * TT:(r + 1) * TT]
                p.op("vector", "tensor_mul", tmp, xk(k), a_sb[:, :], rd=[kx, "a_sb"], wr=[f"htmp{r}"])
                p.op("vector", "tensor_add", dst, tmp, b_sb[:, :], rd=[f"htmp{r}", "b_sb"], wr=["xln"])

            for st in range(4):
                for k in range(KCH):
                    p.op("tensor", "matmul", psE[0:128, st * 8:(st + 1) * 8],
                         xln[:, k * TT + st * 128: k * TT + st * 128 + 128],
                         g["gw_sb"][:, k * 8:(k + 1) * 8],
                         start=(k == 0), stop=(k == KCH - 1),
                         rd=["xln", "gw_sb"], wr=["psE"], inc=(k == KCH - 1))
            p.op("vector", "tensor_copy", sc_sb[:, :], psE[:, 0:32], rd=["psE"], wr=["sc"])
            for st in range(4):
                sc = sc_sb[:, st * 8:(st + 1) * 8]
                eq1 = r1[:, st * 8:(st + 1) * 8]
                s2 = r2[:, st * 8:(st + 1) * 8]
                m1 = rsc[:, st * 6 + 0: st * 6 + 1]; m2 = rsc[:, st * 6 + 1: st * 6 + 2]
                ex = rsc[:, st * 6 + 2: st * 6 + 3]; w1 = rsc[:, st * 6 + 3: st * 6 + 4]
                w2 = rsc[:, st * 6 + 4: st * 6 + 5]; dd = rsc[:, st * 6 + 5: st * 6 + 6]
                p.op("vector", "reduce_max", m1, sc, AXX, rd=["sc"], wr=["m1"])
                p.op("vector", "tensor_scalar", eq1, sc, m1, None, OP.is_equal, rd=["sc", "m1"], wr=["eq1"])
                p.op("vector", "scalar_tensor_tensor", s2, eq1, -1e9, sc, OP.mult, OP.add,
                     rd=["eq1", "sc"], wr=["s2"])
                p.op("vector", "reduce_max", m2, s2, AXX, rd=["s2"], wr=["m2"])
                p.op("vector", "tensor_sub", dd, m2, m1, rd=["m2", "m1"], wr=["dd"])
                p.op("scalar", "activation", ex, dd, AF.Exp, rd=["dd"], wr=["ex"])
                p.op("vector", "tensor_scalar", w1, ex, 1.0, None, OP.add, rd=["ex"], wr=["w1"])
                p.op("vector", "reciprocal", w1, w1, rd=["w1"], wr=["w1"])
                p.op("vector", "tensor_mul", w2, ex, w1, rd=["ex", "w1"], wr=["w2"])
                p.op("vector", "tensor_scalar", s2, s2, m2, None, OP.is_equal, rd=["s2", "m2"], wr=["s2"])
                p.op("vector", "tensor_scalar", eq1, eq1, w1, None, OP.mult, rd=["eq1", "w1"], wr=["eq1"])
                p.op("vector", "scalar_tensor_tensor", eq1, s2, w2, eq1, OP.mult, OP.add,
                     rd=["s2", "w2", "eq1"], wr=["eq1"])
                p.op("vector", "tensor_copy", combb[:, st * 8:(st + 1) * 8], eq1, rd=["eq1"], wr=["combb"])
                p.op("tensor", "transpose", psT8[0:8, st * 128:(st + 1) * 128],
                     combb[:, st * 8:(st + 1) * 8], ident_tr[:, :],
                     rd=["combb", "ident_bf"], wr=["psT8"])
            p.op("vector", "tensor_copy", compR[:, :], psT8[0:8, 0:TT], rd=["psT8"], wr=["compR"])

            for k in range(KCH):
                p.op("tensor", "matmul", psF[0:1, 0:TT], g["seg_sb"][:, k:k+1],
                     xln[:, k * TT:(k + 1) * TT],
                     start=(k == 0), stop=(k == KCH - 1),
                     rd=["xln", "seg_sb"], wr=["psF"], inc=(k == KCH - 1))
            sgA = sgrow[0:1, 0:TT]
            p.op("scalar", "activation", sgA, psF[0:1, 0:TT], AF.Exp, scale=g["cneg1"][0:1, 0:1],
                 rd=["psF"], wr=["sgA"])
            p.op("vector", "tensor_scalar", sgA, sgA, 1.0, None, OP.add, rd=["sgA"], wr=["sgA"])
            p.op("vector", "reciprocal", sgA, sgA, rd=["sgA"], wr=["sgA"])
            p.op("tensor", "matmul", psF[:, :], g["ones_f1"][0:1, 0:128], sgA,
                 start=True, stop=True, rd=["sgA", "ones_f1"], wr=["psF"])
            p.op("vector", "tensor_copy", sgbc[:, :], psF[:, :], rd=["psF"], wr=["sgbc"])

            def wgu_load(e, h, r):
                base = r * 2 * KCH * 128
                dst_g = wgu[:, base: base + KCH * 128]
                dst_u = wgu[:, base + KCH * 128: base + 2 * KCH * 128]
                in_g = wg_d[e].rearrange("(k p) h -> p k h", p=128)[:, :, h * 128:(h + 1) * 128]
                in_u = wu_d[e].rearrange("(k p) h -> p k h", p=128)[:, :, h * 128:(h + 1) * 128]
                p.dma("gpsimd", dst_g.rearrange("p (k h) -> p k h", k=KCH), in_g, wr=[f"wgu{r}"])
                p.dma("gpsimd", dst_u.rearrange("p (k h) -> p k h", k=KCH), in_u, wr=[f"wgu{r}"])

            pairs = [(e, h) for e in range(NEXP) for h in range(8)]
            wgu_load(*pairs[0], 0)
            wgu_load(*pairs[1], 1)
            for i, (e, h) in enumerate(pairs):
                r = i % 2
                base = r * 2 * KCH * 128
                pa, pb = (psA, psB) if r == 0 else (psE, psF)
                ka, kb2 = ("psA", "psB") if r == 0 else ("psE", "psF")
                if h == 0:
                    p.op("tensor", "matmul", psG_[:, :], g["oh8_sb"][0:8, e*128:(e+1)*128],
                         compR[0:8, :], start=True, stop=True, rd=["compR", "oh8_sb"], wr=["psG"])
                    p.op("vector", "tensor_copy", cbc[:, :], psG_[:, :], rd=["psG"], wr=["cbc"])
                for k in range(KCH):
                    p.op("tensor", "matmul", pa[:, :], wgu[:, base + k * 128: base + (k+1) * 128],
                         xln[:, k * TT:(k + 1) * TT], start=(k == 0), stop=(k == KCH - 1),
                         rd=[f"wgu{r}", "xln"], wr=[ka], inc=(k == KCH - 1))
                for k in range(KCH):
                    p.op("tensor", "matmul", pb[:, :], wgu[:, base + KCH*128 + k * 128: base + KCH*128 + (k+1) * 128],
                         xln[:, k * TT:(k + 1) * TT], start=(k == 0), stop=(k == KCH - 1),
                         rd=[f"wgu{r}", "xln"], wr=[kb2], inc=(k == KCH - 1))
                s1 = s1g[:, r * TT:(r + 1) * TT]
                tmp = htmp[:, r * TT:(r + 1) * TT]
                p.op("scalar", "activation", s1, pa[:, :], AF.Silu, rd=[ka], wr=[f"s1{r}"])
                p.op("vector", "tensor_mul", tmp, pb[:, :], s1, rd=[kb2, f"s1{r}"], wr=[f"htmp{r}"])
                hslot = hp[:, (e * 8 + h) * TT: (e * 8 + h + 1) * TT]
                p.op("vector", "tensor_mul", hslot, tmp, cbc[:, :], rd=[f"htmp{r}", "cbc"], wr=["hp"])
                if i + 2 < len(pairs):
                    wgu_load(*pairs[i + 2], r)

            def wsh_load(hs, r):
                base = r * 2 * KCH * 128
                dst_g = wgu[:, base: base + KCH * 128]
                dst_u = wgu[:, base + KCH * 128: base + 2 * KCH * 128]
                in_g = shg_d.rearrange("(k p) h -> p k h", p=128)[:, :, hs * 128:(hs + 1) * 128]
                in_u = shu_d.rearrange("(k p) h -> p k h", p=128)[:, :, hs * 128:(hs + 1) * 128]
                p.dma("gpsimd", dst_g.rearrange("p (k h) -> p k h", k=KCH), in_g, wr=[f"wgu{r}"])
                p.dma("gpsimd", dst_u.rearrange("p (k h) -> p k h", k=KCH), in_u, wr=[f"wgu{r}"])
            wsh_load(0, 0); wsh_load(1, 1)
            for hs in range(16):
                r = hs % 2
                base = r * 2 * KCH * 128
                pa, pb = (psA, psB) if r == 0 else (psE, psF)
                ka, kb2 = ("psA", "psB") if r == 0 else ("psE", "psF")
                for k in range(KCH):
                    p.op("tensor", "matmul", pa[:, :], wgu[:, base + k * 128: base + (k+1) * 128],
                         xln[:, k * TT:(k + 1) * TT], start=(k == 0), stop=(k == KCH - 1),
                         rd=[f"wgu{r}", "xln"], wr=[ka], inc=(k == KCH - 1))
                for k in range(KCH):
                    p.op("tensor", "matmul", pb[:, :], wgu[:, base + KCH*128 + k * 128: base + KCH*128 + (k+1) * 128],
                         xln[:, k * TT:(k + 1) * TT], start=(k == 0), stop=(k == KCH - 1),
                         rd=[f"wgu{r}", "xln"], wr=[kb2], inc=(k == KCH - 1))
                s1 = s1g[:, r * TT:(r + 1) * TT]
                tmp = htmp[:, r * TT:(r + 1) * TT]
                p.op("scalar", "activation", s1, pa[:, :], AF.Silu, rd=[ka], wr=[f"s1{r}"])
                p.op("vector", "tensor_mul", tmp, pb[:, :], s1, rd=[kb2, f"s1{r}"], wr=[f"htmp{r}"])
                p.op("vector", "tensor_mul", hsh[:, hs * TT:(hs + 1) * TT], tmp, sgbc[:, :],
                     rd=[f"htmp{r}", "sgbc"], wr=["hsh"])
                if hs + 2 < 16:
                    wsh_load(hs + 2, r)

            for j in range(6):
                p.transfer_key("wgu0", f"wde{j}")
                p.transfer_key("wgu1", f"wde{j}")
            for r in range(2):
                p.transfer_key("wgu0", f"wshd{r}")
                p.transfer_key("wgu1", f"wshd{r}")

            def wde_load(m, e, j):
                dst = wring[:, j * 1024:(j + 1) * 1024]
                in_t = wd_d[e].rearrange("(hk p) m -> p hk m", p=128)[:, :, m * 128:(m + 1) * 128]
                p.dma("gpsimd", dst.rearrange("p (hk m) -> p hk m", hk=8), in_t, wr=[f"wde{j}"])

            def wshd_load(m, r):
                dst = wring[:, 6144 + r * 2048: 6144 + (r + 1) * 2048]
                inS = shd_d.rearrange("(hs p) m -> p hs m", p=128)[:, :, m * 128:(m + 1) * 128]
                p.dma("gpsimd", dst.rearrange("p (hs m) -> p hs m", hs=16), inS, wr=[f"wshd{r}"])

            me_list = [(m, e) for m in range(KCH) for e in range(NEXP)]
            for jj in range(6):
                wde_load(*me_list[jj], jj)
            wshd_load(0, 0); wshd_load(1, 1)
            for m in range(KCH):
                r = m % 2
                pd = psC if (m % 2 == 0) else psD
                pk = "psC" if (m % 2 == 0) else "psD"
                n_mm = NEXP * 8 + 16
                i_mm = 0
                for e in range(NEXP):
                    j = (m * NEXP + e) % 6
                    for hk in range(8):
                        p.op("tensor", "matmul", pd[:, :],
                             wring[:, j * 1024 + hk * 128: j * 1024 + (hk + 1) * 128],
                             hp[:, (e * 8 + hk) * TT: (e * 8 + hk + 1) * TT],
                             start=(i_mm == 0), stop=(i_mm == n_mm - 1),
                             rd=[f"wde{j}", "hp"], wr=[pk], inc=(hk == 7))
                        i_mm += 1
                    nxt = m * NEXP + e + 6
                    if nxt < len(me_list):
                        wde_load(*me_list[nxt], nxt % 6)
                for hs in range(16):
                    p.op("tensor", "matmul", pd[:, :],
                         wring[:, 6144 + r * 2048 + hs * 128: 6144 + r * 2048 + (hs + 1) * 128],
                         hsh[:, hs * TT: (hs + 1) * TT],
                         start=(i_mm == 0), stop=(i_mm == n_mm - 1),
                         rd=[f"wshd{r}", "hsh"], wr=[pk], inc=(i_mm == n_mm - 1))
                    i_mm += 1
                if m + 2 < KCH:
                    wshd_load(m + 2, r)
                dst = x_res[:, m * TOK + t0: m * TOK + t0 + TT]
                p.op("vector", "scalar_tensor_tensor", dst, pd[:, :], 0.5, dst, OP.mult, OP.add,
                     rd=[pk, kx], wr=[kx])
            for j in range(6):
                p.transfer_key(f"wde{j}", "wgu0"); p.transfer_key(f"wde{j}", "wgu1")
            for r in range(2):
                p.transfer_key(f"wshd{r}", "wgu0"); p.transfer_key(f"wshd{r}", "wgu1")



def emit_ln_to(p, nc, es, g, x_res, out_cb, tag):
    with contextlib.ExitStack() as les:
        S = lambda name, shape, dt: sbuf(nc, les, f"{tag}_{name}", shape, dt)
        P = lambda name, shape, dt=F32: psum(nc, les, f"{tag}_{name}", shape, dt)
        psA = P("psA", [128, 512]); psB = P("psB", [128, 512])
        psC = P("psC", [128, 512]); psD = P("psD", [128, 512])
        xbf = S("xbf", [128, 2 * TT], BF16)
        xsq = S("xsq", [128, 2 * TT], BF16)
        ftmp = S("ftmp", [128, 2 * TT], F32)
        strow = S("strow", [1, 2 * TT], F32)
        srow2 = S("srow2", [1, TT], F32)
        a_sb = S("a_sb", [128, TT], F32)
        b_sb = S("b_sb", [128, TT], F32)
        xln = S("xln", [128, KCH * TT], BF16)
        kx = f"{tag}xres"
        for tt in range(NTT):
            t0 = tt * TT
            xk = lambda k: x_res[:, k * TOK + t0: k * TOK + t0 + TT]
            for k in range(KCH):
                r = k % 2
                xb_ap = xbf[:, r * TT:(r + 1) * TT]
                xs_ap = xsq[:, r * TT:(r + 1) * TT]
                p.op("vector", "tensor_copy", xb_ap, xk(k), rd=[kx], wr=[f"{tag}xbf{r}"])
                p.op("scalar", "activation", xs_ap, xk(k), AF.Square, rd=[kx], wr=[f"{tag}xsq{r}"])
                p.op("tensor", "matmul", psA[0:1, :], g["ones_bf"][0:128, 0:1], xb_ap,
                     start=(k == 0), stop=(k == KCH - 1), rd=[f"{tag}xbf{r}", "ones_bf"], wr=[f"{tag}psA"])
                p.op("tensor", "matmul", psB[0:1, :], g["ones_bf"][0:128, 0:1], xs_ap,
                     start=(k == 0), stop=(k == KCH - 1), rd=[f"{tag}xsq{r}", "ones_bf"], wr=[f"{tag}psB"])
            mean = strow[0:1, 0:TT]; e2 = strow[0:1, TT:2*TT]
            p.op("vector", "tensor_scalar", mean, psA[0:1, :], 1.0 / E, None, OP.mult,
                 rd=[f"{tag}psA"], wr=[f"{tag}m"])
            p.op("vector", "tensor_scalar", e2, psB[0:1, :], 1.0 / E, None, OP.mult,
                 rd=[f"{tag}psB"], wr=[f"{tag}e"])
            p.op("vector", "tensor_mul", srow2[0:1, :], mean, mean, rd=[f"{tag}m"], wr=[f"{tag}s2"])
            p.op("vector", "tensor_sub", e2, e2, srow2[0:1, :], rd=[f"{tag}e", f"{tag}s2"], wr=[f"{tag}e"])
            p.op("vector", "tensor_scalar", e2, e2, EPS_LN, None, OP.add, rd=[f"{tag}e"], wr=[f"{tag}e"])
            p.op("vector", "reciprocal", e2, e2, rd=[f"{tag}e"], wr=[f"{tag}e"])
            p.op("scalar", "activation", e2, e2, AF.Sqrt, rd=[f"{tag}e"], wr=[f"{tag}e"])
            p.op("vector", "scalar_tensor_tensor", mean, mean, -1.0, e2, OP.mult, OP.mult,
                 rd=[f"{tag}m", f"{tag}e"], wr=[f"{tag}m"])
            p.op("tensor", "matmul", psC[:, :], g["ones_f1"][0:1, 0:128], e2,
                 start=True, stop=True, rd=[f"{tag}e", "ones_f1"], wr=[f"{tag}psC"])
            p.op("tensor", "matmul", psD[:, :], g["ones_f1"][0:1, 0:128], mean,
                 start=True, stop=True, rd=[f"{tag}m", "ones_f1"], wr=[f"{tag}psD"])
            p.op("vector", "tensor_copy", a_sb[:, :], psC[:, :], rd=[f"{tag}psC"], wr=[f"{tag}a"])
            p.op("vector", "tensor_copy", b_sb[:, :], psD[:, :], rd=[f"{tag}psD"], wr=[f"{tag}b"])
            for k in range(KCH):
                dst = xln[:, k * TT:(k + 1) * TT]
                r = k % 2
                tmp = ftmp[:, r * TT:(r + 1) * TT]
                p.op("vector", "tensor_mul", tmp, xk(k), a_sb[:, :], rd=[kx, f"{tag}a"], wr=[f"{tag}ft{r}"])
                p.op("vector", "tensor_add", dst, tmp, b_sb[:, :], rd=[f"{tag}ft{r}", f"{tag}b"], wr=[f"{tag}xln"])
            out_cb(tt, xln, f"{tag}xln")


def emit_attention(p, nc, es, g, d):
    SQD = 1.0 / (DK ** 0.5)
    GRP = 8
    NGRP = NCH // GRP
    ag1_in, ag1_out = d["ag1_in"], d["ag1_out"]
    ag2_in, ag2_out = d["ag2_in"], d["ag2_out"]
    gated_dram = d["gated_dram"]

    p.op("gpsimd", "collective_compute", "AllGather", OP.bypass,
         rd=["ag1_in"], wr=["ag1_out"],
         replica_groups=[list(range(8))], ins=[ag1_in.ap()], outs=[ag1_out.ap()])

    S = lambda name, shape, dt: sbuf(nc, es, f"at_{name}", shape, dt)
    P = lambda name, shape, dt=F32: psum(nc, es, f"at_{name}", shape, dt)

    xln_seq = S("xlnseq", [128, KCH * SEQ], BF16)
    offs = g["offs_sb"]
    for half in range(2):
        for k in range(KCH):
            p.op("gpsimd", "indirect_dma_start",
                 rd=["ag1_out", "offs_sb"], wr=["xlnseq"], sem=p.dma_sem("xlnseq"),
                 out=xln_seq[:, k * SEQ + half * TOK: k * SEQ + (half + 1) * TOK],
                 out_offset=None,
                 in_=ag1_out.ap().rearrange("(p a) f -> p (a f)", p=128),
                 in_offset=IndirectOffsetOnAxis(offs[:, half * KCH + k: half * KCH + k + 1], 1))

    psU = P("psU", [128, 512]); psW1 = P("psW1", [128, 512])
    psQ = P("psQ", [128, 512]); psO = [P("psO0", [128, 512]), P("psO1", [128, 512])]
    psX = P("psX", [128, 512])
    psS = psX[:, 0:128]
    psT32v = psX[:, 256:320]
    psVt = P("psVt", [64, 128])
    psV = psVt[:, :]
    psTb = P("psTb", [128, 128], BF16)
    psR = psU[0:64, :]

    if ACUT <= 1:
        return None
    gam = S("gam", [64, SEQ], F32)

    egr = S("egr", [64, SEQ], BF16)
    kwr = S("kwr", [64, SEQ], BF16)
    gam_tm = S("gamtm", [128, 16 * KCH], F32)
    kw_tm_f = S("kwtmf", [128, 16 * KCH], F32)
    eglb = S("eglb", [128, NVH * NCH], F32)
    glrow = S("glrow", [64, NCH], F32)
    p.op("vector", "memset", gam[:, :], 0.0, wr=["gam"])
    p.op("vector", "memset", egr[:, :], 0.0, wr=["egr"])
    p.op("vector", "memset", kwr[:, :], 0.0, wr=["kwr"])
    p.op("vector", "memset", glrow[:, :], 0.0, wr=["glrow"])
    row_es = contextlib.ExitStack()
    bav = sbuf(nc, row_es, "at_bav", [64, SEQ], F32)
    wba_sb = sbuf(nc, row_es, "at_wba", [128, KCH * 64], BF16)
    p.dma("gpsimd", wba_sb.ap().rearrange("p (k c) -> p k c", k=KCH),
          d["wba"].rearrange("(k p) c -> p k c", p=128), wr=["wba_sb"], slot="wba")
    for tt in range(4):
        for k in range(KCH):
            p.op("tensor", "matmul", psR[:, :], wba_sb[:, k * 64:(k + 1) * 64],
                 xln_seq[:, k * SEQ + tt * 512: k * SEQ + (tt + 1) * 512],
                 start=(k == 0), stop=(k == KCH - 1),
                 rd=["wba_sb", "xlnseq"], wr=["at_psU"])
        p.op("vector", "tensor_copy", bav[:, tt * 512:(tt + 1) * 512], psR[:, :],
             rd=["at_psU"], wr=["bav"])
    brows = bav[0:16, :]; arows = bav[32:48, :]
    grows = bav[32:48, :]
    p.op("scalar", "activation", brows, brows, AF.Exp, scale=g["cneg1"][0:16, 0:1],
         rd=["bav"], wr=["bav"])
    p.op("vector", "tensor_scalar", brows, brows, 1.0, None, OP.add, rd=["bav"], wr=["bav"])
    p.op("vector", "reciprocal", brows, brows, rd=["bav"], wr=["bav"])
    spa = gam[32:48, :]
    p.op("vector", "tensor_scalar", arows, arows, g["dt_col"][32:48, 0:1], None, OP.add,
         rd=["bav", "dt_col"], wr=["bav"])
    p.op("scalar", "activation", spa, arows, AF.Abs, rd=["bav"], wr=["gam"])
    p.op("scalar", "activation", spa, spa, AF.Exp, scale=g["cneg1"][32:48, 0:1],
         rd=["gam"], wr=["gam"])
    p.op("scalar", "activation", spa, spa, AF.Ln, bias=g["cone"][32:48, 0:1],
         rd=["gam"], wr=["gam"])
    p.op("vector", "tensor_scalar_max", arows, arows, 0.0, rd=["bav"], wr=["bav"])
    p.op("vector", "tensor_add", arows, arows, spa, rd=["bav", "gam"], wr=["bav"])
    nea = sbuf(nc, row_es, "at_nea", [64, 1], F32)
    p.op("scalar", "activation", nea[32:48, :], g["alog_col"][32:48, 0:1], AF.Exp,
         rd=["alog_col"], wr=["nea"])
    p.op("vector", "tensor_scalar", nea[32:48, :], nea[32:48, :], -1.0, None, OP.mult,
         rd=["nea"], wr=["nea"])
    p.op("vector", "tensor_scalar", arows, arows, nea[32:48, 0:1], None, OP.mult,
         rd=["bav", "nea"], wr=["bav"])
    p.op("scalar", "activation", egr[32:48, :], grows, AF.Exp, rd=["bav"], wr=["egr"])
    p.op("vector", "tensor_copy", egr[0:16, :], brows, rd=["bav"], wr=["egr"])
    p.op("vector", "tensor_copy", glrow[32:48, :],
         bav[32:48, :].rearrange("p (c t) -> p c t", t=C)[:, :, C-1:C].rearrange("p c t -> p (c t)"),
         rd=["bav"], wr=["glrow"])
    p.op("vector", "tensor_copy", gam[32:48, :].rearrange("p (c t) -> p c t", t=C),
         bav[32:48, :].rearrange("p (c t) -> p c t", t=C)[:, :, C-1:C].broadcast_to((16, NCH, C)),
         rd=["bav"], wr=["gam"])
    p.op("vector", "tensor_sub", gam[32:48, :], gam[32:48, :], grows,
         rd=["gam", "bav"], wr=["gam"])
    p.op("scalar", "activation", kwr[32:48, :], gam[32:48, :], AF.Exp, rd=["gam"], wr=["kwr"])
    for c in range(NCH):
        p.op("vector", "tensor_tensor_scan", gam[32:48, c * C:(c + 1) * C],
             grows[:, c * C:(c + 1) * C], grows[:, c * C:(c + 1) * C], 0.0, OP.add, OP.bypass,
             rd=["bav"], wr=["gam"])
    psT32 = psT32v
    for i in range(KCH):
        p.op("tensor", "transpose", psT32[:, 0:64], gam[0:64, i * 128:(i + 1) * 128],
             g["ident_f1"][0:64, 0:64], rd=["gam", "ident_f1"], wr=["psT32"])
        p.op("vector", "tensor_copy", gam_tm[:, i * 16:(i + 1) * 16], psT32[:, 32:48],
             rd=["psT32"], wr=["gamtm"])
        p.op("tensor", "transpose", psTb[:, 0:64], kwr[0:64, i * 128:(i + 1) * 128],
             g["ident_bf"][0:64, 0:64], rd=["kwr", "ident_bf"], wr=["at_psTb"])
        p.op("vector", "tensor_copy", kw_tm_f[:, i * 16:(i + 1) * 16], psTb[:, 32:48],
             rd=["at_psTb"], wr=["kwtmf"])
    for vh in range(NVH):
        p.op("tensor", "matmul", psU[:, 0:NCH], g["ohg_f32"][32:64, vh * 128:(vh + 1) * 128],
             glrow[32:64, :], start=True, stop=True, rd=["glrow", "ohg_f32"], wr=["at_psU"])
        p.op("scalar", "activation", eglb[:, vh * NCH:(vh + 1) * NCH], psU[:, 0:NCH], AF.Exp,
             rd=["at_psU"], wr=["eglb"])
    row_es.close()
    p.barrier()
    return dict(xln_seq=xln_seq, bav=None, gam=gam, egr=egr, kwr=kwr,
                gam_tm=gam_tm, kw_tm_f=kw_tm_f, eglb=eglb,
                psU=psU, psW1=psW1, psQ=psQ, psO=psO, psS=psS, psV=psV, psTb=psTb)


def emit_subgroup(p, nc, es, g, d, ctx, sg):
    SQD = 1.0 / (DK ** 0.5)
    GRP = 8
    NGRP = NCH // GRP
    xln_seq = ctx["xln_seq"]; gam = ctx["gam"]; egr = ctx["egr"]
    gam_tm = ctx["gam_tm"]; kw_tm_f = ctx["kw_tm_f"]; eglb = ctx["eglb"]
    psU = ctx["psU"]; psW1 = ctx["psW1"]; psQ = ctx["psQ"]; psO = ctx["psO"]
    psS = ctx["psS"]; psV = ctx["psV"]; psTb = ctx["psTb"]
    vh0, vh1 = 2 * sg, 2 * sg + 1

    with contextlib.ExitStack() as ses:
        S = lambda name, shape, dt: sbuf(nc, ses, f"sg{sg}_{name}", shape, dt)
        raw = S("raw", [128, (3 + SEQ)], BF16)
        qc = S("qc", [128, SEQ], BF16)
        kc = S("kc", [128, SEQ], BF16)
        vc = S("vc", [128, 2 * SEQ], BF16)
        zbnc = S("zbnc", [128, 512], BF16)
        facc = S("facc", [128, 512], F32)
        wring = S("wring", [128, 2 * KCH * 128], BF16)
        rsr = S("rsr", [64, SEQ], F32)
        rsrb = S("rsrb", [64, SEQ], BF16)
        sqt = S("sqt", [128, 512], BF16)
        ktl = S("ktl", [128, 512], BF16)
        qtl = S("qtl", [128, 512], BF16)
        kbA = S("kbA", [128, 2 * 512], BF16)
        kbg = S("kbg", [128, 2 * 512], BF16)
        vbf = S("vbf", [128, 2 * 512], BF16)
        qtg = S("qtg", [128, 2 * 512], BF16)
        scb = S("scb", [128, 6 * 512], BF16)
        args = S("args", [64, 2 * 512], F32)
        dinc = S("dinc", [64, 2 * 512], BF16)
        dincT = S("dincT", [64, 2 * 512], BF16)
        wA = S("wA", [64, 2 * 512], BF16)
        wR = S("wR", [64, 2 * 512], BF16)
        powA = S("powA", [128, 5 * GRP * 64], BF16)
        powR = S("powR", [128, 6 * GRP * 64], BF16)
        aTt = S("aTt", [128, GRP * 64], BF16)
        vbtm = S("vbtm", [128, GRP * 128], BF16)
        kwtm = S("kwtm", [128, GRP * 128], BF16)
        Sst = S("Sst", [128, 2 * 128], F32)
        Sbf = S("Sbf", [128, 2 * 128], BF16)
        Zb = S("Zb", [128, 2 * 128], BF16)
        Yb = S("Yb", [128, 2 * 128], BF16)
        vnw = S("vnw", [128, 2 * 128], BF16)
        gbc = S("gbc", [128, 512], BF16)
        grs = facc[0:1, 0:512]
        gout = zbnc
        gf = sqt
        siz = S("siz", [128, 512], BF16)

        KEY = lambda n: f"sg{sg}_{n}"

        cols = [("q", sg, qc, 0), ("k", 8 + sg, kc, 1),
                ("v0", 16 + 2 * sg, vc[:, 0:SEQ], 2), ("v1", 17 + 2 * sg, vc[:, SEQ:2*SEQ], 3),
                ("z0", 32 + 2 * sg, d["z_dram"][:, 0:SEQ], None), ("z1", 33 + 2 * sg, d["z_dram"][:, SEQ:2*SEQ], None)]

        def wq_load(cc, r):
            dst = wring[:, r * KCH * 128:(r + 1) * KCH * 128]
            in_t = d["wqkvz"].rearrange("(k p) c -> p k c", p=128)[:, :, cc * 128:(cc + 1) * 128]
            p.dma("gpsimd", dst.rearrange("p (k c) -> p k c", k=KCH), in_t, wr=[KEY(f"wq{r}")], slot=f"wq{r}")

        p.op("vector", "memset", raw[:, 0:3], 0.0, wr=[KEY("raw0"), KEY("raw1")])
        wq_load(cols[0][1], 0)
        wq_load(cols[1][1], 1)
        for ci, (nm, cc, dstbuf, convrow) in enumerate(cols):
            r = ci % 2
            rawslot = raw[:, 0:(3 + SEQ)]
            for tt in range(4):
                for k in range(KCH):
                    p.op("tensor", "matmul", psQ[:, :],
                         wring[:, r * KCH * 128 + k * 128: r * KCH * 128 + (k + 1) * 128],
                         xln_seq[:, k * SEQ + tt * 512: k * SEQ + (tt + 1) * 512],
                         start=(k == 0), stop=(k == KCH - 1),
                         rd=[KEY(f"wq{r}"), "xlnseq"], wr=["at_psQ"], inc=(k == KCH - 1))
                if convrow is None:
                    p.op("vector", "tensor_copy", zbnc[:, :], psQ[:, :],
                         rd=["at_psQ"], wr=[KEY("zbnc")])
                    p.dma("sync", dstbuf[:, tt * 512:(tt + 1) * 512], zbnc[:, :],
                          rd=[KEY("zbnc")], wr=[KEY("zdram")], slot="zdram")
                else:
                    p.op("vector", "tensor_copy", rawslot[:, 3 + tt * 512: 3 + (tt + 1) * 512], psQ[:, :],
                         rd=["at_psQ"], wr=[KEY(f"raw{r}")])
            if ci + 2 < len(cols):
                wq_load(cols[ci + 2][1], r)
            if convrow is not None:
                wcol = lambda j: g["convw_sb"][:, cc * 4 + j: cc * 4 + j + 1]
                bcol = g["convb_sb"][:, cc: cc + 1]
                for tt in range(4):
                    a0 = facc[:, 0:512]
                    base = tt * 512
                    p.op("vector", "tensor_scalar", a0, rawslot[:, base: base + 512],
                         wcol(0), None, OP.mult, rd=[KEY(f"raw{r}"), "convw_sb"], wr=[KEY("fa0")])
                    for j in range(1, 4):
                        p.op("vector", "scalar_tensor_tensor", a0, rawslot[:, base + j: base + j + 512],
                             wcol(j), a0, OP.mult, OP.add,
                             rd=[KEY(f"raw{r}"), "convw_sb", KEY("fa0")], wr=[KEY("fa0")])
                    p.op("scalar", "activation", dstbuf[:, base: base + 512], a0, AF.Silu,
                         bias=bcol, rd=[KEY("fa0"), "convb_sb"], wr=[KEY(nm)])

        if SCUT <= 1:
            return
        p.op("vector", "memset", rsr[:, :], 1.0, wr=[KEY("rsr")])
        p.op("vector", "memset", rsrb[:, :], 0.0, wr=[KEY("rsrb")])
        for qi, (buf, key, row) in enumerate([(qc, KEY("q"), 0), (kc, KEY("k"), 32)]):
            for tt in range(4):
                p.op("scalar", "activation", sqt[:, :], buf[:, tt * 512:(tt + 1) * 512], AF.Square,
                     rd=[key], wr=[KEY("sqt")])
                p.op("tensor", "matmul", psU[0:1, 0:512], g["ones_bf"][0:128, 0:1], sqt[:, :],
                     start=True, stop=True, rd=[KEY("sqt"), "ones_bf"], wr=["at_psU"])
                p.op("vector", "tensor_copy", rsr[row:row+1, tt * 512:(tt + 1) * 512], psU[0:1, 0:512],
                     rd=["at_psU"], wr=[KEY("rsr")])
        p.op("vector", "tensor_scalar", rsr[0:1, :], rsr[0:1, :], float(DK), float(DK) * EPS_L2,
             OP.mult, OP.add, rd=[KEY("rsr")], wr=[KEY("rsr")])
        p.op("vector", "tensor_scalar", rsr[32:33, :], rsr[32:33, :], 1.0, EPS_L2,
             OP.mult, OP.add, rd=[KEY("rsr")], wr=[KEY("rsr")])
        p.op("vector", "reciprocal", rsr[0:1, :], rsr[0:1, :], rd=[KEY("rsr")], wr=[KEY("rsr")])
        p.op("vector", "reciprocal", rsr[32:33, :], rsr[32:33, :], rd=[KEY("rsr")], wr=[KEY("rsr")])
        p.op("scalar", "activation", rsr[0:1, :], rsr[0:1, :], AF.Sqrt, rd=[KEY("rsr")], wr=[KEY("rsr")])
        p.op("scalar", "activation", rsr[32:33, :], rsr[32:33, :], AF.Sqrt, rd=[KEY("rsr")], wr=[KEY("rsr")])
        p.op("vector", "tensor_copy", rsrb[0:1, :], rsr[0:1, :], rd=[KEY("rsr")], wr=[KEY("rsrb")])
        p.op("vector", "tensor_copy", rsrb[32:33, :], rsr[32:33, :], rd=[KEY("rsr")], wr=[KEY("rsrb")])

        if SCUT <= 2:
            return
        p.op("vector", "memset", Sst[:, :], 0.0, wr=[KEY("Sst")])
        p.op("vector", "memset", Sbf[:, :], 0.0, wr=[KEY("Sbf")])

        for grp in range(NGRP):
            g0 = grp * 512
            bc_specs = [(rsrb[0:32, g0:g0+512], 0, 0, "rsr2"),
                        (rsrb[32:64, g0:g0+512], 0, 1, "rsr2"),
                        (egr[0:32, g0:g0+512], vh0, 2, "egr32"),
                        (egr[0:32, g0:g0+512], vh1, 3, "egr32"),
                        (egr[32:64, g0:g0+512], vh0, 4, "egr32"),
                        (egr[32:64, g0:g0+512], vh1, 5, "egr32")]
            for srcap, row, slot, kind in bc_specs:
                if kind == "rsr2":
                    lhs = g["oh32_sb"][(0 if slot == 0 else 32):(32 if slot == 0 else 64), 0:128]
                    rdk = [KEY("rsrb"), "oh32_sb"]
                else:
                    base = 0 if slot in (2, 3) else 32
                    lhs = g["oh32_sb"][base:base + 32, row * 128:(row + 1) * 128]
                    rdk = ["egr", "oh32_sb"]
                p.op("tensor", "matmul", psU[:, :], lhs, srcap, start=True, stop=True,
                     rd=rdk, wr=["at_psU"])
                p.op("vector", "tensor_copy", scb[:, slot * 512:(slot + 1) * 512], psU[:, :],
                     rd=["at_psU"], wr=[KEY(f"scb{slot}")])
            p.op("vector", "tensor_mul", ktl[:, :], kc[:, g0:g0+512], scb[:, 512:1024],
                 rd=[KEY("k"), KEY("scb1")], wr=[KEY("ktl")])
            p.op("vector", "tensor_mul", qtl[:, :], qc[:, g0:g0+512], scb[:, 0:512],
                 rd=[KEY("q"), KEY("scb0")], wr=[KEY("qtl")])
            for vi, vh in enumerate((vh0, vh1)):
                dst = kbA[:, vi * 512:(vi + 1) * 512]
                p.op("vector", "tensor_mul", dst, ktl[:, :], scb[:, (2 + vi) * 512:(3 + vi) * 512],
                     rd=[KEY("ktl"), KEY(f"scb{2+vi}")], wr=[KEY("kbA")])
                p.op("vector", "tensor_mul", kbg[:, vi * 512:(vi + 1) * 512], dst,
                     scb[:, (4 + vi) * 512:(5 + vi) * 512],
                     rd=[KEY("kbA"), KEY(f"scb{4+vi}")], wr=[KEY("kbg")])
                p.op("vector", "tensor_mul", vbf[:, vi * 512:(vi + 1) * 512],
                     vc[:, vi * SEQ + g0: vi * SEQ + g0 + 512],
                     scb[:, (2 + vi) * 512:(3 + vi) * 512],
                     rd=[KEY(f"v{vi}"), KEY(f"scb{2+vi}")], wr=[KEY("vbf")])
                p.op("vector", "tensor_mul", qtg[:, vi * 512:(vi + 1) * 512], qtl[:, :],
                     scb[:, (4 + vi) * 512:(5 + vi) * 512],
                     rd=[KEY("qtl"), KEY(f"scb{4+vi}")], wr=[KEY("qtg")])
            if SCUT == 31:
                continue
            for vi, vh in enumerate((vh0, vh1)):
                p.op("tensor", "matmul", psU[0:64, :], g["ohg_f32"][32:64, vh * 128: vh * 128 + 64],
                     gam[32:64, g0:g0+512], start=True, stop=True,
                     rd=["gam", "ohg_f32"], wr=["at_psU"])
                for j in range(GRP):
                    c_abs = grp * GRP + j
                    tile_i = c_abs // 2
                    pbase = (c_abs % 2) * 64
                    gcol = gam_tm[pbase:pbase + 64, tile_i * 16 + vh: tile_i * 16 + vh + 1]
                    p.op("vector", "tensor_scalar", args[:, vi * 512 + j * 64: vi * 512 + (j + 1) * 64],
                         psU[0:64, j * 64:(j + 1) * 64], gcol, None, OP.subtract,
                         rd=["at_psU", "gamtm"], wr=[KEY("args")])
                av = args[:, vi * 512:(vi + 1) * 512]
                fsc = facc[0:64, 0:512]
                p.op("vector", "tensor_add", fsc, av, g["msk_su"][:, :],
                     rd=[KEY("args"), "msk_su"], wr=[KEY("fa0")])
                p.op("scalar", "activation", dinc[:, vi * 512:(vi + 1) * 512],
                     fsc, AF.Exp, scale=g["cneg1"][0:64, 0:1],
                     rd=[KEY("fa0")], wr=[KEY("dinc")])
                p.op("vector", "tensor_sub", fsc, av, g["msk_sl"][:, :],
                     rd=[KEY("args"), "msk_sl"], wr=[KEY("fa0")])
                p.op("scalar", "activation", dincT[:, vi * 512:(vi + 1) * 512],
                     fsc, AF.Exp,
                     rd=[KEY("fa0")], wr=[KEY("dincT")])
                p.op("vector", "tensor_mul", wA[:, vi * 512:(vi + 1) * 512],
                     dinc[:, vi * 512:(vi + 1) * 512], g["msk_nsl"][:, :],
                     rd=[KEY("dinc"), "msk_nsl"], wr=[KEY("wA")])
                p.op("vector", "tensor_mul", wR[:, vi * 512:(vi + 1) * 512],
                     dincT[:, vi * 512:(vi + 1) * 512], g["msk_nsu"][:, :],
                     rd=[KEY("dincT"), "msk_nsu"], wr=[KEY("wR")])
            if SCUT == 32:
                continue
            for vi in range(2):
                for j in range(GRP):
                    c_abs = grp * GRP + j
                    ksl = ktl[:, j * 64:(j + 1) * 64]
                    kbsl = kbA[:, vi * 512 + j * 64: vi * 512 + (j + 1) * 64]
                    qsl = qtl[:, j * 64:(j + 1) * 64]
                    p.op("tensor", "matmul", psW1[0:64, 0:64], kbsl, ksl, start=True, stop=True,
                         rd=[KEY("kbA"), KEY("ktl")], wr=["at_psW1"])
                    p.op("tensor", "matmul", psW1[0:64, 64:128], ksl, kbsl, start=True, stop=True,
                         rd=[KEY("kbA"), KEY("ktl")], wr=["at_psW1"])
                    p.op("tensor", "matmul", psW1[0:64, 128:192], ksl, qsl, start=True, stop=True,
                         rd=[KEY("qtl"), KEY("ktl")], wr=["at_psW1"])
                    arow = vi * 64
                    p.op("vector", "tensor_mul", powA[arow:arow + 64, 0 * GRP * 64 + j * 64: 0 * GRP * 64 + (j + 1) * 64],
                         psW1[0:64, 0:64], wA[:, vi * 512 + j * 64: vi * 512 + (j + 1) * 64],
                         rd=["at_psW1", KEY("wA")], wr=[KEY("powA")])
                    p.op("vector", "tensor_mul", powR[arow:arow + 64, 0 * GRP * 64 + j * 64: 0 * GRP * 64 + (j + 1) * 64],
                         psW1[0:64, 64:128], wR[:, vi * 512 + j * 64: vi * 512 + (j + 1) * 64],
                         rd=["at_psW1", KEY("wR")], wr=[KEY("powR")])
                    p.op("vector", "tensor_mul", aTt[arow:arow + 64, j * 64:(j + 1) * 64],
                         psW1[0:64, 128:192], dincT[:, vi * 512 + j * 64: vi * 512 + (j + 1) * 64],
                         rd=["at_psW1", KEY("dincT")], wr=[KEY("aTt")])
            if SCUT == 33:
                continue
            for lvl in range(1, 6):
                for vi in range(2):
                    arow = vi * 64
                    for j in range(GRP):
                        Aj = powA[arow:arow + 64, (lvl - 1) * GRP * 64 + j * 64: (lvl - 1) * GRP * 64 + (j + 1) * 64]
                        Rj = powR[arow:arow + 64, (lvl - 1) * GRP * 64 + j * 64: (lvl - 1) * GRP * 64 + (j + 1) * 64]
                        tp = None if vi == 0 else (64, 64)
                        if lvl < 5:
                            p.op("tensor", "matmul", psQ[0:64, j * 64:(j + 1) * 64] if vi == 0 else psQ[64:128, j * 64:(j + 1) * 64],
                                 Rj, Aj, start=True, stop=True, tile_position=tp,
                                 rd=[KEY("powA"), KEY("powR")], wr=["at_psQ"])
                        p.op("tensor", "matmul", psU[0:64, j * 64:(j + 1) * 64] if vi == 0 else psU[64:128, j * 64:(j + 1) * 64],
                             Aj, Rj, start=True, stop=True, tile_position=tp,
                             rd=[KEY("powA"), KEY("powR")], wr=["at_psU"])
                for vi in range(2):
                    pr = slice(vi * 64, vi * 64 + 64)
                    if lvl < 5:
                        p.op("vector", "tensor_copy", powA[pr, lvl * GRP * 64:(lvl + 1) * GRP * 64],
                             (psQ[0:64, 0:512] if vi == 0 else psQ[64:128, 0:512]),
                             rd=["at_psQ"], wr=[KEY("powA")])
                    p.op("vector", "tensor_copy", powR[pr, lvl * GRP * 64:(lvl + 1) * GRP * 64],
                         (psU[0:64, 0:512] if vi == 0 else psU[64:128, 0:512]),
                         rd=["at_psU"], wr=[KEY("powR")])
            if SCUT == 34:
                continue
            for vi in range(2):
                for j in range(GRP):
                    c_abs = grp * GRP + j
                    tile_i = c_abs // 2
                    pbase = (c_abs % 2) * 64
                    p.op("tensor", "transpose", psTb[0:64, :],
                         vbf[:, vi * 512 + j * 64: vi * 512 + (j + 1) * 64], g["ident_bf"][:, :],
                         rd=[KEY("vbf"), "ident_bf"], wr=["at_psTb"])
                    p.op("vector", "tensor_copy", vbtm[vi * 64:(vi + 1) * 64, j * 128:(j + 1) * 128],
                         psTb[0:64, :], rd=["at_psTb"], wr=[KEY("vbtm")])
                    p.op("tensor", "transpose", psTb[0:64, :],
                         ktl[:, j * 64:(j + 1) * 64], g["ident_bf"][:, :],
                         rd=[KEY("ktl"), "ident_bf"], wr=["at_psTb"])
                    kwcol = kw_tm_f[pbase:pbase + 64, tile_i * 16 + vh0 + vi: tile_i * 16 + vh0 + vi + 1]
                    p.op("vector", "tensor_scalar", kwtm[vi * 64:(vi + 1) * 64, j * 128:(j + 1) * 128],
                         psTb[0:64, :], kwcol, None, OP.mult,
                         rd=["at_psTb", "kwtmf"], wr=[KEY("kwtm")])
            if SCUT <= 3:
                continue
            psVv = [psV, psW1[0:64, 0:128]]
            kVv = ["at_psV", "at_psW1"]
            psSv = [psS, psQ[:, 0:128]]
            kSv = ["at_psS", "at_psQ"]
            for j in range(GRP):
                c_abs = grp * GRP + j
                arow_ = [0, 64]
                Ssl_ = [Sst[:, 0:128], Sst[:, 128:256]]
                Sbsl_ = [Sbf[:, 0:128], Sbf[:, 128:256]]
                for vi in range(2):
                    p.op("tensor", "matmul", psVv[vi],
                         kbg[:, vi * 512 + j * 64: vi * 512 + (j + 1) * 64], Sbsl_[vi],
                         start=True, stop=True, rd=[KEY("kbg"), KEY("Sbf")], wr=[kVv[vi]])
                for vi in range(2):
                    arow = arow_[vi]
                    p.op("vector", "tensor_sub", Yb[arow:arow + 64, 0:128],
                         vbtm[arow:arow + 64, j * 128:(j + 1) * 128], psVv[vi],
                         rd=[KEY("vbtm"), kVv[vi]], wr=[KEY("Yb")])
                srcb, dstb = Yb, Zb
                for lvl in range(6):
                    sk = KEY("Yb" if srcb is Yb else "Zb")
                    dk = KEY("Zb" if dstb is Zb else "Yb")
                    for vi in range(2):
                        arow = arow_[vi]
                        Rl = powR[arow:arow + 64, lvl * GRP * 64 + j * 64: lvl * GRP * 64 + (j + 1) * 64]
                        p.op("tensor", "matmul", psVv[vi], g["ident_bf"][arow:arow + 64, arow:arow + 64],
                             srcb[arow:arow + 64, 0:128], start=True, stop=False,
                             rd=[sk, "ident_bf"], wr=[kVv[vi]], inc=False)
                        p.op("tensor", "matmul", psVv[vi], Rl,
                             srcb[arow:arow + 64, 0:128], start=False, stop=True,
                             rd=[sk, KEY("powR")], wr=[kVv[vi]])
                    for vi in range(2):
                        arow = arow_[vi]
                        if lvl == 5:
                            p.op("vector", "tensor_copy", vnw[arow:arow + 64, 0:128], psVv[vi],
                                 rd=[kVv[vi]], wr=[KEY("vnw")])
                        else:
                            p.op("vector", "tensor_copy", dstb[arow:arow + 64, 0:128], psVv[vi],
                                 rd=[kVv[vi]], wr=[dk])
                    srcb, dstb = dstb, srcb
                for vi in range(2):
                    arow = arow_[vi]
                    p.op("tensor", "matmul", psO[vi][:, j * 64:(j + 1) * 64], Sbsl_[vi],
                         qtg[:, vi * 512 + j * 64: vi * 512 + (j + 1) * 64],
                         start=True, stop=False, rd=[KEY("Sbf"), KEY("qtg")], wr=[f"at_psO{vi}"], inc=False)
                    p.op("tensor", "matmul", psO[vi][:, j * 64:(j + 1) * 64],
                         vnw[arow:arow + 64, 0:128], aTt[arow:arow + 64, j * 64:(j + 1) * 64],
                         start=False, stop=True, rd=[KEY("vnw"), KEY("aTt")], wr=[f"at_psO{vi}"])
                for vi in range(2):
                    arow = arow_[vi]
                    p.op("tensor", "matmul", psSv[vi], kwtm[arow:arow + 64, j * 128:(j + 1) * 128],
                         vnw[arow:arow + 64, 0:128], start=True, stop=True,
                         rd=[KEY("kwtm"), KEY("vnw")], wr=[kSv[vi]])
                for vi in range(2):
                    eglcol = eglb[:, (vh0 + vi) * NCH + c_abs: (vh0 + vi) * NCH + c_abs + 1]
                    p.op("vector", "scalar_tensor_tensor", Ssl_[vi], Ssl_[vi], eglcol, psSv[vi],
                         OP.mult, OP.add, rd=[KEY("Sst"), "eglb", kSv[vi]], wr=[KEY("Sst")])
                    p.op("vector", "tensor_copy", Sbsl_[vi], Ssl_[vi], rd=[KEY("Sst")], wr=[KEY("Sbf")])
            if SCUT <= 4:
                continue
            for vi, vh in enumerate((vh0, vh1)):
                p.dma("sync", zbnc[:, :], d["z_dram"][:, vi * SEQ + g0: vi * SEQ + g0 + 512],
                      rd=[KEY("zdram")], wr=[KEY("zbnc")], slot="zbnc_l")
                p.op("scalar", "activation", siz[:, :], zbnc[:, :],
                     AF.Silu, rd=[KEY("zbnc")], wr=[KEY("siz")])
                p.op("scalar", "activation", sqt[:, :], psO[vi][:, :], AF.Square,
                     rd=[f"at_psO{vi}"], wr=[KEY("sqt")])
                p.op("tensor", "matmul", psU[0:1, 0:512], g["ones_bf"][0:128, 0:1], sqt[:, :],
                     start=True, stop=True, rd=[KEY("sqt"), "ones_bf"], wr=["at_psU"])
                p.op("vector", "tensor_scalar", grs, psU[0:1, 0:512], 1.0 / DV, EPS_RMS,
                     OP.mult, OP.add, rd=["at_psU"], wr=[KEY("fa0")])
                p.op("vector", "reciprocal", grs, grs, rd=[KEY("fa0")], wr=[KEY("fa0")])
                p.op("scalar", "activation", grs, grs, AF.Sqrt,
                     rd=[KEY("fa0")], wr=[KEY("fa0")])
                p.op("tensor", "matmul", psU[:, :], g["ones_f1"][0:1, 0:128], grs,
                     start=True, stop=True, rd=[KEY("fa0"), "ones_f1"], wr=["at_psU"])
                p.op("vector", "tensor_copy", gbc[:, :], psU[:, :], rd=["at_psU"], wr=[KEY("gbc")])
                p.op("vector", "scalar_tensor_tensor", gf[:, :], psO[vi][:, :], g["nw_col"][:, 0:1],
                     gbc[:, :], OP.mult, OP.mult,
                     rd=[f"at_psO{vi}", "nw_col", KEY("gbc")], wr=[KEY("sqt")])
                p.op("vector", "tensor_mul", gout[:, :], gf[:, :], siz[:, :],
                     rd=[KEY("sqt"), KEY("siz")], wr=[KEY("zbnc")])
                p.dma("sync", d["gated_dram"].rearrange("(v h t) c -> v h t c", h=2, t=TOK)
                      if False else d["gated_dram"][sg * 256 + vi * 128: sg * 256 + (vi + 1) * 128,
                                                    g0:g0 + 512],
                      gout[:, :], rd=[KEY("zbnc")], wr=["gated_dram"], slot="gdram")
        if d.get("dbg") and sg == 0:
            dbg = d["dbg"]
            p.dma("sync", dbg["dbg_qc"], qc[:, :], rd=[KEY("q")], slot="dbg")
            p.dma("sync", dbg["dbg_kc"], kc[:, :], rd=[KEY("k")], slot="dbg")
            p.dma("sync", dbg["dbg_vc"], vc[:, :], rd=[KEY("v0"), KEY("v1")], slot="dbg")
            p.dma("sync", dbg["dbg_rsr"], rsr[:, :], rd=[KEY("rsr")], slot="dbg")
            p.dma("sync", dbg["dbg_Sst"], Sst[:, :], rd=[KEY("Sst")], slot="dbg")
            p.dma("sync", dbg["dbg_vbtm"], vbtm[:, :], rd=[KEY("vbtm")], slot="dbg")
            p.dma("sync", dbg["dbg_kwtm"], kwtm[:, :], rd=[KEY("kwtm")], slot="dbg")
            p.dma("sync", dbg["dbg_aTt"], aTt[:, :], rd=[KEY("aTt")], slot="dbg")
            p.dma("sync", dbg["dbg_powR"], powR[:, :], rd=[KEY("powR")], slot="dbg")
            p.dma("sync", dbg["dbg_qtg"], qtg[:, :], rd=[KEY("qtg")], slot="dbg")
            p.dma("sync", dbg["dbg_kbg"], kbg[:, :], rd=[KEY("kbg")], slot="dbg")
            p.dma("sync", dbg["dbg_vbf"], vbf[:, :], rd=[KEY("vbf")], slot="dbg")
            p.dma("sync", dbg["dbg_ktl"], ktl[:, :], rd=[KEY("ktl")], slot="dbg")
            p.dma("sync", dbg["dbg_dincT"], dincT[:, :], rd=[KEY("dincT")], slot="dbg")
            p.dma("sync", dbg["dbg_wA"], wA[:, :], rd=[KEY("wA")], slot="dbg")


def emit_ag2(p, nc, es, g, d):
    gated_rows = d["gated_dram"].rearrange("v (h t) -> (v h) t", h=2)
    with contextlib.ExitStack() as aes:
        bnc = sbuf(nc, aes, "ag2bnc", [128, TOK], BF16)
        for vch in range(16):
            p.op("gpsimd", "indirect_dma_start",
                 rd=["gated_dram", "offs_sb"], wr=["ag2bnc"], sem=p.dma_sem("ag2bnc"),
                 out=bnc[:, :], out_offset=None,
                 in_=d["gated_dram"].rearrange("(p a) f -> p (a f)", p=128),
                 in_offset=IndirectOffsetOnAxis(g["offs_sb"][:, 32 + vch: 33 + vch], 1))
            p.dma("gpsimd", d["ag2_in"][vch * 128:(vch + 1) * 128, :], bnc[:, :],
                  rd=["ag2bnc"], wr=["ag2_in"], slot="ag2_in")
    p.op("gpsimd", "collective_compute", "AllGather", OP.bypass,
         rd=["ag2_in"], wr=["ag2_out"],
         replica_groups=[list(range(8))], ins=[d["ag2_in"]], outs=[d["ag2_out"]])


def emit_wout(p, nc, es, g, d, x_res2):
    with contextlib.ExitStack() as wes:
        S = lambda name, shape, dt: sbuf(nc, wes, f"wo_{name}", shape, dt)
        P_ = lambda name, shape, dt=F32: psum(nc, wes, f"wo_{name}", shape, dt)
        psA = P_("psA", [128, 512]); psB = P_("psB", [128, 512])
        gt = S("gt", [128, 32 * TOK], BF16)
        wrg = S("wrg", [128, 2 * 32 * 128], BF16)
        gated_rows = d["gated_dram"].rearrange("v (h t) -> (v h) t", h=2)
        for vch in range(16):
            p.op("gpsimd", "indirect_dma_start",
                 rd=["gated_dram", "offs_sb"], wr=[f"gtg{vch // 8}"], sem=p.dma_sem(f"gtg{vch // 8}"),
                 out=gt[:, vch * TOK:(vch + 1) * TOK], out_offset=None,
                 in_=d["gated_dram"].rearrange("(p a) f -> p (a f)", p=128),
                 in_offset=IndirectOffsetOnAxis(g["offs_sb"][:, 48 + vch: 49 + vch], 1))
        for vch in range(16):
            p.op("gpsimd", "indirect_dma_start",
                 rd=["ag2_out", "offs_sb"], wr=[f"gtg{2 + vch // 8}"], sem=p.dma_sem(f"gtg{2 + vch // 8}"),
                 out=gt[:, (16 + vch) * TOK:(17 + vch) * TOK], out_offset=None,
                 in_=d["ag2_out"].rearrange("(p a) f -> p (a f)", p=128),
                 in_offset=IndirectOffsetOnAxis(g["offs_sb"][:, 64 + vch: 65 + vch], 1))

        def w_load(m, r):
            dst = wrg[:, r * 32 * 128:(r + 1) * 32 * 128]
            p.dma("gpsimd", dst[:, 0:16 * 128].rearrange("p (v m) -> p v m", v=16),
                  d["w_out_mine"].rearrange("(v p) m -> p v m", p=128)[:, :, m * 128:(m + 1) * 128],
                  wr=[f"wo_w{r}"])
            p.dma("gpsimd", dst[:, 16 * 128:32 * 128].rearrange("p (v m) -> p v m", v=16),
                  d["w_out_partner"].rearrange("(v p) m -> p v m", p=128)[:, :, m * 128:(m + 1) * 128],
                  wr=[f"wo_w{r}"])
        w_load(0, 0); w_load(1, 1)
        for m in range(KCH):
            r = m % 2
            pd = psA if r == 0 else psB
            pk = "wo_psA" if r == 0 else "wo_psB"
            for tt in range(NTT):
                for vch in range(32):
                    p.op("tensor", "matmul", pd[:, :],
                         wrg[:, r * 32 * 128 + vch * 128: r * 32 * 128 + (vch + 1) * 128],
                         gt[:, vch * TOK + tt * TT: vch * TOK + (tt + 1) * TT],
                         start=(vch == 0), stop=(vch == 31),
                         rd=[f"wo_w{r}", f"gtg{vch // 8}"], wr=[pk], inc=(vch == 31))
                dst = x_res2[:, m * TOK + tt * TT: m * TOK + (tt + 1) * TT]
                p.op("vector", "tensor_add", dst, pd[:, :], dst, rd=[pk, "ffn2xres"], wr=["ffn2xres"])
            if m + 2 < KCH:
                w_load(m + 2, r)



def build_program(stages=("ffn1", "attn", "ffn2")):
    nc = bass.Bass("TRN2", target_bir_lowering=False, debug=False)
    d = {}
    di = lambda name, shape, dt=F32: nc.dram_tensor(name, shape, dt, kind="ExternalInput").ap()
    d["x_in"] = di("x_in", [KCH, 128, TOK])
    d["wg"] = [di(f"wg{e}", [E, HEXP]) for e in range(NEXP)]
    d["wu"] = [di(f"wu{e}", [E, HEXP]) for e in range(NEXP)]
    d["wd"] = [di(f"wd{e}", [HEXP, E]) for e in range(NEXP)]
    d["shg"] = di("shg", [E, HSH]); d["shu"] = di("shu", [E, HSH])
    d["shd"] = di("shd", [HSH, E])
    d["gw"] = di("gw", [E, NEXP]); d["seg"] = di("seg", [E, 1])
    d["ident_in"] = di("ident_in", [128, 128], F32)
    d["oh8_in"] = di("oh8_in", [8, 8 * 128], F32)
    d["oh32_in"] = di("oh32_in", [64, 16 * 128], F32)
    d["oh2_in"] = di("oh2_in", [2, 2 * 128], F32)
    d["wqkvz"] = di("wqkvz", [E, 6144])
    d["wba"] = di("wba", [E, 64])
    d["convw_in"] = di("convw_in", [128, 128], F32)
    d["convb_in"] = di("convb_in", [128, 32], F32)
    d["alog_in"] = di("alog_in", [64, 1], F32)
    d["dt_in"] = di("dt_in", [64, 1], F32)
    d["nw_in"] = di("nw_in", [128, 1], F32)
    d["w_out_mine"] = di("w_out_mine", [2048, 2048])
    d["w_out_partner"] = di("w_out_partner", [2048, 2048])
    d["msk_in"] = di("msk_in", [64, 4 * 512], F32)
    d["offs_in"] = di("offs_in", [128, 80], I32)
    y = nc.dram_tensor("y", [KCH, 128, TOK], F32, kind="ExternalOutput").ap()
    dbg = {}
    if DEBUG:
        for nm, shape, dt in [("dbg_xlnseq", [128, KCH * SEQ], BF16),
                              ("dbg_gam", [64, SEQ], F32), ("dbg_egr", [64, SEQ], BF16),
                              ("dbg_kwr", [64, SEQ], BF16), ("dbg_eglb", [128, NVH * NCH], F32),
                              ("dbg_gamtm", [128, 16 * KCH], F32), ("dbg_kwtmf", [128, 16 * KCH], F32),
                              ("dbg_qc", [128, SEQ], BF16), ("dbg_kc", [128, SEQ], BF16),
                              ("dbg_vc", [128, 2 * SEQ], BF16), ("dbg_rsr", [64, SEQ], F32),
                              ("dbg_Sst", [128, 2 * 128], F32),
                              ("dbg_vbtm", [128, 8 * 128], BF16), ("dbg_kwtm", [128, 8 * 128], BF16),
                              ("dbg_aTt", [128, 8 * 64], BF16), ("dbg_powR", [128, 6 * 8 * 64], BF16),
                              ("dbg_qtg", [128, 2 * 512], BF16), ("dbg_kbg", [128, 2 * 512], BF16),
                              ("dbg_vbf", [128, 2 * 512], BF16), ("dbg_ktl", [128, 512], BF16),
                              ("dbg_gated", [2048, SEQ], BF16), ("dbg_zd", [128, 2 * SEQ], BF16),
                              ("dbg_dincT", [64, 2 * 512], BF16), ("dbg_wA", [64, 2 * 512], BF16)]:
            dbg[nm] = nc.dram_tensor(nm, shape, dt, kind="ExternalOutput").ap()
    d["dbg"] = dbg
    d["ag1_in"] = nc.dram_tensor("ag1_in", [128, KCH * TOK], BF16)
    d["ag1_out"] = nc.dram_tensor("ag1_out", [8 * 128, KCH * TOK], BF16)
    d["xres_dram"] = nc.dram_tensor("xres_dram", [128, KCH * TOK], F32).ap()
    d["gated_dram"] = nc.dram_tensor("gated_dram", [2048, SEQ], BF16).ap()
    d["ag2_in"] = nc.dram_tensor("ag2_in", [2048, TOK], BF16).ap()
    d["z_dram"] = nc.dram_tensor("z_dram", [128, 2 * SEQ], BF16).ap()
    d["ag2_out"] = nc.dram_tensor("ag2_out", [8 * 2048, TOK], BF16).ap()

    with contextlib.ExitStack() as es:
        p = PB(nc, es)
        g = dict(d)
        for nm, shape, dt in [("ones_bf", [128, 128], BF16), ("ones_f1", [1, 128], F32),
                              ("cneg1", [128, 1], F32), ("cone", [128, 1], F32),
                              ("ident_bf", [128, 128], BF16),
                              ("oh8_sb", [8, 8 * 128], BF16),
                              ("gw_sb", [128, KCH * NEXP], BF16), ("seg_sb", [128, KCH], BF16),
                              ("nw_col", [128, 1], F32),
                              ("offs_sb", [128, 80], I32)]:
            g[nm] = sbuf(nc, es, nm, shape, dt)
        g["ident64"] = g["ident_bf"]

        p.op("vector", "memset", g["ones_bf"][:, :], 1.0, wr=["ones_bf"])
        p.op("vector", "memset", g["ones_f1"][:, :], 1.0, wr=["ones_f1"])
        p.op("vector", "memset", g["cneg1"][:, :], -1.0, wr=["cneg1"])
        p.op("vector", "memset", g["cone"][:, :], 1.0, wr=["cone"])
        p.dma("gpsimd", g["ident_bf"][:, :], d["ident_in"][:, :], wr=["ident_bf"], slot="consts")
        p.dma("gpsimd", g["oh8_sb"][:, :], d["oh8_in"][:, :], wr=["oh8_sb"], slot="consts")
        p.dma("gpsimd", g["gw_sb"].ap().rearrange("p (k e) -> p k e", k=KCH),
              d["gw"].rearrange("(k p) e -> p k e", p=128), wr=["gw_sb"], slot="consts")
        p.dma("gpsimd", g["seg_sb"].ap().rearrange("p (k e) -> p k e", k=KCH),
              d["seg"].rearrange("(k p) e -> p k e", p=128), wr=["seg_sb"], slot="consts",
              allow_slow_non_contiguous=True)
        p.dma("gpsimd", g["nw_col"][:, :], d["nw_in"][:, :], wr=["nw_col"], slot="consts")
        ev = p.dma("gpsimd", g["offs_sb"][:, :], d["offs_in"][:, :], wr=["offs_sb"], slot="consts")
        for kk in ["ident_bf", "oh8_sb", "gw_sb", "seg_sb", "nw_col", "offs_sb"]:
            p.last_w[kk] = ev

        do_attn = "attn" in stages
        with contextlib.ExitStack() as s1:
            x_res = sbuf(nc, s1, "x_res", [128, KCH * TOK], F32)
            g["x_res"] = x_res
            p.dma("sync", x_res.ap().rearrange("p (k t) -> p k t", k=KCH),
                  d["x_in"].rearrange("k p t -> p k t"), wr=["ffn1xres"])
            if "ffn1" in stages:
                emit_ffn(p, nc, s1, g, "ffn1")
            if do_attn:
                p.barrier()
                def ln2_cb(tt, xln, key):
                    p.dma("sync", d["ag1_in"].ap().rearrange("p (k t) -> p k t", k=KCH)[:, :, tt * TT:(tt + 1) * TT],
                          xln.ap().rearrange("p (k t) -> p k t", k=KCH), rd=[key], wr=["ag1_in"],
                          slot="ag1_in")
                emit_ln_to(p, nc, s1, g, x_res, ln2_cb, "ln2")
                p.dma("sync", d["xres_dram"].rearrange("p (k t) -> p k t", k=KCH),
                      x_res.ap().rearrange("p (k t) -> p k t", k=KCH), rd=["ffn1xres"], wr=["xres_spill"])
            else:
                p.dma("sync", y.rearrange("k p t -> p k t"),
                      x_res.ap().rearrange("p (k t) -> p k t", k=KCH), rd=["ffn1xres"], wr=["yout"])
        if do_attn:
            p.barrier()
            with contextlib.ExitStack() as s2:
                for nm, shape, dt in [("ident_f1", [128, 128], F32),
                                      ("oh32_sb", [64, 16 * 128], BF16),
                                      ("oh2_sb", [2, 2 * 128], BF16),
                                      ("convw_sb", [128, 128], F32), ("convb_sb", [128, 32], F32),
                                      ("alog_col", [64, 1], F32), ("dt_col", [64, 1], F32),
                                      ("msk_su", [64, 512], F32), ("msk_sl", [64, 512], F32),
                                      ("msk_nsl", [64, 512], F32), ("msk_nsu", [64, 512], F32)]:
                    g[nm] = sbuf(nc, s2, nm, shape, dt)
                _ac = ["ident_f1", "oh32_sb", "oh2_sb", "convw_sb", "convb_sb",
                       "alog_col", "dt_col", "msk_su", "msk_sl", "msk_nsl", "msk_nsu"]
                p.dma("gpsimd", g["ident_f1"][:, :], d["ident_in"][:, :], wr=["ident_f1"], slot="aconsts")
                p.dma("gpsimd", g["oh32_sb"][:, :], d["oh32_in"][:, :], wr=["oh32_sb"], slot="aconsts")
                p.dma("gpsimd", g["oh2_sb"][:, :], d["oh2_in"][:, :], wr=["oh2_sb"], slot="aconsts")
                p.dma("gpsimd", g["convw_sb"][:, :], d["convw_in"][:, :], wr=["convw_sb"], slot="aconsts")
                p.dma("gpsimd", g["convb_sb"][:, :], d["convb_in"][:, :], wr=["convb_sb"], slot="aconsts")
                p.dma("gpsimd", g["alog_col"][:, :], d["alog_in"][:, :], wr=["alog_col"], slot="aconsts")
                p.dma("gpsimd", g["dt_col"][:, :], d["dt_in"][:, :], wr=["dt_col"], slot="aconsts")
                p.dma("gpsimd", g["msk_su"][:, :], d["msk_in"][:, 0:512], wr=["msk_su"], slot="aconsts")
                p.dma("gpsimd", g["msk_sl"][:, :], d["msk_in"][:, 512:1024], wr=["msk_sl"], slot="aconsts")
                p.dma("gpsimd", g["msk_nsl"][:, :], d["msk_in"][:, 1024:1536], wr=["msk_nsl"], slot="aconsts")
                p.dma("gpsimd", g["msk_nsu"][:, :], d["msk_in"][:, 1536:2048], wr=["msk_nsu"], slot="aconsts")
                g["ohg_f32"] = sbuf(nc, s2, "ohg_f32", [64, 2048], F32)
                ev = p.dma("gpsimd", g["ohg_f32"][:, :], d["oh32_in"][:, 0:2048],
                           wr=["ohg_f32"], slot="aconsts")
                for kk in _ac + ["ohg_f32"]:
                    p.last_w[kk] = ev
                ctx = emit_attention(p, nc, s2, g, d)
                if ctx is None:
                    pass
                if DEBUG and ctx is not None:
                    dbg = d["dbg"]
                    p.dma("sync", dbg["dbg_xlnseq"], ctx["xln_seq"][:, :], rd=["xlnseq"], slot="dbg")
                    p.dma("sync", dbg["dbg_gam"], ctx["gam"][:, :], rd=["gam"], slot="dbg")
                    p.dma("sync", dbg["dbg_egr"], ctx["egr"][:, :], rd=["egr"], slot="dbg")
                    p.dma("sync", dbg["dbg_kwr"], ctx["kwr"][:, :], rd=["kwr"], slot="dbg")
                    p.dma("sync", dbg["dbg_eglb"], ctx["eglb"][:, :], rd=["eglb"], slot="dbg")
                    p.dma("sync", dbg["dbg_gamtm"], ctx["gam_tm"][:, :], rd=["gamtm"], slot="dbg")
                    p.dma("sync", dbg["dbg_kwtmf"], ctx["kw_tm_f"][:, :], rd=["kwtmf"], slot="dbg")
                if ctx is not None and ACUT >= 3:
                    for sg in range(NSG):
                        emit_subgroup(p, nc, s2, g, d, ctx, sg)
                        p.barrier()
                if DEBUG:
                    p.dma("sync", d["dbg"]["dbg_gated"], d["gated_dram"][:, :], rd=["gated_dram"], slot="dbg")
                    p.dma("sync", d["dbg"]["dbg_zd"], d["z_dram"][:, :], rd=["sg7_zdram"], slot="dbg")
                if ACUT >= 4:
                    emit_ag2(p, nc, s2, g, d)
            p.barrier()
            with contextlib.ExitStack() as s3:
                x_res2 = sbuf(nc, s3, "x_res2", [128, KCH * TOK], F32)
                g["x_res"] = x_res2
                p.dma("sync", x_res2.ap().rearrange("p (k t) -> p k t", k=KCH),
                      d["xres_dram"].rearrange("p (k t) -> p k t", k=KCH),
                      rd=["xres_spill"], wr=["ffn2xres"])
                if ACUT >= 5:
                    emit_wout(p, nc, s3, g, d, x_res2)
                p.barrier()
                if "ffn2" in stages:
                    emit_ffn(p, nc, s3, g, "ffn2")
                p.dma("sync", y.rearrange("k p t -> p k t"),
                      x_res2.ap().rearrange("p (k t) -> p k t", k=KCH), rd=["ffn2xres"], wr=["yout"])
        yev = p.last_w["yout"]
        p.stream["sync"].append(lambda h, s=p.sem[yev[0]], v=yev[1]: h.wait_ge(s, v))
        block = es.enter_context(nc.Block())
        p.finalize(block)
    return nc


def host_prep(inputs):
    x = np.ascontiguousarray(np.asarray(inputs["L0_attn_hidden_states"]))
    wq = np.asarray(inputs["L0_attn_w_qkvz"]); wba = np.asarray(inputs["L0_attn_w_ba"])
    convw = np.asarray(inputs["L0_attn_conv_w"]); convb = np.asarray(inputs["L0_attn_conv_b"])
    alog = np.asarray(inputs["L0_attn_A_log"]); dtb = np.asarray(inputs["L0_attn_dt_bias"])
    nw = np.asarray(inputs["L0_attn_norm_weight"]); wo = np.asarray(inputs["L0_attn_w_out"])
    gw = np.asarray(inputs["L0_ffn_gate_w"])
    wg = np.asarray(inputs["L0_ffn_w_gate"]); wu = np.asarray(inputs["L0_ffn_w_up"])
    wd = np.asarray(inputs["L0_ffn_w_down"])
    shg = np.asarray(inputs["L0_ffn_sh_gate"]); shu = np.asarray(inputs["L0_ffn_sh_up"])
    shd = np.asarray(inputs["L0_ffn_sh_down"]); seg = np.asarray(inputs["L0_ffn_se_gate_w"])

    xt = np.ascontiguousarray(x.transpose(0, 2, 1))
    wg_t = np.ascontiguousarray(wg.transpose(0, 2, 1))
    wu_t = np.ascontiguousarray(wu.transpose(0, 2, 1))
    wd_t = np.ascontiguousarray(wd.transpose(0, 2, 1))
    shg_t = np.ascontiguousarray(shg.T); shu_t = np.ascontiguousarray(shu.T)
    shd_t = np.ascontiguousarray(shd.T)
    gw_t = np.ascontiguousarray(gw.T); seg_t = np.ascontiguousarray(seg.T)
    ident = np.eye(128, dtype=np.float32)
    oh8 = np.zeros((8, 8 * 128), np.float32)
    for e in range(8):
        oh8[e, e * 128:(e + 1) * 128] = 1.0
    oh16 = np.zeros((32, 16 * 128), np.float32)
    for e in range(16):
        oh16[e, e * 128:(e + 1) * 128] = 1.0
    oh32 = np.concatenate([oh16, oh16], axis=0)
    oh2 = np.zeros((2, 2 * 128), np.float32)
    oh2[0, 0:128] = 1.0; oh2[1, 128:256] = 1.0
    su = np.triu(np.ones((64, 64), np.float32), 1)
    sl = np.tril(np.ones((64, 64), np.float32), -1)
    msk = np.concatenate([np.tile(1000.0 * su, (1, 8)), np.tile(1000.0 * sl, (1, 8)),
                          np.tile(-sl, (1, 8)), np.tile(-su, (1, 8))], axis=1)

    in_maps = []
    for c in range(8):
        b, h = c // 2, c % 2
        gg = h
        xs = np.ascontiguousarray(xt[b, :, h * TOK:(h + 1) * TOK]).reshape(KCH, 128, TOK)
        qb = wq[:, gg * 1024:(gg + 1) * 1024]
        kb = wq[:, 2048 + gg * 1024: 2048 + (gg + 1) * 1024]
        vb = wq[:, 4096 + gg * 2048: 4096 + (gg + 1) * 2048]
        zb = wq[:, 8192 + gg * 2048: 8192 + (gg + 1) * 2048]
        wqkvz = np.ascontiguousarray(np.concatenate([qb, kb, vb, zb], axis=1))
        cw_cols = np.concatenate([convw[:, gg * 1024:(gg + 1) * 1024],
                                  convw[:, 2048 + gg * 1024: 2048 + (gg + 1) * 1024],
                                  convw[:, 4096 + gg * 2048: 4096 + (gg + 1) * 2048]], axis=1)
        cb_cols = np.concatenate([convb[gg * 1024:(gg + 1) * 1024],
                                  convb[2048 + gg * 1024: 2048 + (gg + 1) * 1024],
                                  convb[4096 + gg * 2048: 4096 + (gg + 1) * 2048]])
        convw_pt = np.ascontiguousarray(cw_cols.T.reshape(32, 128, 4).transpose(1, 0, 2).reshape(128, 128))
        convb_pt = np.ascontiguousarray(cb_cols.reshape(32, 128).T)
        wba_c = np.zeros((2048, 64), np.float32)
        wba_c[:, 0:16] = wba[:, gg * 16:(gg + 1) * 16]
        wba_c[:, 32:48] = wba[:, 32 + gg * 16: 32 + (gg + 1) * 16]
        wba_c = np.ascontiguousarray(wba_c)
        av = np.zeros((64, 1), np.float32); av[32:48, 0] = alog[gg * 16:(gg + 1) * 16]
        dv = np.zeros((64, 1), np.float32); dv[32:48, 0] = dtb[gg * 16:(gg + 1) * 16]
        nwc = np.ascontiguousarray(nw.reshape(128, 1))
        wom = np.ascontiguousarray(wo[gg * 2048:(gg + 1) * 2048])
        wop = np.ascontiguousarray(wo[(1 - gg) * 2048:(2 - gg) * 2048])
        offs = np.zeros((128, 80), np.int32)
        par = np.arange(128, dtype=np.int32)
        for half in range(2):
            for k in range(16):
                offs[:, half * 16 + k] = ((2 * b + half) * 128 + par) * (KCH * TOK) + k * TOK
        for vch in range(16):
            offs[:, 32 + vch] = ((vch * 128 + par) * 2 + (1 - h)) * TOK
            offs[:, 48 + vch] = ((vch * 128 + par) * 2 + h) * TOK
            offs[:, 64 + vch] = ((c ^ 1) * 2048 + vch * 128 + par) * TOK
        m = {"x_in": xs, "shg": shg_t, "shu": shu_t, "shd": shd_t,
             "gw": gw_t, "seg": seg_t, "ident_in": ident, "oh8_in": oh8,
             "oh32_in": oh32, "oh2_in": oh2, "wqkvz": wqkvz, "wba": wba_c,
             "convw_in": convw_pt, "convb_in": convb_pt, "alog_in": av, "dt_in": dv,
             "nw_in": nwc, "w_out_mine": wom, "w_out_partner": wop,
             "msk_in": msk, "offs_in": offs}
        for e in range(NEXP):
            m[f"wg{e}"] = wg_t[e]; m[f"wu{e}"] = wu_t[e]; m[f"wd{e}"] = wd_t[e]
        in_maps.append(m)
    return in_maps


STAGES = ("ffn1", "attn", "ffn2")
DEBUG = False
NSG = 8
ACUT = 5
SCUT = 5

def kernel(**inputs):
    in_maps = host_prep(inputs)
    nc = build_program(stages=STAGES)
    res = run_bass_kernel_spmd(nc, in_maps, core_ids=list(range(8)))
    out = np.zeros((4, 2048, 2048), np.float32)
    for c in range(8):
        b, h = c // 2, c % 2
        yk = np.asarray(res.results[c]["y"])
        out[b, h * TOK:(h + 1) * TOK, :] = yk.reshape(E, TOK).T
    return out


# version 06 (final)
# speedup vs baseline: 1.0852x; 1.0102x over previous
import contextlib
import numpy as np
import concourse.bass as bass
import concourse.mybir as mybir
from concourse.bass_core_rust import IndirectOffsetOnAxis
from concourse.bass_utils import run_bass_kernel_spmd

F32 = mybir.dt.float32
BF16 = mybir.dt.bfloat16
I32 = mybir.dt.int32
AF = mybir.ActivationFunctionType
OP = mybir.AluOpType
AXX = mybir.AxisListType.X

ENGINES = ("tensor", "vector", "scalar", "gpsimd", "sync")
ENG_SEM = {"tensor": "sT", "vector": "sV", "scalar": "sA", "gpsimd": "sG", "sync": "sQ"}
Q_SEM = {"gpsimd": "qG", "sync": "qS", "scalar": "qA", "vector": "qV"}

E = 2048
KCH = 16
TOK = 1024
TT = 512
NTT = TOK // TT
SEQ = 2048
NVH = 16
NKH = 8
C = 64
NCH = SEQ // C
DK = 128
DV = 128
NEXP = 8
HEXP = 1024
HSH = 2048
EPS_LN = 1e-5
EPS_RMS = 1e-6
EPS_L2 = 1e-6


class PB:

    def __init__(self, nc, es):
        self.nc = nc
        self.es = es
        self.stream = {e: [] for e in ENGINES}
        names = sorted(set(ENG_SEM.values()) | set(Q_SEM.values()))
        self.sem = {n: es.enter_context(nc.semaphore(n)) for n in names}
        self.cnt = {n: 0 for n in names}
        self.floor = {e: {n: 0 for n in names} for e in ENGINES}
        self.last_w = {}
        self.last_r = {}
        self.pending = {e: [] for e in ENGINES}
        self.n_ops = 0

    def _wait(self, eng, ev):
        if ev is None:
            return
        if isinstance(ev, tuple) and ev[0] == "PEND":
            if ev[1] == eng:
                return
            raise RuntimeError(f"cross-engine dependency on un-inc'd op from {ev[1]} -> {eng}")
        s, v = ev
        if self.floor[eng][s] >= v:
            return
        self.floor[eng][s] = v
        sem = self.sem[s]
        self.stream[eng].append(lambda h, sem=sem, v=v: h.wait_ge(sem, v))

    def _resolve_pending(self, eng, ev):
        for rec in self.pending[eng]:
            kind, key = rec
            if kind == "w":
                if isinstance(self.last_w.get(key), tuple) and self.last_w[key][0] == "PEND":
                    self.last_w[key] = ev
            else:
                lst = self.last_r.get(key, [])
                self.last_r[key] = [ev if (isinstance(x, tuple) and x[0] == "PEND") else x for x in lst]
        self.pending[eng] = []

    def op(self, eng, fname, *args, rd=(), wr=(), inc=True, sem=None, **kw):
        import traceback as _tb
        _site = "".join(_tb.format_stack(limit=4)[:-1])
        for k in rd:
            self._wait(eng, self.last_w.get(k))
        for k in wr:
            self._wait(eng, self.last_w.get(k))
            for ev in self.last_r.get(k, ()):
                self._wait(eng, ev)
        sname = sem or ENG_SEM[eng]
        if inc:
            self.cnt[sname] += 16 if sname.startswith("q") else 1
            ev = (sname, self.cnt[sname])
            semh = self.sem[sname]
            amt = 16 if sname.startswith("q") else 1
            def fn(h, fname=fname, args=args, kw=kw, semh=semh, amt=amt, _site=_site):
                try:
                    getattr(h, fname)(*args, **kw).then_inc(semh, amt)
                except Exception as e:
                    raise RuntimeError(f"emit fail {fname}: {e}\nEMIT SITE:\n{_site}")
        else:
            ev = ("PEND", eng)
            def fn(h, fname=fname, args=args, kw=kw, _site=_site):
                try:
                    getattr(h, fname)(*args, **kw)
                except Exception as e:
                    raise RuntimeError(f"emit fail {fname}: {e}\nEMIT SITE:\n{_site}")
        self.stream[eng].append(fn)
        self.n_ops += 1
        for k in wr:
            self.last_w[k] = ev
            self.last_r[k] = []
        for k in rd:
            self.last_r.setdefault(k, []).append(ev)
        if not inc:
            for k in wr:
                self.pending[eng].append(("w", k))
            for k in rd:
                self.pending[eng].append(("r", k))
        else:
            self._resolve_pending(eng, ev)
        return ev

    def barrier(self):
        snap = dict(self.cnt)
        for e in ENGINES:
            for s, v in snap.items():
                if v > self.floor[e][s]:
                    self.floor[e][s] = v
                    sem = self.sem[s]
                    self.stream[e].append(lambda h, sem=sem, v=v: h.wait_ge(sem, v))

    def transfer_key(self, old, new):
        if old in self.last_w:
            prev = self.last_w.get(new)
            self.last_w[new] = self.last_w[old]
        self.last_r.setdefault(new, []).extend(self.last_r.get(old, ()))

    def dma_sem(self, slot):
        name = f"q_{slot}"
        if name not in self.sem:
            self.sem[name] = self.es.enter_context(self.nc.semaphore(name))
            self.cnt[name] = 0
            for e in ENGINES:
                self.floor[e][name] = 0
        return name

    def dma(self, eng, out, in_, rd=(), wr=(), slot=None, **kw):
        slot = slot or (wr[0] if wr else "misc")
        return self.op(eng, "dma_start", rd=rd, wr=wr, sem=self.dma_sem(slot),
                       out=out, in_=in_, **kw)

    def finalize(self, block):
        for eng in ENGINES:
            dec = getattr(block, eng)
            stream = self.stream[eng]
            def make(stream):
                def body(h):
                    for fn in stream:
                        fn(h)
                return body
            dec(make(stream))


def sbuf(nc, es, name, shape, dtype):
    return es.enter_context(nc.sbuf_tensor(name, shape, dtype))


def psum(nc, es, name, shape, dtype=F32):
    return es.enter_context(nc.psum_tensor(name, shape, dtype))


def emit_ffn(p, nc, es, g, tag):
    x_res = g["x_res"]
    with contextlib.ExitStack() as ffn_es:
        S = lambda name, shape, dt: sbuf(nc, ffn_es, f"{tag}_{name}", shape, dt)
        P = lambda name, shape, dt=F32: psum(nc, ffn_es, f"{tag}_{name}", shape, dt)
        psA = P("psA", [128, 512]); psB = P("psB", [128, 512])
        psC = P("psC", [128, 512]); psD = P("psD", [128, 512])
        psE = P("psE", [128, 512]); psF = P("psF", [128, 512])
        psG_ = P("psG", [128, 512]); psT8 = P("psT8", [8, 512], BF16)
        xln = S("xln", [128, KCH * TT], BF16)
        xbf = S("xbf", [128, 2 * TT], BF16)
        xsq = S("xsq", [128, 2 * TT], BF16)
        strow = S("strow", [1, 2 * TT], F32)
        a_sb = S("a_sb", [128, TT], F32)
        b_sb = S("b_sb", [128, TT], F32)
        sc_sb = S("sc_sb", [128, 4 * 8], F32)
        r1 = S("r1", [128, 4 * 8], F32)
        r2 = S("r2", [128, 4 * 8], F32)
        rsc = S("rsc", [128, 4 * 6], F32)
        combb = S("combb", [128, 4 * 8], BF16)
        compR = S("compR", [8, TT], BF16)
        sgrow = S("sgrow", [1, TT], F32)
        cbc = S("cbc", [128, TT], BF16)
        sgbc = S("sgbc", [128, TT], BF16)
        hp = S("hp", [128, 64 * TT], BF16)
        hsh = S("hsh", [128, 16 * TT], BF16)
        s1g = S("s1g", [128, 2 * TT], BF16)
        htmp = S("htmp", [128, 2 * TT], F32)
        wring = S("wring", [128, 10240], BF16)
        wgu = wring
        ident_tr = g["ident_bf"]

        wg_d, wu_d, wd_d, shg_d, shu_d, shd_d, gw_d, seg_d = (
            g["wg"], g["wu"], g["wd"], g["shg"], g["shu"], g["shd"], g["gw"], g["seg"])

        for tt in range(NTT):
            t0 = tt * TT
            xk = lambda k: x_res[:, k * TOK + t0: k * TOK + t0 + TT]
            kx = f"{tag}xres"

            for k in range(KCH):
                r = k % 2
                xb_ap = xbf[:, r * TT:(r + 1) * TT]
                xs_ap = xsq[:, r * TT:(r + 1) * TT]
                p.op("vector", "tensor_copy", xb_ap, xk(k), rd=[kx], wr=[f"xbf{r}"])
                p.op("scalar", "activation", xs_ap, xk(k), AF.Square, rd=[kx], wr=[f"xsq{r}"])
                p.op("tensor", "matmul", psA[0:1, :], g["ones_bf"][0:128, 0:1], xb_ap,
                     start=(k == 0), stop=(k == KCH - 1),
                     rd=[f"xbf{r}", "ones_bf"], wr=["psA"])
                p.op("tensor", "matmul", psB[0:1, :], g["ones_bf"][0:128, 0:1], xs_ap,
                     start=(k == 0), stop=(k == KCH - 1),
                     rd=[f"xsq{r}", "ones_bf"], wr=["psB"])
            mean = strow[0:1, 0:TT]; e2 = strow[0:1, TT:2*TT]; var = e2
            rstd = e2; brow = mean
            p.op("vector", "tensor_scalar", mean, psA[0:1, :], 1.0 / E, None, OP.mult,
                 rd=["psA"], wr=["st_m"])
            p.op("vector", "tensor_scalar", e2, psB[0:1, :], 1.0 / E, None, OP.mult,
                 rd=["psB"], wr=["st_e2"])
            tmpv = sgrow[0:1, 0:TT]
            p.op("vector", "tensor_mul", tmpv, mean, mean, rd=["st_m"], wr=["sgA"])
            p.op("vector", "tensor_sub", var, e2, tmpv, rd=["st_e2", "sgA"], wr=["st_e2"])
            p.op("vector", "tensor_scalar", var, var, EPS_LN, None, OP.add, rd=["st_e2"], wr=["st_e2"])
            p.op("vector", "reciprocal", var, var, rd=["st_e2"], wr=["st_e2"])
            p.op("scalar", "activation", rstd, var, AF.Sqrt, rd=["st_e2"], wr=["st_e2"])
            p.op("vector", "scalar_tensor_tensor", brow, mean, -1.0, rstd, OP.mult, OP.mult,
                 rd=["st_m", "st_e2"], wr=["st_m"])
            p.op("tensor", "matmul", psC[:, :], g["ones_f1"][0:1, 0:128], rstd,
                 start=True, stop=True, rd=["st_e2", "ones_f1"], wr=["psC"])
            p.op("tensor", "matmul", psD[:, :], g["ones_f1"][0:1, 0:128], brow,
                 start=True, stop=True, rd=["st_m", "ones_f1"], wr=["psD"])
            p.op("vector", "tensor_copy", a_sb[:, :], psC[:, :], rd=["psC"], wr=["a_sb"])
            p.op("vector", "tensor_copy", b_sb[:, :], psD[:, :], rd=["psD"], wr=["b_sb"])
            for k in range(KCH):
                dst = xln[:, k * TT:(k + 1) * TT]
                r = k % 2
                tmp = htmp[:, r * TT:(r + 1) * TT]
                p.op("vector", "tensor_mul", tmp, xk(k), a_sb[:, :], rd=[kx, "a_sb"], wr=[f"htmp{r}"])
                p.op("vector", "tensor_add", dst, tmp, b_sb[:, :], rd=[f"htmp{r}", "b_sb"], wr=["xln"])

            for st in range(4):
                for k in range(KCH):
                    p.op("tensor", "matmul", psE[0:128, st * 8:(st + 1) * 8],
                         xln[:, k * TT + st * 128: k * TT + st * 128 + 128],
                         g["gw_sb"][:, k * 8:(k + 1) * 8],
                         start=(k == 0), stop=(k == KCH - 1),
                         rd=["xln", "gw_sb"], wr=["psE"], inc=(k == KCH - 1))
            p.op("vector", "tensor_copy", sc_sb[:, :], psE[:, 0:32], rd=["psE"], wr=["sc"])
            for st in range(4):
                sc = sc_sb[:, st * 8:(st + 1) * 8]
                eq1 = r1[:, st * 8:(st + 1) * 8]
                s2 = r2[:, st * 8:(st + 1) * 8]
                m1 = rsc[:, st * 6 + 0: st * 6 + 1]; m2 = rsc[:, st * 6 + 1: st * 6 + 2]
                ex = rsc[:, st * 6 + 2: st * 6 + 3]; w1 = rsc[:, st * 6 + 3: st * 6 + 4]
                w2 = rsc[:, st * 6 + 4: st * 6 + 5]; dd = rsc[:, st * 6 + 5: st * 6 + 6]
                p.op("vector", "reduce_max", m1, sc, AXX, rd=["sc"], wr=["m1"])
                p.op("vector", "tensor_scalar", eq1, sc, m1, None, OP.is_equal, rd=["sc", "m1"], wr=["eq1"])
                p.op("vector", "scalar_tensor_tensor", s2, eq1, -1e9, sc, OP.mult, OP.add,
                     rd=["eq1", "sc"], wr=["s2"])
                p.op("vector", "reduce_max", m2, s2, AXX, rd=["s2"], wr=["m2"])
                p.op("vector", "tensor_sub", dd, m2, m1, rd=["m2", "m1"], wr=["dd"])
                p.op("scalar", "activation", ex, dd, AF.Exp, rd=["dd"], wr=["ex"])
                p.op("vector", "tensor_scalar", w1, ex, 1.0, None, OP.add, rd=["ex"], wr=["w1"])
                p.op("vector", "reciprocal", w1, w1, rd=["w1"], wr=["w1"])
                p.op("vector", "tensor_mul", w2, ex, w1, rd=["ex", "w1"], wr=["w2"])
                p.op("vector", "tensor_scalar", s2, s2, m2, None, OP.is_equal, rd=["s2", "m2"], wr=["s2"])
                p.op("vector", "tensor_scalar", eq1, eq1, w1, None, OP.mult, rd=["eq1", "w1"], wr=["eq1"])
                p.op("vector", "scalar_tensor_tensor", eq1, s2, w2, eq1, OP.mult, OP.add,
                     rd=["s2", "w2", "eq1"], wr=["eq1"])
                p.op("vector", "tensor_copy", combb[:, st * 8:(st + 1) * 8], eq1, rd=["eq1"], wr=["combb"])
                p.op("tensor", "transpose", psT8[0:8, st * 128:(st + 1) * 128],
                     combb[:, st * 8:(st + 1) * 8], ident_tr[:, :],
                     rd=["combb", "ident_bf"], wr=["psT8"])
            p.op("vector", "tensor_copy", compR[:, :], psT8[0:8, 0:TT], rd=["psT8"], wr=["compR"])

            for k in range(KCH):
                p.op("tensor", "matmul", psF[0:1, 0:TT], g["seg_sb"][:, k:k+1],
                     xln[:, k * TT:(k + 1) * TT],
                     start=(k == 0), stop=(k == KCH - 1),
                     rd=["xln", "seg_sb"], wr=["psF"], inc=(k == KCH - 1))
            sgA = sgrow[0:1, 0:TT]
            p.op("scalar", "activation", sgA, psF[0:1, 0:TT], AF.Exp, scale=g["cneg1"][0:1, 0:1],
                 rd=["psF"], wr=["sgA"])
            p.op("vector", "tensor_scalar", sgA, sgA, 1.0, None, OP.add, rd=["sgA"], wr=["sgA"])
            p.op("vector", "reciprocal", sgA, sgA, rd=["sgA"], wr=["sgA"])
            p.op("tensor", "matmul", psF[:, :], g["ones_f1"][0:1, 0:128], sgA,
                 start=True, stop=True, rd=["sgA", "ones_f1"], wr=["psF"])
            p.op("vector", "tensor_copy", sgbc[:, :], psF[:, :], rd=["psF"], wr=["sgbc"])

            def wgu_load(e, h, r):
                base = r * 2 * KCH * 128
                dst_g = wgu[:, base: base + KCH * 128]
                dst_u = wgu[:, base + KCH * 128: base + 2 * KCH * 128]
                in_g = wg_d[e].rearrange("(k p) h -> p k h", p=128)[:, :, h * 128:(h + 1) * 128]
                in_u = wu_d[e].rearrange("(k p) h -> p k h", p=128)[:, :, h * 128:(h + 1) * 128]
                p.dma("gpsimd", dst_g.rearrange("p (k h) -> p k h", k=KCH), in_g, wr=[f"wgu{r}"])
                p.dma("gpsimd", dst_u.rearrange("p (k h) -> p k h", k=KCH), in_u, wr=[f"wgu{r}"])

            pairs = [(e, h) for e in range(NEXP) for h in range(8)]
            wgu_load(*pairs[0], 0)
            wgu_load(*pairs[1], 1)
            for i, (e, h) in enumerate(pairs):
                r = i % 2
                base = r * 2 * KCH * 128
                pa, pb = (psA, psB) if r == 0 else (psE, psF)
                ka, kb2 = ("psA", "psB") if r == 0 else ("psE", "psF")
                if h == 0:
                    p.op("tensor", "matmul", psG_[:, :], g["oh8_sb"][0:8, e*128:(e+1)*128],
                         compR[0:8, :], start=True, stop=True, rd=["compR", "oh8_sb"], wr=["psG"])
                    p.op("vector", "tensor_copy", cbc[:, :], psG_[:, :], rd=["psG"], wr=["cbc"])
                for k in range(KCH):
                    p.op("tensor", "matmul", pa[:, :], wgu[:, base + k * 128: base + (k+1) * 128],
                         xln[:, k * TT:(k + 1) * TT], start=(k == 0), stop=(k == KCH - 1),
                         rd=[f"wgu{r}", "xln"], wr=[ka], inc=(k == KCH - 1))
                for k in range(KCH):
                    p.op("tensor", "matmul", pb[:, :], wgu[:, base + KCH*128 + k * 128: base + KCH*128 + (k+1) * 128],
                         xln[:, k * TT:(k + 1) * TT], start=(k == 0), stop=(k == KCH - 1),
                         rd=[f"wgu{r}", "xln"], wr=[kb2], inc=(k == KCH - 1))
                s1 = s1g[:, r * TT:(r + 1) * TT]
                tmp = htmp[:, r * TT:(r + 1) * TT]
                p.op("scalar", "activation", s1, pa[:, :], AF.Silu, rd=[ka], wr=[f"s1{r}"])
                p.op("vector", "tensor_mul", tmp, pb[:, :], s1, rd=[kb2, f"s1{r}"], wr=[f"htmp{r}"])
                hslot = hp[:, (e * 8 + h) * TT: (e * 8 + h + 1) * TT]
                p.op("vector", "tensor_mul", hslot, tmp, cbc[:, :], rd=[f"htmp{r}", "cbc"], wr=["hp"])
                if i + 2 < len(pairs):
                    wgu_load(*pairs[i + 2], r)

            def wsh_load(hs, r):
                base = r * 2 * KCH * 128
                dst_g = wgu[:, base: base + KCH * 128]
                dst_u = wgu[:, base + KCH * 128: base + 2 * KCH * 128]
                in_g = shg_d.rearrange("(k p) h -> p k h", p=128)[:, :, hs * 128:(hs + 1) * 128]
                in_u = shu_d.rearrange("(k p) h -> p k h", p=128)[:, :, hs * 128:(hs + 1) * 128]
                p.dma("gpsimd", dst_g.rearrange("p (k h) -> p k h", k=KCH), in_g, wr=[f"wgu{r}"])
                p.dma("gpsimd", dst_u.rearrange("p (k h) -> p k h", k=KCH), in_u, wr=[f"wgu{r}"])
            wsh_load(0, 0); wsh_load(1, 1)
            for hs in range(16):
                r = hs % 2
                base = r * 2 * KCH * 128
                pa, pb = (psA, psB) if r == 0 else (psE, psF)
                ka, kb2 = ("psA", "psB") if r == 0 else ("psE", "psF")
                for k in range(KCH):
                    p.op("tensor", "matmul", pa[:, :], wgu[:, base + k * 128: base + (k+1) * 128],
                         xln[:, k * TT:(k + 1) * TT], start=(k == 0), stop=(k == KCH - 1),
                         rd=[f"wgu{r}", "xln"], wr=[ka], inc=(k == KCH - 1))
                for k in range(KCH):
                    p.op("tensor", "matmul", pb[:, :], wgu[:, base + KCH*128 + k * 128: base + KCH*128 + (k+1) * 128],
                         xln[:, k * TT:(k + 1) * TT], start=(k == 0), stop=(k == KCH - 1),
                         rd=[f"wgu{r}", "xln"], wr=[kb2], inc=(k == KCH - 1))
                s1 = s1g[:, r * TT:(r + 1) * TT]
                tmp = htmp[:, r * TT:(r + 1) * TT]
                p.op("scalar", "activation", s1, pa[:, :], AF.Silu, rd=[ka], wr=[f"s1{r}"])
                p.op("vector", "tensor_mul", tmp, pb[:, :], s1, rd=[kb2, f"s1{r}"], wr=[f"htmp{r}"])
                p.op("vector", "tensor_mul", hsh[:, hs * TT:(hs + 1) * TT], tmp, sgbc[:, :],
                     rd=[f"htmp{r}", "sgbc"], wr=["hsh"])
                if hs + 2 < 16:
                    wsh_load(hs + 2, r)

            for j in range(6):
                p.transfer_key("wgu0", f"wde{j}")
                p.transfer_key("wgu1", f"wde{j}")
            for r in range(2):
                p.transfer_key("wgu0", f"wshd{r}")
                p.transfer_key("wgu1", f"wshd{r}")

            def wde_load(m, e, j):
                dst = wring[:, j * 1024:(j + 1) * 1024]
                in_t = wd_d[e].rearrange("(hk p) m -> p hk m", p=128)[:, :, m * 128:(m + 1) * 128]
                p.dma("gpsimd", dst.rearrange("p (hk m) -> p hk m", hk=8), in_t, wr=[f"wde{j}"])

            def wshd_load(m, r):
                dst = wring[:, 6144 + r * 2048: 6144 + (r + 1) * 2048]
                inS = shd_d.rearrange("(hs p) m -> p hs m", p=128)[:, :, m * 128:(m + 1) * 128]
                p.dma("gpsimd", dst.rearrange("p (hs m) -> p hs m", hs=16), inS, wr=[f"wshd{r}"])

            me_list = [(m, e) for m in range(KCH) for e in range(NEXP)]
            for jj in range(6):
                wde_load(*me_list[jj], jj)
            wshd_load(0, 0); wshd_load(1, 1)
            for m in range(KCH):
                r = m % 2
                pd = psC if (m % 2 == 0) else psD
                pk = "psC" if (m % 2 == 0) else "psD"
                n_mm = NEXP * 8 + 16
                i_mm = 0
                for e in range(NEXP):
                    j = (m * NEXP + e) % 6
                    for hk in range(8):
                        p.op("tensor", "matmul", pd[:, :],
                             wring[:, j * 1024 + hk * 128: j * 1024 + (hk + 1) * 128],
                             hp[:, (e * 8 + hk) * TT: (e * 8 + hk + 1) * TT],
                             start=(i_mm == 0), stop=(i_mm == n_mm - 1),
                             rd=[f"wde{j}", "hp"], wr=[pk], inc=(hk == 7))
                        i_mm += 1
                    nxt = m * NEXP + e + 6
                    if nxt < len(me_list):
                        wde_load(*me_list[nxt], nxt % 6)
                for hs in range(16):
                    p.op("tensor", "matmul", pd[:, :],
                         wring[:, 6144 + r * 2048 + hs * 128: 6144 + r * 2048 + (hs + 1) * 128],
                         hsh[:, hs * TT: (hs + 1) * TT],
                         start=(i_mm == 0), stop=(i_mm == n_mm - 1),
                         rd=[f"wshd{r}", "hsh"], wr=[pk], inc=(i_mm == n_mm - 1))
                    i_mm += 1
                if m + 2 < KCH:
                    wshd_load(m + 2, r)
                dst = x_res[:, m * TOK + t0: m * TOK + t0 + TT]
                p.op("vector", "scalar_tensor_tensor", dst, pd[:, :], 0.5, dst, OP.mult, OP.add,
                     rd=[pk, kx], wr=[kx])
            for j in range(6):
                p.transfer_key(f"wde{j}", "wgu0"); p.transfer_key(f"wde{j}", "wgu1")
            for r in range(2):
                p.transfer_key(f"wshd{r}", "wgu0"); p.transfer_key(f"wshd{r}", "wgu1")



def emit_ln_to(p, nc, es, g, x_res, out_cb, tag):
    with contextlib.ExitStack() as les:
        S = lambda name, shape, dt: sbuf(nc, les, f"{tag}_{name}", shape, dt)
        P = lambda name, shape, dt=F32: psum(nc, les, f"{tag}_{name}", shape, dt)
        psA = P("psA", [128, 512]); psB = P("psB", [128, 512])
        psC = P("psC", [128, 512]); psD = P("psD", [128, 512])
        xbf = S("xbf", [128, 2 * TT], BF16)
        xsq = S("xsq", [128, 2 * TT], BF16)
        ftmp = S("ftmp", [128, 2 * TT], F32)
        strow = S("strow", [1, 2 * TT], F32)
        srow2 = S("srow2", [1, TT], F32)
        a_sb = S("a_sb", [128, TT], F32)
        b_sb = S("b_sb", [128, TT], F32)
        xln = S("xln", [128, KCH * TT], BF16)
        kx = f"{tag}xres"
        for tt in range(NTT):
            t0 = tt * TT
            xk = lambda k: x_res[:, k * TOK + t0: k * TOK + t0 + TT]
            for k in range(KCH):
                r = k % 2
                xb_ap = xbf[:, r * TT:(r + 1) * TT]
                xs_ap = xsq[:, r * TT:(r + 1) * TT]
                p.op("vector", "tensor_copy", xb_ap, xk(k), rd=[kx], wr=[f"{tag}xbf{r}"])
                p.op("scalar", "activation", xs_ap, xk(k), AF.Square, rd=[kx], wr=[f"{tag}xsq{r}"])
                p.op("tensor", "matmul", psA[0:1, :], g["ones_bf"][0:128, 0:1], xb_ap,
                     start=(k == 0), stop=(k == KCH - 1), rd=[f"{tag}xbf{r}", "ones_bf"], wr=[f"{tag}psA"])
                p.op("tensor", "matmul", psB[0:1, :], g["ones_bf"][0:128, 0:1], xs_ap,
                     start=(k == 0), stop=(k == KCH - 1), rd=[f"{tag}xsq{r}", "ones_bf"], wr=[f"{tag}psB"])
            mean = strow[0:1, 0:TT]; e2 = strow[0:1, TT:2*TT]
            p.op("vector", "tensor_scalar", mean, psA[0:1, :], 1.0 / E, None, OP.mult,
                 rd=[f"{tag}psA"], wr=[f"{tag}m"])
            p.op("vector", "tensor_scalar", e2, psB[0:1, :], 1.0 / E, None, OP.mult,
                 rd=[f"{tag}psB"], wr=[f"{tag}e"])
            p.op("vector", "tensor_mul", srow2[0:1, :], mean, mean, rd=[f"{tag}m"], wr=[f"{tag}s2"])
            p.op("vector", "tensor_sub", e2, e2, srow2[0:1, :], rd=[f"{tag}e", f"{tag}s2"], wr=[f"{tag}e"])
            p.op("vector", "tensor_scalar", e2, e2, EPS_LN, None, OP.add, rd=[f"{tag}e"], wr=[f"{tag}e"])
            p.op("vector", "reciprocal", e2, e2, rd=[f"{tag}e"], wr=[f"{tag}e"])
            p.op("scalar", "activation", e2, e2, AF.Sqrt, rd=[f"{tag}e"], wr=[f"{tag}e"])
            p.op("vector", "scalar_tensor_tensor", mean, mean, -1.0, e2, OP.mult, OP.mult,
                 rd=[f"{tag}m", f"{tag}e"], wr=[f"{tag}m"])
            p.op("tensor", "matmul", psC[:, :], g["ones_f1"][0:1, 0:128], e2,
                 start=True, stop=True, rd=[f"{tag}e", "ones_f1"], wr=[f"{tag}psC"])
            p.op("tensor", "matmul", psD[:, :], g["ones_f1"][0:1, 0:128], mean,
                 start=True, stop=True, rd=[f"{tag}m", "ones_f1"], wr=[f"{tag}psD"])
            p.op("vector", "tensor_copy", a_sb[:, :], psC[:, :], rd=[f"{tag}psC"], wr=[f"{tag}a"])
            p.op("vector", "tensor_copy", b_sb[:, :], psD[:, :], rd=[f"{tag}psD"], wr=[f"{tag}b"])
            for k in range(KCH):
                dst = xln[:, k * TT:(k + 1) * TT]
                r = k % 2
                tmp = ftmp[:, r * TT:(r + 1) * TT]
                p.op("vector", "tensor_mul", tmp, xk(k), a_sb[:, :], rd=[kx, f"{tag}a"], wr=[f"{tag}ft{r}"])
                p.op("vector", "tensor_add", dst, tmp, b_sb[:, :], rd=[f"{tag}ft{r}", f"{tag}b"], wr=[f"{tag}xln"])
            out_cb(tt, xln, f"{tag}xln")


def emit_attention(p, nc, es, g, d):
    SQD = 1.0 / (DK ** 0.5)
    GRP = 8
    NGRP = NCH // GRP
    ag1_in, ag1_out = d["ag1_in"], d["ag1_out"]
    ag2_in, ag2_out = d["ag2_in"], d["ag2_out"]
    gated_dram = d["gated_dram"]

    p.op("gpsimd", "collective_compute", "AllGather", OP.bypass,
         rd=["ag1_in"], wr=["ag1_out"],
         replica_groups=[list(range(8))], ins=[ag1_in.ap()], outs=[ag1_out.ap()])

    S = lambda name, shape, dt: sbuf(nc, es, f"at_{name}", shape, dt)
    P = lambda name, shape, dt=F32: psum(nc, es, f"at_{name}", shape, dt)

    xln_seq = S("xlnseq", [128, KCH * SEQ], BF16)
    offs = g["offs_sb"]
    for half in range(2):
        for k in range(KCH):
            p.op("gpsimd", "indirect_dma_start",
                 rd=["ag1_out", "offs_sb"], wr=["xlnseq"], sem=p.dma_sem("xlnseq"),
                 out=xln_seq[:, k * SEQ + half * TOK: k * SEQ + (half + 1) * TOK],
                 out_offset=None,
                 in_=ag1_out.ap().rearrange("(p a) f -> p (a f)", p=128),
                 in_offset=IndirectOffsetOnAxis(offs[:, half * KCH + k: half * KCH + k + 1], 1))

    psU = P("psU", [128, 512]); psW1 = P("psW1", [128, 512])
    psQ = P("psQ", [128, 512]); psO = [P("psO0", [128, 512]), P("psO1", [128, 512])]
    psX = P("psX", [128, 512])
    psS = psX[:, 0:128]
    psT32v = psX[:, 256:320]
    psVt = P("psVt", [64, 128])
    psV = psVt[:, :]
    psTb = P("psTb", [128, 128], BF16)
    psR = psU[0:64, :]

    if ACUT <= 1:
        return None
    gam = S("gam", [64, SEQ], F32)

    egr = S("egr", [64, SEQ], BF16)
    kwr = S("kwr", [64, SEQ], BF16)
    gam_tm = S("gamtm", [128, 16 * KCH], F32)
    kw_tm_f = S("kwtmf", [128, 16 * KCH], F32)
    eglb = S("eglb", [128, NVH * NCH], F32)
    glrow = S("glrow", [64, NCH], F32)
    p.op("vector", "memset", gam[:, :], 0.0, wr=["gam"])
    p.op("vector", "memset", egr[:, :], 0.0, wr=["egr"])
    p.op("vector", "memset", kwr[:, :], 0.0, wr=["kwr"])
    p.op("vector", "memset", glrow[:, :], 0.0, wr=["glrow"])
    row_es = contextlib.ExitStack()
    bav = sbuf(nc, row_es, "at_bav", [64, SEQ], F32)
    wba_sb = sbuf(nc, row_es, "at_wba", [128, KCH * 64], BF16)
    p.dma("gpsimd", wba_sb.ap().rearrange("p (k c) -> p k c", k=KCH),
          d["wba"].rearrange("(k p) c -> p k c", p=128), wr=["wba_sb"], slot="wba")
    for tt in range(4):
        for k in range(KCH):
            p.op("tensor", "matmul", psR[:, :], wba_sb[:, k * 64:(k + 1) * 64],
                 xln_seq[:, k * SEQ + tt * 512: k * SEQ + (tt + 1) * 512],
                 start=(k == 0), stop=(k == KCH - 1),
                 rd=["wba_sb", "xlnseq"], wr=["at_psU"])
        p.op("vector", "tensor_copy", bav[:, tt * 512:(tt + 1) * 512], psR[:, :],
             rd=["at_psU"], wr=["bav"])
    brows = bav[0:16, :]; arows = bav[32:48, :]
    grows = bav[32:48, :]
    p.op("scalar", "activation", brows, brows, AF.Exp, scale=g["cneg1"][0:16, 0:1],
         rd=["bav"], wr=["bav"])
    p.op("vector", "tensor_scalar", brows, brows, 1.0, None, OP.add, rd=["bav"], wr=["bav"])
    p.op("vector", "reciprocal", brows, brows, rd=["bav"], wr=["bav"])
    spa = gam[32:48, :]
    p.op("vector", "tensor_scalar", arows, arows, g["dt_col"][32:48, 0:1], None, OP.add,
         rd=["bav", "dt_col"], wr=["bav"])
    p.op("scalar", "activation", spa, arows, AF.Abs, rd=["bav"], wr=["gam"])
    p.op("scalar", "activation", spa, spa, AF.Exp, scale=g["cneg1"][32:48, 0:1],
         rd=["gam"], wr=["gam"])
    p.op("scalar", "activation", spa, spa, AF.Ln, bias=g["cone"][32:48, 0:1],
         rd=["gam"], wr=["gam"])
    p.op("vector", "tensor_scalar_max", arows, arows, 0.0, rd=["bav"], wr=["bav"])
    p.op("vector", "tensor_add", arows, arows, spa, rd=["bav", "gam"], wr=["bav"])
    nea = sbuf(nc, row_es, "at_nea", [64, 1], F32)
    p.op("scalar", "activation", nea[32:48, :], g["alog_col"][32:48, 0:1], AF.Exp,
         rd=["alog_col"], wr=["nea"])
    p.op("vector", "tensor_scalar", nea[32:48, :], nea[32:48, :], -1.0, None, OP.mult,
         rd=["nea"], wr=["nea"])
    p.op("vector", "tensor_scalar", arows, arows, nea[32:48, 0:1], None, OP.mult,
         rd=["bav", "nea"], wr=["bav"])
    p.op("scalar", "activation", egr[32:48, :], grows, AF.Exp, rd=["bav"], wr=["egr"])
    p.op("vector", "tensor_copy", egr[0:16, :], brows, rd=["bav"], wr=["egr"])
    p.op("vector", "tensor_copy", glrow[32:48, :],
         bav[32:48, :].rearrange("p (c t) -> p c t", t=C)[:, :, C-1:C].rearrange("p c t -> p (c t)"),
         rd=["bav"], wr=["glrow"])
    p.op("vector", "tensor_copy", gam[32:48, :].rearrange("p (c t) -> p c t", t=C),
         bav[32:48, :].rearrange("p (c t) -> p c t", t=C)[:, :, C-1:C].broadcast_to((16, NCH, C)),
         rd=["bav"], wr=["gam"])
    p.op("vector", "tensor_sub", gam[32:48, :], gam[32:48, :], grows,
         rd=["gam", "bav"], wr=["gam"])
    p.op("scalar", "activation", kwr[32:48, :], gam[32:48, :], AF.Exp, rd=["gam"], wr=["kwr"])
    for c in range(NCH):
        p.op("vector", "tensor_tensor_scan", gam[32:48, c * C:(c + 1) * C],
             grows[:, c * C:(c + 1) * C], grows[:, c * C:(c + 1) * C], 0.0, OP.add, OP.bypass,
             rd=["bav"], wr=["gam"])
    psT32 = psT32v
    for i in range(KCH):
        p.op("tensor", "transpose", psT32[:, 0:64], gam[0:64, i * 128:(i + 1) * 128],
             g["ident_f1"][0:64, 0:64], rd=["gam", "ident_f1"], wr=["psT32"])
        p.op("vector", "tensor_copy", gam_tm[:, i * 16:(i + 1) * 16], psT32[:, 32:48],
             rd=["psT32"], wr=["gamtm"])
        p.op("tensor", "transpose", psTb[:, 0:64], kwr[0:64, i * 128:(i + 1) * 128],
             g["ident_bf"][0:64, 0:64], rd=["kwr", "ident_bf"], wr=["at_psTb"])
        p.op("vector", "tensor_copy", kw_tm_f[:, i * 16:(i + 1) * 16], psTb[:, 32:48],
             rd=["at_psTb"], wr=["kwtmf"])
    for vh in range(NVH):
        p.op("tensor", "matmul", psU[:, 0:NCH], g["ohg_f32"][32:64, vh * 128:(vh + 1) * 128],
             glrow[32:64, :], start=True, stop=True, rd=["glrow", "ohg_f32"], wr=["at_psU"])
        p.op("scalar", "activation", eglb[:, vh * NCH:(vh + 1) * NCH], psU[:, 0:NCH], AF.Exp,
             rd=["at_psU"], wr=["eglb"])
    row_es.close()
    p.barrier()
    return dict(xln_seq=xln_seq, bav=None, gam=gam, egr=egr, kwr=kwr,
                gam_tm=gam_tm, kw_tm_f=kw_tm_f, eglb=eglb,
                psU=psU, psW1=psW1, psQ=psQ, psO=psO, psS=psS, psV=psV, psTb=psTb)


def emit_subgroup(p, nc, es, g, d, ctx, sg):
    SQD = 1.0 / (DK ** 0.5)
    GRP = 8
    NGRP = NCH // GRP
    xln_seq = ctx["xln_seq"]; gam = ctx["gam"]; egr = ctx["egr"]
    gam_tm = ctx["gam_tm"]; kw_tm_f = ctx["kw_tm_f"]; eglb = ctx["eglb"]
    psU = ctx["psU"]; psW1 = ctx["psW1"]; psQ = ctx["psQ"]; psO = ctx["psO"]
    psS = ctx["psS"]; psV = ctx["psV"]; psTb = ctx["psTb"]
    vh0, vh1 = 2 * sg, 2 * sg + 1

    with contextlib.ExitStack() as ses:
        S = lambda name, shape, dt: sbuf(nc, ses, f"sg{sg}_{name}", shape, dt)
        raw = S("raw", [128, (3 + SEQ)], BF16)
        qc = S("qc", [128, SEQ], BF16)
        kc = S("kc", [128, SEQ], BF16)
        vc = S("vc", [128, 2 * SEQ], BF16)
        zbnc = S("zbnc", [128, 512], BF16)
        facc = S("facc", [128, 512], F32)
        wring = S("wring", [128, 2 * KCH * 128], BF16)
        rsr = S("rsr", [64, SEQ], F32)
        rsrb = S("rsrb", [64, SEQ], BF16)
        sqt = S("sqt", [128, 512], BF16)
        ktl = S("ktl", [128, 512], BF16)
        qtl = S("qtl", [128, 512], BF16)
        kbA = S("kbA", [128, 2 * 512], BF16)
        kbg = S("kbg", [128, 2 * 512], BF16)
        vbf = S("vbf", [128, 2 * 512], BF16)
        qtg = S("qtg", [128, 2 * 512], BF16)
        scb = S("scb", [128, 6 * 512], BF16)
        args = S("args", [64, 2 * 512], F32)
        dinc = S("dinc", [64, 2 * 512], BF16)
        dincT = S("dincT", [64, 2 * 512], BF16)
        wA = S("wA", [64, 2 * 512], BF16)
        wR = S("wR", [64, 2 * 512], BF16)
        powA = S("powA", [128, 5 * GRP * 64], BF16)
        powR = S("powR", [128, 6 * GRP * 64], BF16)
        aTt = S("aTt", [128, GRP * 64], BF16)
        vbtm = S("vbtm", [128, GRP * 128], BF16)
        kwtm = S("kwtm", [128, GRP * 128], BF16)
        Sst = S("Sst", [128, 2 * 128], F32)
        Sbf = S("Sbf", [128, 2 * 128], BF16)
        Zb = S("Zb", [128, 2 * 128], BF16)
        Yb = S("Yb", [128, 2 * 128], BF16)
        vnw = S("vnw", [128, 2 * 128], BF16)
        gbc = S("gbc", [128, 512], BF16)
        grs = facc[0:1, 0:512]
        gout = zbnc
        gf = sqt
        siz = S("siz", [128, 512], BF16)

        KEY = lambda n: f"sg{sg}_{n}"

        cols = [("q", sg, qc, 0), ("k", 8 + sg, kc, 1),
                ("v0", 16 + 2 * sg, vc[:, 0:SEQ], 2), ("v1", 17 + 2 * sg, vc[:, SEQ:2*SEQ], 3),
                ("z0", 32 + 2 * sg, d["z_dram"][:, 0:SEQ], None), ("z1", 33 + 2 * sg, d["z_dram"][:, SEQ:2*SEQ], None)]

        def wq_load(cc, r):
            dst = wring[:, r * KCH * 128:(r + 1) * KCH * 128]
            in_t = d["wqkvz"].rearrange("(k p) c -> p k c", p=128)[:, :, cc * 128:(cc + 1) * 128]
            p.dma("gpsimd", dst.rearrange("p (k c) -> p k c", k=KCH), in_t, wr=[KEY(f"wq{r}")], slot=f"wq{r}")

        p.op("vector", "memset", raw[:, 0:3], 0.0, wr=[KEY("raw0"), KEY("raw1")])
        wq_load(cols[0][1], 0)
        wq_load(cols[1][1], 1)
        for ci, (nm, cc, dstbuf, convrow) in enumerate(cols):
            r = ci % 2
            rawslot = raw[:, 0:(3 + SEQ)]
            for tt in range(4):
                for k in range(KCH):
                    p.op("tensor", "matmul", psQ[:, :],
                         wring[:, r * KCH * 128 + k * 128: r * KCH * 128 + (k + 1) * 128],
                         xln_seq[:, k * SEQ + tt * 512: k * SEQ + (tt + 1) * 512],
                         start=(k == 0), stop=(k == KCH - 1),
                         rd=[KEY(f"wq{r}"), "xlnseq"], wr=["at_psQ"], inc=(k == KCH - 1))
                if convrow is None:
                    p.op("vector", "tensor_copy", zbnc[:, :], psQ[:, :],
                         rd=["at_psQ"], wr=[KEY("zbnc")])
                    p.dma("sync", dstbuf[:, tt * 512:(tt + 1) * 512], zbnc[:, :],
                          rd=[KEY("zbnc")], wr=[KEY("zdram")], slot="zdram")
                else:
                    p.op("vector", "tensor_copy", rawslot[:, 3 + tt * 512: 3 + (tt + 1) * 512], psQ[:, :],
                         rd=["at_psQ"], wr=[KEY(f"raw{r}")])
            if ci + 2 < len(cols):
                wq_load(cols[ci + 2][1], r)
            if convrow is not None:
                wcol = lambda j: g["convw_sb"][:, cc * 4 + j: cc * 4 + j + 1]
                bcol = g["convb_sb"][:, cc: cc + 1]
                for tt in range(4):
                    a0 = facc[:, 0:512]
                    base = tt * 512
                    p.op("vector", "tensor_scalar", a0, rawslot[:, base: base + 512],
                         wcol(0), None, OP.mult, rd=[KEY(f"raw{r}"), "convw_sb"], wr=[KEY("fa0")])
                    for j in range(1, 4):
                        p.op("vector", "scalar_tensor_tensor", a0, rawslot[:, base + j: base + j + 512],
                             wcol(j), a0, OP.mult, OP.add,
                             rd=[KEY(f"raw{r}"), "convw_sb", KEY("fa0")], wr=[KEY("fa0")])
                    p.op("scalar", "activation", dstbuf[:, base: base + 512], a0, AF.Silu,
                         bias=bcol, rd=[KEY("fa0"), "convb_sb"], wr=[KEY(nm)])

        if SCUT <= 1:
            return
        p.op("vector", "memset", rsr[:, :], 1.0, wr=[KEY("rsr")])
        p.op("vector", "memset", rsrb[:, :], 0.0, wr=[KEY("rsrb")])
        for qi, (buf, key, row) in enumerate([(qc, KEY("q"), 0), (kc, KEY("k"), 32)]):
            for tt in range(4):
                p.op("scalar", "activation", sqt[:, :], buf[:, tt * 512:(tt + 1) * 512], AF.Square,
                     rd=[key], wr=[KEY("sqt")])
                p.op("tensor", "matmul", psU[0:1, 0:512], g["ones_bf"][0:128, 0:1], sqt[:, :],
                     start=True, stop=True, rd=[KEY("sqt"), "ones_bf"], wr=["at_psU"])
                p.op("vector", "tensor_copy", rsr[row:row+1, tt * 512:(tt + 1) * 512], psU[0:1, 0:512],
                     rd=["at_psU"], wr=[KEY("rsr")])
        p.op("vector", "tensor_scalar", rsr[0:1, :], rsr[0:1, :], float(DK), float(DK) * EPS_L2,
             OP.mult, OP.add, rd=[KEY("rsr")], wr=[KEY("rsr")])
        p.op("vector", "tensor_scalar", rsr[32:33, :], rsr[32:33, :], 1.0, EPS_L2,
             OP.mult, OP.add, rd=[KEY("rsr")], wr=[KEY("rsr")])
        p.op("vector", "reciprocal", rsr[0:1, :], rsr[0:1, :], rd=[KEY("rsr")], wr=[KEY("rsr")])
        p.op("vector", "reciprocal", rsr[32:33, :], rsr[32:33, :], rd=[KEY("rsr")], wr=[KEY("rsr")])
        p.op("scalar", "activation", rsr[0:1, :], rsr[0:1, :], AF.Sqrt, rd=[KEY("rsr")], wr=[KEY("rsr")])
        p.op("scalar", "activation", rsr[32:33, :], rsr[32:33, :], AF.Sqrt, rd=[KEY("rsr")], wr=[KEY("rsr")])
        p.op("vector", "tensor_copy", rsrb[0:1, :], rsr[0:1, :], rd=[KEY("rsr")], wr=[KEY("rsrb")])
        p.op("vector", "tensor_copy", rsrb[32:33, :], rsr[32:33, :], rd=[KEY("rsr")], wr=[KEY("rsrb")])

        if SCUT <= 2:
            return
        p.op("vector", "memset", Sst[:, :], 0.0, wr=[KEY("Sst")])
        p.op("vector", "memset", Sbf[:, :], 0.0, wr=[KEY("Sbf")])

        for grp in range(NGRP):
            g0 = grp * 512
            bc_specs = [(rsrb[0:32, g0:g0+512], 0, 0, "rsr2"),
                        (rsrb[32:64, g0:g0+512], 0, 1, "rsr2"),
                        (egr[0:32, g0:g0+512], vh0, 2, "egr32"),
                        (egr[0:32, g0:g0+512], vh1, 3, "egr32"),
                        (egr[32:64, g0:g0+512], vh0, 4, "egr32"),
                        (egr[32:64, g0:g0+512], vh1, 5, "egr32")]
            for srcap, row, slot, kind in bc_specs:
                if kind == "rsr2":
                    lhs = g["oh32_sb"][(0 if slot == 0 else 32):(32 if slot == 0 else 64), 0:128]
                    rdk = [KEY("rsrb"), "oh32_sb"]
                else:
                    base = 0 if slot in (2, 3) else 32
                    lhs = g["oh32_sb"][base:base + 32, row * 128:(row + 1) * 128]
                    rdk = ["egr", "oh32_sb"]
                p.op("tensor", "matmul", psU[:, :], lhs, srcap, start=True, stop=True,
                     rd=rdk, wr=["at_psU"])
                p.op("vector", "tensor_copy", scb[:, slot * 512:(slot + 1) * 512], psU[:, :],
                     rd=["at_psU"], wr=[KEY(f"scb{slot}")])
            p.op("vector", "tensor_mul", ktl[:, :], kc[:, g0:g0+512], scb[:, 512:1024],
                 rd=[KEY("k"), KEY("scb1")], wr=[KEY("ktl")])
            p.op("vector", "tensor_mul", qtl[:, :], qc[:, g0:g0+512], scb[:, 0:512],
                 rd=[KEY("q"), KEY("scb0")], wr=[KEY("qtl")])
            for vi, vh in enumerate((vh0, vh1)):
                dst = kbA[:, vi * 512:(vi + 1) * 512]
                p.op("vector", "tensor_mul", dst, ktl[:, :], scb[:, (2 + vi) * 512:(3 + vi) * 512],
                     rd=[KEY("ktl"), KEY(f"scb{2+vi}")], wr=[KEY("kbA")])
                p.op("vector", "tensor_mul", kbg[:, vi * 512:(vi + 1) * 512], dst,
                     scb[:, (4 + vi) * 512:(5 + vi) * 512],
                     rd=[KEY("kbA"), KEY(f"scb{4+vi}")], wr=[KEY("kbg")])
                p.op("vector", "tensor_mul", vbf[:, vi * 512:(vi + 1) * 512],
                     vc[:, vi * SEQ + g0: vi * SEQ + g0 + 512],
                     scb[:, (2 + vi) * 512:(3 + vi) * 512],
                     rd=[KEY(f"v{vi}"), KEY(f"scb{2+vi}")], wr=[KEY("vbf")])
                p.op("vector", "tensor_mul", qtg[:, vi * 512:(vi + 1) * 512], qtl[:, :],
                     scb[:, (4 + vi) * 512:(5 + vi) * 512],
                     rd=[KEY("qtl"), KEY(f"scb{4+vi}")], wr=[KEY("qtg")])
            if SCUT == 31:
                continue
            for vi, vh in enumerate((vh0, vh1)):
                p.op("tensor", "matmul", psU[0:64, :], g["ohg_f32"][32:64, vh * 128: vh * 128 + 64],
                     gam[32:64, g0:g0+512], start=True, stop=True,
                     rd=["gam", "ohg_f32"], wr=["at_psU"])
                for j in range(GRP):
                    c_abs = grp * GRP + j
                    tile_i = c_abs // 2
                    pbase = (c_abs % 2) * 64
                    gcol = gam_tm[pbase:pbase + 64, tile_i * 16 + vh: tile_i * 16 + vh + 1]
                    p.op("vector", "tensor_scalar", args[:, vi * 512 + j * 64: vi * 512 + (j + 1) * 64],
                         psU[0:64, j * 64:(j + 1) * 64], gcol, None, OP.subtract,
                         rd=["at_psU", "gamtm"], wr=[KEY("args")])
                av = args[:, vi * 512:(vi + 1) * 512]
                fsc = facc[0:64, 0:512]
                p.op("vector", "tensor_add", fsc, av, g["msk_su"][:, :],
                     rd=[KEY("args"), "msk_su"], wr=[KEY("fa0")])
                p.op("scalar", "activation", dinc[:, vi * 512:(vi + 1) * 512],
                     fsc, AF.Exp, scale=g["cneg1"][0:64, 0:1],
                     rd=[KEY("fa0")], wr=[KEY("dinc")])
                p.op("vector", "tensor_sub", fsc, av, g["msk_sl"][:, :],
                     rd=[KEY("args"), "msk_sl"], wr=[KEY("fa0")])
                p.op("scalar", "activation", dincT[:, vi * 512:(vi + 1) * 512],
                     fsc, AF.Exp,
                     rd=[KEY("fa0")], wr=[KEY("dincT")])
                p.op("vector", "tensor_mul", wA[:, vi * 512:(vi + 1) * 512],
                     dinc[:, vi * 512:(vi + 1) * 512], g["msk_nsl"][:, :],
                     rd=[KEY("dinc"), "msk_nsl"], wr=[KEY("wA")])
                p.op("vector", "tensor_mul", wR[:, vi * 512:(vi + 1) * 512],
                     dincT[:, vi * 512:(vi + 1) * 512], g["msk_nsu"][:, :],
                     rd=[KEY("dincT"), "msk_nsu"], wr=[KEY("wR")])
            if SCUT == 32:
                continue
            for vi in range(2):
                for j in range(GRP):
                    c_abs = grp * GRP + j
                    ksl = ktl[:, j * 64:(j + 1) * 64]
                    kbsl = kbA[:, vi * 512 + j * 64: vi * 512 + (j + 1) * 64]
                    qsl = qtl[:, j * 64:(j + 1) * 64]
                    p.op("tensor", "matmul", psW1[0:64, 0:64], kbsl, ksl, start=True, stop=True,
                         rd=[KEY("kbA"), KEY("ktl")], wr=["at_psW1"])
                    p.op("tensor", "matmul", psW1[0:64, 64:128], ksl, kbsl, start=True, stop=True,
                         rd=[KEY("kbA"), KEY("ktl")], wr=["at_psW1"])
                    p.op("tensor", "matmul", psW1[0:64, 128:192], ksl, qsl, start=True, stop=True,
                         rd=[KEY("qtl"), KEY("ktl")], wr=["at_psW1"])
                    arow = vi * 64
                    p.op("vector", "tensor_mul", powA[arow:arow + 64, 0 * GRP * 64 + j * 64: 0 * GRP * 64 + (j + 1) * 64],
                         psW1[0:64, 0:64], wA[:, vi * 512 + j * 64: vi * 512 + (j + 1) * 64],
                         rd=["at_psW1", KEY("wA")], wr=[KEY("powA")])
                    p.op("vector", "tensor_mul", powR[arow:arow + 64, 0 * GRP * 64 + j * 64: 0 * GRP * 64 + (j + 1) * 64],
                         psW1[0:64, 64:128], wR[:, vi * 512 + j * 64: vi * 512 + (j + 1) * 64],
                         rd=["at_psW1", KEY("wR")], wr=[KEY("powR")])
                    p.op("vector", "tensor_mul", aTt[arow:arow + 64, j * 64:(j + 1) * 64],
                         psW1[0:64, 128:192], dincT[:, vi * 512 + j * 64: vi * 512 + (j + 1) * 64],
                         rd=["at_psW1", KEY("dincT")], wr=[KEY("aTt")])
            if SCUT == 33:
                continue
            for lvl in range(1, 6):
                for vi in range(2):
                    arow = vi * 64
                    for j in range(GRP):
                        Aj = powA[arow:arow + 64, (lvl - 1) * GRP * 64 + j * 64: (lvl - 1) * GRP * 64 + (j + 1) * 64]
                        Rj = powR[arow:arow + 64, (lvl - 1) * GRP * 64 + j * 64: (lvl - 1) * GRP * 64 + (j + 1) * 64]
                        tp = None if vi == 0 else (64, 64)
                        if lvl < 5:
                            p.op("tensor", "matmul", psQ[0:64, j * 64:(j + 1) * 64] if vi == 0 else psQ[64:128, j * 64:(j + 1) * 64],
                                 Rj, Aj, start=True, stop=True, tile_position=tp,
                                 rd=[KEY("powA"), KEY("powR")], wr=["at_psQ"])
                        p.op("tensor", "matmul", psU[0:64, j * 64:(j + 1) * 64] if vi == 0 else psU[64:128, j * 64:(j + 1) * 64],
                             Aj, Rj, start=True, stop=True, tile_position=tp,
                             rd=[KEY("powA"), KEY("powR")], wr=["at_psU"])
                for vi in range(2):
                    pr = slice(vi * 64, vi * 64 + 64)
                    if lvl < 5:
                        p.op("vector", "tensor_copy", powA[pr, lvl * GRP * 64:(lvl + 1) * GRP * 64],
                             (psQ[0:64, 0:512] if vi == 0 else psQ[64:128, 0:512]),
                             rd=["at_psQ"], wr=[KEY("powA")])
                    p.op("vector", "tensor_copy", powR[pr, lvl * GRP * 64:(lvl + 1) * GRP * 64],
                         (psU[0:64, 0:512] if vi == 0 else psU[64:128, 0:512]),
                         rd=["at_psU"], wr=[KEY("powR")])
            if SCUT == 34:
                continue
            for vi in range(2):
                for j in range(GRP):
                    c_abs = grp * GRP + j
                    tile_i = c_abs // 2
                    pbase = (c_abs % 2) * 64
                    p.op("tensor", "transpose", psTb[0:64, :],
                         vbf[:, vi * 512 + j * 64: vi * 512 + (j + 1) * 64], g["ident_bf"][:, :],
                         rd=[KEY("vbf"), "ident_bf"], wr=["at_psTb"])
                    p.op("vector", "tensor_copy", vbtm[vi * 64:(vi + 1) * 64, j * 128:(j + 1) * 128],
                         psTb[0:64, :], rd=["at_psTb"], wr=[KEY("vbtm")])
                    p.op("tensor", "transpose", psTb[0:64, :],
                         ktl[:, j * 64:(j + 1) * 64], g["ident_bf"][:, :],
                         rd=[KEY("ktl"), "ident_bf"], wr=["at_psTb"])
                    kwcol = kw_tm_f[pbase:pbase + 64, tile_i * 16 + vh0 + vi: tile_i * 16 + vh0 + vi + 1]
                    p.op("vector", "tensor_scalar", kwtm[vi * 64:(vi + 1) * 64, j * 128:(j + 1) * 128],
                         psTb[0:64, :], kwcol, None, OP.mult,
                         rd=["at_psTb", "kwtmf"], wr=[KEY("kwtm")])
            if SCUT <= 3:
                continue
            psVv = [psV, psW1[0:64, 0:128]]
            kVv = ["at_psV", "at_psW1"]
            psSv = [psS, psQ[:, 0:128]]
            kSv = ["at_psS", "at_psQ"]
            for j in range(GRP):
                c_abs = grp * GRP + j
                arow_ = [0, 64]
                Ssl_ = [Sst[:, 0:128], Sst[:, 128:256]]
                Sbsl_ = [Sbf[:, 0:128], Sbf[:, 128:256]]
                for vi in range(2):
                    p.op("tensor", "matmul", psVv[vi],
                         kbg[:, vi * 512 + j * 64: vi * 512 + (j + 1) * 64], Sbsl_[vi],
                         start=True, stop=True, rd=[KEY("kbg"), KEY("Sbf")], wr=[kVv[vi]])
                for vi in range(2):
                    arow = arow_[vi]
                    p.op("vector", "tensor_sub", Yb[arow:arow + 64, 0:128],
                         vbtm[arow:arow + 64, j * 128:(j + 1) * 128], psVv[vi],
                         rd=[KEY("vbtm"), kVv[vi]], wr=[KEY("Yb")])
                srcb, dstb = Yb, Zb
                for lvl in range(6):
                    sk = KEY("Yb" if srcb is Yb else "Zb")
                    dk = KEY("Zb" if dstb is Zb else "Yb")
                    for vi in range(2):
                        arow = arow_[vi]
                        Rl = powR[arow:arow + 64, lvl * GRP * 64 + j * 64: lvl * GRP * 64 + (j + 1) * 64]
                        p.op("tensor", "matmul", psVv[vi], Rl,
                             srcb[arow:arow + 64, 0:128], start=True, stop=True,
                             rd=[sk, KEY("powR")], wr=[kVv[vi]])
                    for vi in range(2):
                        arow = arow_[vi]
                        dst_ap = (vnw if lvl == 5 else dstb)[arow:arow + 64, 0:128]
                        dkk = KEY("vnw") if lvl == 5 else dk
                        p.op("vector", "tensor_add", dst_ap, psVv[vi],
                             srcb[arow:arow + 64, 0:128],
                             rd=[kVv[vi], sk], wr=[dkk])
                    srcb, dstb = dstb, srcb
                for vi in range(2):
                    arow = arow_[vi]
                    p.op("tensor", "matmul", psO[vi][:, j * 64:(j + 1) * 64], Sbsl_[vi],
                         qtg[:, vi * 512 + j * 64: vi * 512 + (j + 1) * 64],
                         start=True, stop=False, rd=[KEY("Sbf"), KEY("qtg")], wr=[f"at_psO{vi}"], inc=False)
                    p.op("tensor", "matmul", psO[vi][:, j * 64:(j + 1) * 64],
                         vnw[arow:arow + 64, 0:128], aTt[arow:arow + 64, j * 64:(j + 1) * 64],
                         start=False, stop=True, rd=[KEY("vnw"), KEY("aTt")], wr=[f"at_psO{vi}"])
                for vi in range(2):
                    arow = arow_[vi]
                    p.op("tensor", "matmul", psSv[vi], kwtm[arow:arow + 64, j * 128:(j + 1) * 128],
                         vnw[arow:arow + 64, 0:128], start=True, stop=True,
                         rd=[KEY("kwtm"), KEY("vnw")], wr=[kSv[vi]])
                for vi in range(2):
                    eglcol = eglb[:, (vh0 + vi) * NCH + c_abs: (vh0 + vi) * NCH + c_abs + 1]
                    p.op("vector", "scalar_tensor_tensor", Ssl_[vi], Ssl_[vi], eglcol, psSv[vi],
                         OP.mult, OP.add, rd=[KEY("Sst"), "eglb", kSv[vi]], wr=[KEY("Sst")])
                    p.op("vector", "tensor_copy", Sbsl_[vi], Ssl_[vi], rd=[KEY("Sst")], wr=[KEY("Sbf")])
            if SCUT <= 4:
                continue
            for vi, vh in enumerate((vh0, vh1)):
                p.dma("sync", zbnc[:, :], d["z_dram"][:, vi * SEQ + g0: vi * SEQ + g0 + 512],
                      rd=[KEY("zdram")], wr=[KEY("zbnc")], slot="zbnc_l")
                p.op("scalar", "activation", siz[:, :], zbnc[:, :],
                     AF.Silu, rd=[KEY("zbnc")], wr=[KEY("siz")])
                p.op("scalar", "activation", sqt[:, :], psO[vi][:, :], AF.Square,
                     rd=[f"at_psO{vi}"], wr=[KEY("sqt")])
                p.op("tensor", "matmul", psU[0:1, 0:512], g["ones_bf"][0:128, 0:1], sqt[:, :],
                     start=True, stop=True, rd=[KEY("sqt"), "ones_bf"], wr=["at_psU"])
                p.op("vector", "tensor_scalar", grs, psU[0:1, 0:512], 1.0 / DV, EPS_RMS,
                     OP.mult, OP.add, rd=["at_psU"], wr=[KEY("fa0")])
                p.op("vector", "reciprocal", grs, grs, rd=[KEY("fa0")], wr=[KEY("fa0")])
                p.op("scalar", "activation", grs, grs, AF.Sqrt,
                     rd=[KEY("fa0")], wr=[KEY("fa0")])
                p.op("tensor", "matmul", psU[:, :], g["ones_f1"][0:1, 0:128], grs,
                     start=True, stop=True, rd=[KEY("fa0"), "ones_f1"], wr=["at_psU"])
                p.op("vector", "tensor_copy", gbc[:, :], psU[:, :], rd=["at_psU"], wr=[KEY("gbc")])
                p.op("vector", "scalar_tensor_tensor", gf[:, :], psO[vi][:, :], g["nw_col"][:, 0:1],
                     gbc[:, :], OP.mult, OP.mult,
                     rd=[f"at_psO{vi}", "nw_col", KEY("gbc")], wr=[KEY("sqt")])
                p.op("vector", "tensor_mul", gout[:, :], gf[:, :], siz[:, :],
                     rd=[KEY("sqt"), KEY("siz")], wr=[KEY("zbnc")])
                p.dma("sync", d["gated_dram"].rearrange("(v h t) c -> v h t c", h=2, t=TOK)
                      if False else d["gated_dram"][sg * 256 + vi * 128: sg * 256 + (vi + 1) * 128,
                                                    g0:g0 + 512],
                      gout[:, :], rd=[KEY("zbnc")], wr=["gated_dram"], slot="gdram")
        if d.get("dbg") and sg == 0:
            dbg = d["dbg"]
            p.dma("sync", dbg["dbg_qc"], qc[:, :], rd=[KEY("q")], slot="dbg")
            p.dma("sync", dbg["dbg_kc"], kc[:, :], rd=[KEY("k")], slot="dbg")
            p.dma("sync", dbg["dbg_vc"], vc[:, :], rd=[KEY("v0"), KEY("v1")], slot="dbg")
            p.dma("sync", dbg["dbg_rsr"], rsr[:, :], rd=[KEY("rsr")], slot="dbg")
            p.dma("sync", dbg["dbg_Sst"], Sst[:, :], rd=[KEY("Sst")], slot="dbg")
            p.dma("sync", dbg["dbg_vbtm"], vbtm[:, :], rd=[KEY("vbtm")], slot="dbg")
            p.dma("sync", dbg["dbg_kwtm"], kwtm[:, :], rd=[KEY("kwtm")], slot="dbg")
            p.dma("sync", dbg["dbg_aTt"], aTt[:, :], rd=[KEY("aTt")], slot="dbg")
            p.dma("sync", dbg["dbg_powR"], powR[:, :], rd=[KEY("powR")], slot="dbg")
            p.dma("sync", dbg["dbg_qtg"], qtg[:, :], rd=[KEY("qtg")], slot="dbg")
            p.dma("sync", dbg["dbg_kbg"], kbg[:, :], rd=[KEY("kbg")], slot="dbg")
            p.dma("sync", dbg["dbg_vbf"], vbf[:, :], rd=[KEY("vbf")], slot="dbg")
            p.dma("sync", dbg["dbg_ktl"], ktl[:, :], rd=[KEY("ktl")], slot="dbg")
            p.dma("sync", dbg["dbg_dincT"], dincT[:, :], rd=[KEY("dincT")], slot="dbg")
            p.dma("sync", dbg["dbg_wA"], wA[:, :], rd=[KEY("wA")], slot="dbg")


def emit_ag2(p, nc, es, g, d):
    gated_rows = d["gated_dram"].rearrange("v (h t) -> (v h) t", h=2)
    with contextlib.ExitStack() as aes:
        bnc = sbuf(nc, aes, "ag2bnc", [128, TOK], BF16)
        for vch in range(16):
            p.op("gpsimd", "indirect_dma_start",
                 rd=["gated_dram", "offs_sb"], wr=["ag2bnc"], sem=p.dma_sem("ag2bnc"),
                 out=bnc[:, :], out_offset=None,
                 in_=d["gated_dram"].rearrange("(p a) f -> p (a f)", p=128),
                 in_offset=IndirectOffsetOnAxis(g["offs_sb"][:, 32 + vch: 33 + vch], 1))
            p.dma("gpsimd", d["ag2_in"][vch * 128:(vch + 1) * 128, :], bnc[:, :],
                  rd=["ag2bnc"], wr=["ag2_in"], slot="ag2_in")
    p.op("gpsimd", "collective_compute", "AllGather", OP.bypass,
         rd=["ag2_in"], wr=["ag2_out"],
         replica_groups=[list(range(8))], ins=[d["ag2_in"]], outs=[d["ag2_out"]])


def emit_wout(p, nc, es, g, d, x_res2):
    with contextlib.ExitStack() as wes:
        S = lambda name, shape, dt: sbuf(nc, wes, f"wo_{name}", shape, dt)
        P_ = lambda name, shape, dt=F32: psum(nc, wes, f"wo_{name}", shape, dt)
        psA = P_("psA", [128, 512]); psB = P_("psB", [128, 512])
        gt = S("gt", [128, 32 * TOK], BF16)
        wrg = S("wrg", [128, 2 * 32 * 128], BF16)
        gated_rows = d["gated_dram"].rearrange("v (h t) -> (v h) t", h=2)
        for vch in range(16):
            p.op("gpsimd", "indirect_dma_start",
                 rd=["gated_dram", "offs_sb"], wr=[f"gtg{vch // 8}"], sem=p.dma_sem(f"gtg{vch // 8}"),
                 out=gt[:, vch * TOK:(vch + 1) * TOK], out_offset=None,
                 in_=d["gated_dram"].rearrange("(p a) f -> p (a f)", p=128),
                 in_offset=IndirectOffsetOnAxis(g["offs_sb"][:, 48 + vch: 49 + vch], 1))
        for vch in range(16):
            p.op("gpsimd", "indirect_dma_start",
                 rd=["ag2_out", "offs_sb"], wr=[f"gtg{2 + vch // 8}"], sem=p.dma_sem(f"gtg{2 + vch // 8}"),
                 out=gt[:, (16 + vch) * TOK:(17 + vch) * TOK], out_offset=None,
                 in_=d["ag2_out"].rearrange("(p a) f -> p (a f)", p=128),
                 in_offset=IndirectOffsetOnAxis(g["offs_sb"][:, 64 + vch: 65 + vch], 1))

        def w_load(m, r):
            dst = wrg[:, r * 32 * 128:(r + 1) * 32 * 128]
            p.dma("gpsimd", dst[:, 0:16 * 128].rearrange("p (v m) -> p v m", v=16),
                  d["w_out_mine"].rearrange("(v p) m -> p v m", p=128)[:, :, m * 128:(m + 1) * 128],
                  wr=[f"wo_w{r}"])
            p.dma("gpsimd", dst[:, 16 * 128:32 * 128].rearrange("p (v m) -> p v m", v=16),
                  d["w_out_partner"].rearrange("(v p) m -> p v m", p=128)[:, :, m * 128:(m + 1) * 128],
                  wr=[f"wo_w{r}"])
        w_load(0, 0); w_load(1, 1)
        for m in range(KCH):
            r = m % 2
            pd = psA if r == 0 else psB
            pk = "wo_psA" if r == 0 else "wo_psB"
            for tt in range(NTT):
                for vch in range(32):
                    p.op("tensor", "matmul", pd[:, :],
                         wrg[:, r * 32 * 128 + vch * 128: r * 32 * 128 + (vch + 1) * 128],
                         gt[:, vch * TOK + tt * TT: vch * TOK + (tt + 1) * TT],
                         start=(vch == 0), stop=(vch == 31),
                         rd=[f"wo_w{r}", f"gtg{vch // 8}"], wr=[pk], inc=(vch == 31))
                dst = x_res2[:, m * TOK + tt * TT: m * TOK + (tt + 1) * TT]
                p.op("vector", "tensor_add", dst, pd[:, :], dst, rd=[pk, "ffn2xres"], wr=["ffn2xres"])
            if m + 2 < KCH:
                w_load(m + 2, r)



def build_program(stages=("ffn1", "attn", "ffn2")):
    nc = bass.Bass("TRN2", target_bir_lowering=False, debug=False)
    d = {}
    di = lambda name, shape, dt=F32: nc.dram_tensor(name, shape, dt, kind="ExternalInput").ap()
    d["x_in"] = di("x_in", [KCH, 128, TOK])
    d["wg"] = [di(f"wg{e}", [E, HEXP]) for e in range(NEXP)]
    d["wu"] = [di(f"wu{e}", [E, HEXP]) for e in range(NEXP)]
    d["wd"] = [di(f"wd{e}", [HEXP, E]) for e in range(NEXP)]
    d["shg"] = di("shg", [E, HSH]); d["shu"] = di("shu", [E, HSH])
    d["shd"] = di("shd", [HSH, E])
    d["gw"] = di("gw", [E, NEXP]); d["seg"] = di("seg", [E, 1])
    d["ident_in"] = di("ident_in", [128, 128], F32)
    d["oh8_in"] = di("oh8_in", [8, 8 * 128], F32)
    d["oh32_in"] = di("oh32_in", [64, 16 * 128], F32)
    d["oh2_in"] = di("oh2_in", [2, 2 * 128], F32)
    d["wqkvz"] = di("wqkvz", [E, 6144])
    d["wba"] = di("wba", [E, 64])
    d["convw_in"] = di("convw_in", [128, 128], F32)
    d["convb_in"] = di("convb_in", [128, 32], F32)
    d["alog_in"] = di("alog_in", [64, 1], F32)
    d["dt_in"] = di("dt_in", [64, 1], F32)
    d["nw_in"] = di("nw_in", [128, 1], F32)
    d["w_out_mine"] = di("w_out_mine", [2048, 2048])
    d["w_out_partner"] = di("w_out_partner", [2048, 2048])
    d["msk_in"] = di("msk_in", [64, 4 * 512], F32)
    d["offs_in"] = di("offs_in", [128, 80], I32)
    y = nc.dram_tensor("y", [KCH, 128, TOK], F32, kind="ExternalOutput").ap()
    dbg = {}
    if DEBUG:
        for nm, shape, dt in [("dbg_xlnseq", [128, KCH * SEQ], BF16),
                              ("dbg_gam", [64, SEQ], F32), ("dbg_egr", [64, SEQ], BF16),
                              ("dbg_kwr", [64, SEQ], BF16), ("dbg_eglb", [128, NVH * NCH], F32),
                              ("dbg_gamtm", [128, 16 * KCH], F32), ("dbg_kwtmf", [128, 16 * KCH], F32),
                              ("dbg_qc", [128, SEQ], BF16), ("dbg_kc", [128, SEQ], BF16),
                              ("dbg_vc", [128, 2 * SEQ], BF16), ("dbg_rsr", [64, SEQ], F32),
                              ("dbg_Sst", [128, 2 * 128], F32),
                              ("dbg_vbtm", [128, 8 * 128], BF16), ("dbg_kwtm", [128, 8 * 128], BF16),
                              ("dbg_aTt", [128, 8 * 64], BF16), ("dbg_powR", [128, 6 * 8 * 64], BF16),
                              ("dbg_qtg", [128, 2 * 512], BF16), ("dbg_kbg", [128, 2 * 512], BF16),
                              ("dbg_vbf", [128, 2 * 512], BF16), ("dbg_ktl", [128, 512], BF16),
                              ("dbg_gated", [2048, SEQ], BF16), ("dbg_zd", [128, 2 * SEQ], BF16),
                              ("dbg_dincT", [64, 2 * 512], BF16), ("dbg_wA", [64, 2 * 512], BF16)]:
            dbg[nm] = nc.dram_tensor(nm, shape, dt, kind="ExternalOutput").ap()
    d["dbg"] = dbg
    d["ag1_in"] = nc.dram_tensor("ag1_in", [128, KCH * TOK], BF16)
    d["ag1_out"] = nc.dram_tensor("ag1_out", [8 * 128, KCH * TOK], BF16)
    d["xres_dram"] = nc.dram_tensor("xres_dram", [128, KCH * TOK], F32).ap()
    d["gated_dram"] = nc.dram_tensor("gated_dram", [2048, SEQ], BF16).ap()
    d["ag2_in"] = nc.dram_tensor("ag2_in", [2048, TOK], BF16).ap()
    d["z_dram"] = nc.dram_tensor("z_dram", [128, 2 * SEQ], BF16).ap()
    d["ag2_out"] = nc.dram_tensor("ag2_out", [8 * 2048, TOK], BF16).ap()

    with contextlib.ExitStack() as es:
        p = PB(nc, es)
        g = dict(d)
        for nm, shape, dt in [("ones_bf", [128, 128], BF16), ("ones_f1", [1, 128], F32),
                              ("cneg1", [128, 1], F32), ("cone", [128, 1], F32),
                              ("ident_bf", [128, 128], BF16),
                              ("oh8_sb", [8, 8 * 128], BF16),
                              ("gw_sb", [128, KCH * NEXP], BF16), ("seg_sb", [128, KCH], BF16),
                              ("nw_col", [128, 1], F32),
                              ("offs_sb", [128, 80], I32)]:
            g[nm] = sbuf(nc, es, nm, shape, dt)
        g["ident64"] = g["ident_bf"]

        p.op("vector", "memset", g["ones_bf"][:, :], 1.0, wr=["ones_bf"])
        p.op("vector", "memset", g["ones_f1"][:, :], 1.0, wr=["ones_f1"])
        p.op("vector", "memset", g["cneg1"][:, :], -1.0, wr=["cneg1"])
        p.op("vector", "memset", g["cone"][:, :], 1.0, wr=["cone"])
        p.dma("gpsimd", g["ident_bf"][:, :], d["ident_in"][:, :], wr=["ident_bf"], slot="consts")
        p.dma("gpsimd", g["oh8_sb"][:, :], d["oh8_in"][:, :], wr=["oh8_sb"], slot="consts")
        p.dma("gpsimd", g["gw_sb"].ap().rearrange("p (k e) -> p k e", k=KCH),
              d["gw"].rearrange("(k p) e -> p k e", p=128), wr=["gw_sb"], slot="consts")
        p.dma("gpsimd", g["seg_sb"].ap().rearrange("p (k e) -> p k e", k=KCH),
              d["seg"].rearrange("(k p) e -> p k e", p=128), wr=["seg_sb"], slot="consts",
              allow_slow_non_contiguous=True)
        p.dma("gpsimd", g["nw_col"][:, :], d["nw_in"][:, :], wr=["nw_col"], slot="consts")
        ev = p.dma("gpsimd", g["offs_sb"][:, :], d["offs_in"][:, :], wr=["offs_sb"], slot="consts")
        for kk in ["ident_bf", "oh8_sb", "gw_sb", "seg_sb", "nw_col", "offs_sb"]:
            p.last_w[kk] = ev

        do_attn = "attn" in stages
        with contextlib.ExitStack() as s1:
            x_res = sbuf(nc, s1, "x_res", [128, KCH * TOK], F32)
            g["x_res"] = x_res
            p.dma("sync", x_res.ap().rearrange("p (k t) -> p k t", k=KCH),
                  d["x_in"].rearrange("k p t -> p k t"), wr=["ffn1xres"])
            if "ffn1" in stages:
                emit_ffn(p, nc, s1, g, "ffn1")
            if do_attn:
                p.barrier()
                def ln2_cb(tt, xln, key):
                    p.dma("sync", d["ag1_in"].ap().rearrange("p (k t) -> p k t", k=KCH)[:, :, tt * TT:(tt + 1) * TT],
                          xln.ap().rearrange("p (k t) -> p k t", k=KCH), rd=[key], wr=["ag1_in"],
                          slot="ag1_in")
                emit_ln_to(p, nc, s1, g, x_res, ln2_cb, "ln2")
                p.dma("sync", d["xres_dram"].rearrange("p (k t) -> p k t", k=KCH),
                      x_res.ap().rearrange("p (k t) -> p k t", k=KCH), rd=["ffn1xres"], wr=["xres_spill"])
            else:
                p.dma("sync", y.rearrange("k p t -> p k t"),
                      x_res.ap().rearrange("p (k t) -> p k t", k=KCH), rd=["ffn1xres"], wr=["yout"])
        if do_attn:
            p.barrier()
            with contextlib.ExitStack() as s2:
                for nm, shape, dt in [("ident_f1", [128, 128], F32),
                                      ("oh32_sb", [64, 16 * 128], BF16),
                                      ("oh2_sb", [2, 2 * 128], BF16),
                                      ("convw_sb", [128, 128], F32), ("convb_sb", [128, 32], F32),
                                      ("alog_col", [64, 1], F32), ("dt_col", [64, 1], F32),
                                      ("msk_su", [64, 512], F32), ("msk_sl", [64, 512], F32),
                                      ("msk_nsl", [64, 512], F32), ("msk_nsu", [64, 512], F32)]:
                    g[nm] = sbuf(nc, s2, nm, shape, dt)
                _ac = ["ident_f1", "oh32_sb", "oh2_sb", "convw_sb", "convb_sb",
                       "alog_col", "dt_col", "msk_su", "msk_sl", "msk_nsl", "msk_nsu"]
                p.dma("gpsimd", g["ident_f1"][:, :], d["ident_in"][:, :], wr=["ident_f1"], slot="aconsts")
                p.dma("gpsimd", g["oh32_sb"][:, :], d["oh32_in"][:, :], wr=["oh32_sb"], slot="aconsts")
                p.dma("gpsimd", g["oh2_sb"][:, :], d["oh2_in"][:, :], wr=["oh2_sb"], slot="aconsts")
                p.dma("gpsimd", g["convw_sb"][:, :], d["convw_in"][:, :], wr=["convw_sb"], slot="aconsts")
                p.dma("gpsimd", g["convb_sb"][:, :], d["convb_in"][:, :], wr=["convb_sb"], slot="aconsts")
                p.dma("gpsimd", g["alog_col"][:, :], d["alog_in"][:, :], wr=["alog_col"], slot="aconsts")
                p.dma("gpsimd", g["dt_col"][:, :], d["dt_in"][:, :], wr=["dt_col"], slot="aconsts")
                p.dma("gpsimd", g["msk_su"][:, :], d["msk_in"][:, 0:512], wr=["msk_su"], slot="aconsts")
                p.dma("gpsimd", g["msk_sl"][:, :], d["msk_in"][:, 512:1024], wr=["msk_sl"], slot="aconsts")
                p.dma("gpsimd", g["msk_nsl"][:, :], d["msk_in"][:, 1024:1536], wr=["msk_nsl"], slot="aconsts")
                p.dma("gpsimd", g["msk_nsu"][:, :], d["msk_in"][:, 1536:2048], wr=["msk_nsu"], slot="aconsts")
                g["ohg_f32"] = sbuf(nc, s2, "ohg_f32", [64, 2048], F32)
                ev = p.dma("gpsimd", g["ohg_f32"][:, :], d["oh32_in"][:, 0:2048],
                           wr=["ohg_f32"], slot="aconsts")
                for kk in _ac + ["ohg_f32"]:
                    p.last_w[kk] = ev
                ctx = emit_attention(p, nc, s2, g, d)
                if ctx is None:
                    pass
                if DEBUG and ctx is not None:
                    dbg = d["dbg"]
                    p.dma("sync", dbg["dbg_xlnseq"], ctx["xln_seq"][:, :], rd=["xlnseq"], slot="dbg")
                    p.dma("sync", dbg["dbg_gam"], ctx["gam"][:, :], rd=["gam"], slot="dbg")
                    p.dma("sync", dbg["dbg_egr"], ctx["egr"][:, :], rd=["egr"], slot="dbg")
                    p.dma("sync", dbg["dbg_kwr"], ctx["kwr"][:, :], rd=["kwr"], slot="dbg")
                    p.dma("sync", dbg["dbg_eglb"], ctx["eglb"][:, :], rd=["eglb"], slot="dbg")
                    p.dma("sync", dbg["dbg_gamtm"], ctx["gam_tm"][:, :], rd=["gamtm"], slot="dbg")
                    p.dma("sync", dbg["dbg_kwtmf"], ctx["kw_tm_f"][:, :], rd=["kwtmf"], slot="dbg")
                if ctx is not None and ACUT >= 3:
                    for sg in range(NSG):
                        emit_subgroup(p, nc, s2, g, d, ctx, sg)
                        p.barrier()
                if DEBUG:
                    p.dma("sync", d["dbg"]["dbg_gated"], d["gated_dram"][:, :], rd=["gated_dram"], slot="dbg")
                    p.dma("sync", d["dbg"]["dbg_zd"], d["z_dram"][:, :], rd=["sg7_zdram"], slot="dbg")
                if ACUT >= 4:
                    emit_ag2(p, nc, s2, g, d)
            p.barrier()
            with contextlib.ExitStack() as s3:
                x_res2 = sbuf(nc, s3, "x_res2", [128, KCH * TOK], F32)
                g["x_res"] = x_res2
                p.dma("sync", x_res2.ap().rearrange("p (k t) -> p k t", k=KCH),
                      d["xres_dram"].rearrange("p (k t) -> p k t", k=KCH),
                      rd=["xres_spill"], wr=["ffn2xres"])
                if ACUT >= 5:
                    emit_wout(p, nc, s3, g, d, x_res2)
                p.barrier()
                if "ffn2" in stages:
                    emit_ffn(p, nc, s3, g, "ffn2")
                p.dma("sync", y.rearrange("k p t -> p k t"),
                      x_res2.ap().rearrange("p (k t) -> p k t", k=KCH), rd=["ffn2xres"], wr=["yout"])
        yev = p.last_w["yout"]
        p.stream["sync"].append(lambda h, s=p.sem[yev[0]], v=yev[1]: h.wait_ge(s, v))
        block = es.enter_context(nc.Block())
        p.finalize(block)
    return nc


def host_prep(inputs):
    x = np.ascontiguousarray(np.asarray(inputs["L0_attn_hidden_states"]))
    wq = np.asarray(inputs["L0_attn_w_qkvz"]); wba = np.asarray(inputs["L0_attn_w_ba"])
    convw = np.asarray(inputs["L0_attn_conv_w"]); convb = np.asarray(inputs["L0_attn_conv_b"])
    alog = np.asarray(inputs["L0_attn_A_log"]); dtb = np.asarray(inputs["L0_attn_dt_bias"])
    nw = np.asarray(inputs["L0_attn_norm_weight"]); wo = np.asarray(inputs["L0_attn_w_out"])
    gw = np.asarray(inputs["L0_ffn_gate_w"])
    wg = np.asarray(inputs["L0_ffn_w_gate"]); wu = np.asarray(inputs["L0_ffn_w_up"])
    wd = np.asarray(inputs["L0_ffn_w_down"])
    shg = np.asarray(inputs["L0_ffn_sh_gate"]); shu = np.asarray(inputs["L0_ffn_sh_up"])
    shd = np.asarray(inputs["L0_ffn_sh_down"]); seg = np.asarray(inputs["L0_ffn_se_gate_w"])

    xt = np.ascontiguousarray(x.transpose(0, 2, 1))
    wg_t = np.ascontiguousarray(wg.transpose(0, 2, 1))
    wu_t = np.ascontiguousarray(wu.transpose(0, 2, 1))
    wd_t = np.ascontiguousarray(wd.transpose(0, 2, 1))
    shg_t = np.ascontiguousarray(shg.T); shu_t = np.ascontiguousarray(shu.T)
    shd_t = np.ascontiguousarray(shd.T)
    gw_t = np.ascontiguousarray(gw.T); seg_t = np.ascontiguousarray(seg.T)
    ident = np.eye(128, dtype=np.float32)
    oh8 = np.zeros((8, 8 * 128), np.float32)
    for e in range(8):
        oh8[e, e * 128:(e + 1) * 128] = 1.0
    oh16 = np.zeros((32, 16 * 128), np.float32)
    for e in range(16):
        oh16[e, e * 128:(e + 1) * 128] = 1.0
    oh32 = np.concatenate([oh16, oh16], axis=0)
    oh2 = np.zeros((2, 2 * 128), np.float32)
    oh2[0, 0:128] = 1.0; oh2[1, 128:256] = 1.0
    su = np.triu(np.ones((64, 64), np.float32), 1)
    sl = np.tril(np.ones((64, 64), np.float32), -1)
    msk = np.concatenate([np.tile(1000.0 * su, (1, 8)), np.tile(1000.0 * sl, (1, 8)),
                          np.tile(-sl, (1, 8)), np.tile(-su, (1, 8))], axis=1)

    in_maps = []
    for c in range(8):
        b, h = c // 2, c % 2
        gg = h
        xs = np.ascontiguousarray(xt[b, :, h * TOK:(h + 1) * TOK]).reshape(KCH, 128, TOK)
        qb = wq[:, gg * 1024:(gg + 1) * 1024]
        kb = wq[:, 2048 + gg * 1024: 2048 + (gg + 1) * 1024]
        vb = wq[:, 4096 + gg * 2048: 4096 + (gg + 1) * 2048]
        zb = wq[:, 8192 + gg * 2048: 8192 + (gg + 1) * 2048]
        wqkvz = np.ascontiguousarray(np.concatenate([qb, kb, vb, zb], axis=1))
        cw_cols = np.concatenate([convw[:, gg * 1024:(gg + 1) * 1024],
                                  convw[:, 2048 + gg * 1024: 2048 + (gg + 1) * 1024],
                                  convw[:, 4096 + gg * 2048: 4096 + (gg + 1) * 2048]], axis=1)
        cb_cols = np.concatenate([convb[gg * 1024:(gg + 1) * 1024],
                                  convb[2048 + gg * 1024: 2048 + (gg + 1) * 1024],
                                  convb[4096 + gg * 2048: 4096 + (gg + 1) * 2048]])
        convw_pt = np.ascontiguousarray(cw_cols.T.reshape(32, 128, 4).transpose(1, 0, 2).reshape(128, 128))
        convb_pt = np.ascontiguousarray(cb_cols.reshape(32, 128).T)
        wba_c = np.zeros((2048, 64), np.float32)
        wba_c[:, 0:16] = wba[:, gg * 16:(gg + 1) * 16]
        wba_c[:, 32:48] = wba[:, 32 + gg * 16: 32 + (gg + 1) * 16]
        wba_c = np.ascontiguousarray(wba_c)
        av = np.zeros((64, 1), np.float32); av[32:48, 0] = alog[gg * 16:(gg + 1) * 16]
        dv = np.zeros((64, 1), np.float32); dv[32:48, 0] = dtb[gg * 16:(gg + 1) * 16]
        nwc = np.ascontiguousarray(nw.reshape(128, 1))
        wom = np.ascontiguousarray(wo[gg * 2048:(gg + 1) * 2048])
        wop = np.ascontiguousarray(wo[(1 - gg) * 2048:(2 - gg) * 2048])
        offs = np.zeros((128, 80), np.int32)
        par = np.arange(128, dtype=np.int32)
        for half in range(2):
            for k in range(16):
                offs[:, half * 16 + k] = ((2 * b + half) * 128 + par) * (KCH * TOK) + k * TOK
        for vch in range(16):
            offs[:, 32 + vch] = ((vch * 128 + par) * 2 + (1 - h)) * TOK
            offs[:, 48 + vch] = ((vch * 128 + par) * 2 + h) * TOK
            offs[:, 64 + vch] = ((c ^ 1) * 2048 + vch * 128 + par) * TOK
        m = {"x_in": xs, "shg": shg_t, "shu": shu_t, "shd": shd_t,
             "gw": gw_t, "seg": seg_t, "ident_in": ident, "oh8_in": oh8,
             "oh32_in": oh32, "oh2_in": oh2, "wqkvz": wqkvz, "wba": wba_c,
             "convw_in": convw_pt, "convb_in": convb_pt, "alog_in": av, "dt_in": dv,
             "nw_in": nwc, "w_out_mine": wom, "w_out_partner": wop,
             "msk_in": msk, "offs_in": offs}
        for e in range(NEXP):
            m[f"wg{e}"] = wg_t[e]; m[f"wu{e}"] = wu_t[e]; m[f"wd{e}"] = wd_t[e]
        in_maps.append(m)
    return in_maps


STAGES = ("ffn1", "attn", "ffn2")
DEBUG = False
NSG = 8
ACUT = 5
SCUT = 5

def kernel(**inputs):
    in_maps = host_prep(inputs)
    nc = build_program(stages=STAGES)
    res = run_bass_kernel_spmd(nc, in_maps, core_ids=list(range(8)))
    out = np.zeros((4, 2048, 2048), np.float32)
    for c in range(8):
        b, h = c // 2, c % 2
        yk = np.asarray(res.results[c]["y"])
        out[b, h * TOK:(h + 1) * TOK, :] = yk.reshape(E, TOK).T
    return out
